# version 01
# baseline (speedup 1.0000x reference)
import contextlib
import numpy as np
import ml_dtypes
import concourse.bass as bass
import concourse.mybir as mybir
from concourse.bass_utils import run_bass_kernel_spmd

F32 = mybir.dt.float32
BF16 = mybir.dt.bfloat16
AF = mybir.ActivationFunctionType
ALU = mybir.AluOpType
AX = mybir.AxisListType
NPBF = ml_dtypes.bfloat16

NB, T, DM = 4, 4096, 1024
H, DH, G = 16, 64, 409
TPC = 2048
DI, DS = 2048, 16
NE, DF = 24, 4096
NCORES = 8


class Ctx:
    QUEUES = ("sync", "vector", "scalar", "tensor", "gpsimd")

    def __init__(self, nc, es):
        self.nc = nc
        self.sem = {q: es.enter_context(nc.semaphore(f"prog_{q}")) for q in self.QUEUES}
        self.NDMA = 10
        self.dsem, self.duses = {}, {}
        for q in ("sync", "scalar", "gpsimd", "vector"):
            self.dsem[q] = [es.enter_context(nc.semaphore(f"dma_{q}{i}")) for i in range(self.NDMA)]
            self.duses[q] = [0] * self.NDMA
        self.drr = {q: 0 for q in self.dsem}
        self.cnt = {q: 0 for q in self.QUEUES}
        self.last_write = {}
        self.readers = {}
        self.watermark = {q: {} for q in self.QUEUES}
        self.queue = {q: [] for q in self.QUEUES}

    def _wait(self, eng, kind, q, c):
        wm = self.watermark[eng]
        k = (kind, q)
        if wm.get(k, 0) >= c:
            return
        sem = self.sem[q] if kind == "prog" else self.dsem[q[0]][q[1]]
        self.queue[eng].append(("wait", sem, c))
        wm[k] = c

    def _raw(self, eng, keys):
        for key in keys:
            lw = self.last_write.get(key)
            if lw is not None:
                self._wait(eng, lw[0], lw[1], lw[2])

    def _war_waw(self, eng, keys):
        for key in keys:
            lw = self.last_write.get(key)
            if lw is not None:
                self._wait(eng, lw[0], lw[1], lw[2])
            for (kind, q), c in self.readers.get(key, {}).items():
                self._wait(eng, kind, q, c)

    def _record(self, kind, q, c, reads, writes):
        for key in writes:
            self.last_write[key] = (kind, q, c)
            self.readers[key] = {}
        for key in reads:
            d = self.readers.setdefault(key, {})
            k = (kind, q)
            d[k] = max(d.get(k, 0), c)

    def op(self, eng, reads, writes, fn, accum=False):
        self._raw(eng, reads)
        if accum:
            for key in writes:
                lw = self.last_write.get(key)
                if lw is not None and not (lw[0] == "prog" and lw[1] == eng):
                    self._wait(eng, lw[0], lw[1], lw[2])
                for (kind, q), c in self.readers.get(key, {}).items():
                    if not (kind == "prog" and q == eng):
                        self._wait(eng, kind, q, c)
        else:
            self._war_waw(eng, writes)
        self.cnt[eng] += 1
        self.queue[eng].append(("op", fn, self.sem[eng], 1))
        self._record("prog", eng, self.cnt[eng], reads, writes)

    def dma(self, q, reads, writes, fn):
        self._raw(q, reads)
        self._war_waw(q, writes)
        slot = self.drr[q]
        self.drr[q] = (slot + 1) % self.NDMA
        prev = self.duses[q][slot]
        if prev > 0:
            self._wait(q, "dma", (q, slot), 16 * prev)
        self.duses[q][slot] = prev + 1
        self.queue[q].append(("op", fn, self.dsem[q][slot], 16))
        self._record("dma", (q, slot), 16 * (prev + 1), reads, writes)

    def sync_on(self, eng, keys):
        self._raw(eng, keys)

    def emit(self, block):
        def make_body(qname):
            def body(engine):
                for item in self.queue[qname]:
                    if item[0] == "wait":
                        engine.wait_ge(item[1], item[2])
                    else:
                        _, fn, sem, inc = item
                        fn(engine).then_inc(sem, inc)
            return body
        for qname in self.QUEUES:
            if self.queue[qname]:
                getattr(block, qname)(make_body(qname))


def _sb(nc, es, name, shape, dt):
    return es.enter_context(nc.sbuf_tensor(name, list(shape), dt))


def _ps(nc, es, name, shape, dt=F32):
    return es.enter_context(nc.psum_tensor(name, list(shape), dt))


def _ln_tile(C, xin_ap, xin_key, xn, mean, var, scr, epscol):
    C.op("vector", [xin_key], ["mean"], lambda e: e.reduce_sum(mean[:], xin_ap, axis=AX.X))
    C.op("vector", ["mean"], ["mean"], lambda e: e.tensor_scalar_mul(mean[:], mean[:], 1.0 / DM))
    C.op("vector", [xin_key, "mean"], ["scr"],
         lambda e: e.tensor_scalar(scr[:], xin_ap, mean[:], None, op0=ALU.subtract))
    C.op("scalar", ["scr"], ["xn"], lambda e: e.activation(xn[:], scr[:], AF.Square))
    C.op("vector", ["xn"], ["var"], lambda e: e.reduce_sum(var[:], xn[:], axis=AX.X))
    C.op("vector", ["var"], ["var"], lambda e: e.tensor_scalar_mul(var[:], var[:], 1.0 / DM))
    C.op("scalar", ["var", "epscol"], ["var"],
         lambda e: e.activation(var[:], var[:], AF.Sqrt, bias=epscol[:]))
    C.op("vector", ["var"], ["var"], lambda e: e.reciprocal(var[:], var[:]))
    C.op("vector", ["scr", "var"], ["xn"],
         lambda e: e.tensor_scalar(xn[:], scr[:], var[:], None, op0=ALU.mult))


def build_p1():
    nc = bass.Bass("TRN2", target_bir_lowering=False)
    x = nc.dram_tensor("x", [TPC, DM], F32, kind="ExternalInput").ap()
    wq = nc.dram_tensor("wq", [DM, DM], F32, kind="ExternalInput").ap()
    wk = nc.dram_tensor("wk", [DM, DM], F32, kind="ExternalInput").ap()
    wv = nc.dram_tensor("wv", [DM, DM], F32, kind="ExternalInput").ap()
    cosd = nc.dram_tensor("cosd", [128, TPC], F32, kind="ExternalInput").ap()
    sind = nc.dram_tensor("sind", [128, TPC], F32, kind="ExternalInput").ap()
    eyed = nc.dram_tensor("eyed", [128, 128], F32, kind="ExternalInput").ap()
    epsd = nc.dram_tensor("epsd", [128, 1], F32, kind="ExternalInput").ap()
    qt = nc.dram_tensor("qt", [DM, TPC], F32, kind="ExternalOutput").ap()
    kt = nc.dram_tensor("kt", [DM, TPC], F32, kind="ExternalOutput").ap()
    vout = nc.dram_tensor("vout", [TPC, DM], BF16, kind="ExternalOutput").ap()
    xnt = nc.dram_tensor("xnt", [DM, TPC], BF16, kind="ExternalOutput").ap()
    es = contextlib.ExitStack()
    with es:
        xtile = _sb(nc, es, "xtile", (128, DM), F32)
        xn = _sb(nc, es, "xn", (128, DM), F32)
        scr = _sb(nc, es, "scr", (128, DM), F32)
        mean = _sb(nc, es, "mean", (128, 1), F32)
        var = _sb(nc, es, "var", (128, 1), F32)
        epscol = _sb(nc, es, "epscol", (128, 1), F32)
        eye = _sb(nc, es, "eye", (128, 128), F32)
        cos = _sb(nc, es, "cos", (128, TPC), F32)
        sin = _sb(nc, es, "sin", (128, TPC), F32)
        xnt_bf = _sb(nc, es, "xnt_bf", (128, 8 * TPC), BF16)
        wstage = _sb(nc, es, "wstage", (128, 4 * DM), F32)
        wbf = _sb(nc, es, "wbf", (128, 8 * DM), BF16)
        qraw = _sb(nc, es, "qraw", (128, 8 * TPC), F32)
        t1 = _sb(nc, es, "t1", (128, TPC), F32)
        t2 = _sb(nc, es, "t2", (128, TPC), F32)
        vsb = _sb(nc, es, "vsb", (128, DM), BF16)
        ptr = _ps(nc, es, "ptr", (128, 128))
        pq0 = _ps(nc, es, "pq0", (128, 512))
        pq1 = _ps(nc, es, "pq1", (128, 512))
        C = Ctx(nc, es)
        block = es.enter_context(nc.Block())

        C.dma("sync", [], ["epscol"], lambda e: e.dma_start(out=epscol[:], in_=epsd))
        C.dma("sync", [], ["eye"], lambda e: e.dma_start(out=eye[:], in_=eyed))
        C.dma("sync", [], ["cos"], lambda e: e.dma_start(out=cos[:], in_=cosd))
        C.dma("sync", [], ["sin"], lambda e: e.dma_start(out=sin[:], in_=sind))

        for i in range(16):
            C.dma("sync", [], ["xtile"],
                  lambda e, i=i: e.dma_start(out=xtile[:], in_=x[i * 128:(i + 1) * 128, :]))
            _ln_tile(C, xtile[:], "xtile", xn, mean, var, scr, epscol)
            for kc in range(8):
                C.op("tensor", ["xn", "eye"], ["ptr"],
                     lambda e, kc=kc: e.transpose(ptr[:], xn[:, kc * 128:(kc + 1) * 128], eye[:]))
                C.op("vector", ["ptr"], ["xnt_bf"],
                     lambda e, kc=kc, i=i: e.tensor_copy(
                         xnt_bf[:, kc * TPC + i * 128: kc * TPC + (i + 1) * 128], ptr[:]))
        C.dma("gpsimd", ["xnt_bf"], ["xnt"],
              lambda e: e.dma_start(out=xnt.rearrange("(kc p) t -> p kc t", p=128),
                                    in_=xnt_bf[:].rearrange("p (kc t) -> p kc t", t=TPC)))

        def load_weight_bf(wsrc):
            for hf in range(2):
                C.dma("sync", [], ["wstage"],
                      lambda e, wsrc=wsrc, hf=hf: e.dma_start(
                          out=wstage[:].rearrange("p (kc m) -> p kc m", m=DM),
                          in_=wsrc.rearrange("(kc p) m -> p kc m", p=128)[:, hf * 4:(hf + 1) * 4, :]))
                C.op("gpsimd", ["wstage"], ["wbf"],
                     lambda e, hf=hf: e.tensor_copy(
                         wbf[:, hf * 4 * DM:(hf + 1) * 4 * DM], wstage[:]))

        for wsrc, outap, okey in ((wq, qt, "qt"), (wk, kt, "kt")):
            load_weight_bf(wsrc)
            for m in range(8):
                for tt in range(4):
                    pb = pq0 if tt % 2 == 0 else pq1
                    pk = f"pq{tt % 2}"
                    for kc in range(8):
                        C.op("tensor", ["wbf", "xnt_bf"], [pk],
                             lambda e, m=m, tt=tt, kc=kc, pb=pb: e.matmul(
                                 pb[:], wbf[:, kc * DM + m * 128: kc * DM + (m + 1) * 128],
                                 xnt_bf[:, kc * TPC + tt * 512: kc * TPC + (tt + 1) * 512],
                                 start=(kc == 0), stop=(kc == 7), skip_group_check=True),
                             accum=(kc != 0))
                    C.op("vector", [pk], ["qraw"],
                         lambda e, m=m, tt=tt, pb=pb: e.tensor_copy(
                             qraw[:, m * TPC + tt * 512: m * TPC + (tt + 1) * 512], pb[:]))
            for m in range(4):
                a = qraw[:, m * TPC:(m + 1) * TPC]
                b = qraw[:, (m + 4) * TPC:(m + 5) * TPC]
                C.op("vector", ["qraw", "cos"], ["t1"],
                     lambda e, a=a: e.tensor_tensor(t1[:], a, cos[:], op=ALU.mult))
                C.op("vector", ["qraw", "sin"], ["t2"],
                     lambda e, a=a: e.tensor_tensor(t2[:], a, sin[:], op=ALU.mult))
                C.op("vector", ["qraw", "sin"], ["qraw"],
                     lambda e, a=a, b=b: e.tensor_tensor(a, b, sin[:], op=ALU.mult))
                C.op("vector", ["qraw", "cos"], ["qraw"],
                     lambda e, b=b: e.tensor_tensor(b, b, cos[:], op=ALU.mult))
                C.op("vector", ["qraw", "t2"], ["qraw"],
                     lambda e, b=b: e.tensor_tensor(b, t2[:], b, op=ALU.add))
                C.op("vector", ["qraw", "t1"], ["qraw"],
                     lambda e, a=a: e.tensor_tensor(a, t1[:], a, op=ALU.subtract))
            C.dma("gpsimd", ["qraw"], [okey],
                  lambda e, outap=outap: e.dma_start(
                      out=outap.rearrange("(m p) t -> p m t", p=128),
                      in_=qraw[:].rearrange("p (m t) -> p m t", t=TPC)))

        load_weight_bf(wv)
        for i in range(16):
            for nn in range(2):
                pb = pq0 if nn == 0 else pq1
                pk = f"pq{nn}"
                for kc in range(8):
                    C.op("tensor", ["wbf", "xnt_bf"], [pk],
                         lambda e, i=i, nn=nn, kc=kc, pb=pb: e.matmul(
                             pb[:], xnt_bf[:, kc * TPC + i * 128: kc * TPC + (i + 1) * 128],
                             wbf[:, kc * DM + nn * 512: kc * DM + (nn + 1) * 512],
                             start=(kc == 0), stop=(kc == 7), skip_group_check=True),
                         accum=(kc != 0))
                C.op("vector", [pk], ["vsb"],
                     lambda e, nn=nn, pb=pb: e.tensor_copy(vsb[:, nn * 512:(nn + 1) * 512], pb[:]))
            C.dma("gpsimd", ["vsb"], ["vout"],
                  lambda e, i=i: e.dma_start(out=vout[i * 128:(i + 1) * 128, :], in_=vsb[:]))
        C.sync_on("gpsimd", ["xnt", "qt", "kt", "vout"])
        C.emit(block)
    return nc


def build_p2():
    nc = bass.Bass("TRN2", target_bir_lowering=False)
    xnt = nc.dram_tensor("xnt", [DM, TPC], BF16, kind="ExternalInput").ap()
    w1f = nc.dram_tensor("w1f", [DM, 4096], F32, kind="ExternalInput").ap()
    w2f = nc.dram_tensor("w2f", [4096, DM], F32, kind="ExternalInput").ap()
    wcf = nc.dram_tensor("wcf", [DM, 128], F32, kind="ExternalInput").ap()
    bcb = nc.dram_tensor("bcb", [128, 1], F32, kind="ExternalInput").ap()
    tied = nc.dram_tensor("tied", [128, 512], F32, kind="ExternalInput").ap()
    b1fd = nc.dram_tensor("b1fd", [128, 32], F32, kind="ExternalInput").ap()
    b2fd = nc.dram_tensor("b2fd", [128, DM], F32, kind="ExternalInput").ap()
    f0 = nc.dram_tensor("f0", [TPC, DM], F32, kind="ExternalOutput").ap()
    es = contextlib.ExitStack()
    with es:
        xnb = _sb(nc, es, "xnb", (128, 8 * TPC), BF16)
        w1b = _sb(nc, es, "w1b", (128, 8 * 4096), BF16)
        w2b = _sb(nc, es, "w2b", (128, 32 * DM), BF16)
        wcb = _sb(nc, es, "wcb", (128, 8 * 128), BF16)
        stage = _sb(nc, es, "stage", (128, 1024), F32)
        bcc = _sb(nc, es, "bcc", (128, 1), F32)
        tie = _sb(nc, es, "tie", (128, 512), F32)
        b1f = _sb(nc, es, "b1f", (128, 32), F32)
        b2b = _sb(nc, es, "b2b", (128, DM), BF16)
        logits = _sb(nc, es, "logits", (128, TPC), F32)
        gmax = _sb(nc, es, "gmax", (128, 64), F32)
        perm = _sb(nc, es, "perm", (128, TPC), F32)
        permb = _sb(nc, es, "permb", (128, TPC), BF16)
        mif = _sb(nc, es, "mif", (128, 128), F32)
        mib0 = _sb(nc, es, "mib0", (128, 128), BF16)
        mib1 = _sb(nc, es, "mib1", (128, 128), BF16)
        f0sb = _sb(nc, es, "f0sb", (128, DM), F32)
        pin0 = _ps(nc, es, "pin0", (128, 128))
        pin1 = _ps(nc, es, "pin1", (128, 128))
        po0 = _ps(nc, es, "po0", (128, 512))
        po1 = _ps(nc, es, "po1", (128, 512))
        C = Ctx(nc, es)
        block = es.enter_context(nc.Block())

        C.dma("sync", [], ["xnb"],
              lambda e: e.dma_start(out=xnb[:].rearrange("p (kc t) -> p kc t", t=TPC),
                                    in_=xnt.rearrange("(kc p) t -> p kc t", p=128)))
        C.dma("sync", [], ["bcc"], lambda e: e.dma_start(out=bcc[:], in_=bcb))
        C.dma("sync", [], ["tie"], lambda e: e.dma_start(out=tie[:], in_=tied))
        C.dma("sync", [], ["b1f"], lambda e: e.dma_start(out=b1f[:], in_=b1fd))
        C.dma("sync", [], ["stage"],
              lambda e: e.dma_start(out=stage[:, 0:DM], in_=b2fd))
        C.op("gpsimd", ["stage"], ["b2b"], lambda e: e.tensor_copy(b2b[:], stage[:, 0:DM]))
        C.dma("sync", [], ["stage"],
              lambda e: e.dma_start(out=stage[:].rearrange("p (kc m) -> p kc m", m=128),
                                    in_=wcf.rearrange("(kc p) m -> p kc m", p=128)))
        C.op("gpsimd", ["stage"], ["wcb"], lambda e: e.tensor_copy(wcb[:], stage[:]))
        for ch in range(32):
            C.dma("sync", [], ["stage"],
                  lambda e, ch=ch: e.dma_start(
                      out=stage[:].rearrange("p (kc m) -> p kc m", m=128),
                      in_=w1f[:, ch * 128:(ch + 1) * 128].rearrange("(kc p) m -> p kc m", p=128)))
            C.op("gpsimd", ["stage"], ["w1b"],
                 lambda e, ch=ch: e.tensor_copy(
                     w1b[:].rearrange("p (kc f) -> p kc f", f=4096)[:, :, ch * 128:(ch + 1) * 128],
                     stage[:].rearrange("p (kc m) -> p kc m", m=128)))
        for ch in range(32):
            C.dma("sync", [], ["stage"],
                  lambda e, ch=ch: e.dma_start(
                      out=stage[:],
                      in_=w2f[ch * 128:(ch + 1) * 128, :].rearrange("(o p) m -> p (o m)", p=128)))
            C.op("gpsimd", ["stage"], ["w2b"],
                 lambda e, ch=ch: e.tensor_copy(
                     w2b[:, ch * DM:(ch + 1) * DM], stage[:]))
        for tt in range(4):
            pk = "po0"
            for kc in range(8):
                C.op("tensor", ["wcb", "xnb"], [pk],
                     lambda e, tt=tt, kc=kc: e.matmul(
                         po0[:], wcb[:, kc * 128:(kc + 1) * 128],
                         xnb[:, kc * TPC + tt * 512: kc * TPC + (tt + 1) * 512],
                         start=(kc == 0), stop=(kc == 7), skip_group_check=True),
                     accum=(kc != 0))
            C.op("vector", [pk, "bcc"], ["logits"],
                 lambda e, tt=tt: e.tensor_scalar(
                     logits[:, tt * 512:(tt + 1) * 512], po0[:], bcc[:], None, op0=ALU.add))
            C.op("vector", ["logits", "tie"], ["logits"],
                 lambda e, tt=tt: e.tensor_tensor(
                     logits[:, tt * 512:(tt + 1) * 512],
                     logits[:, tt * 512:(tt + 1) * 512], tie[:], op=ALU.add))
        C.op("vector", ["logits"], ["gmax"],
             lambda e: e.tensor_reduce(gmax[:], logits[:].rearrange("p (g w) -> p g w", w=32),
                                       axis=AX.X, op=ALU.max))
        C.op("vector", ["logits", "gmax"], ["perm"],
             lambda e: e.tensor_tensor(perm[:].rearrange("p (g w) -> p g w", w=32),
                                       logits[:].rearrange("p (g w) -> p g w", w=32),
                                       gmax[:].unsqueeze(2).broadcast_to([128, 64, 32]),
                                       op=ALU.is_ge))
        C.op("gpsimd", ["perm"], ["permb"], lambda e: e.tensor_copy(permb[:], perm[:]))
        for tc in range(16):
            C.op("tensor", ["permb", "b2b"], ["po0"],
                 lambda e, tc=tc: e.matmul(po0[:], permb[:, tc * 128:(tc + 1) * 128],
                                           b2b[:, 0:512], start=True, stop=False, skip_group_check=True))
            C.op("tensor", ["permb", "b2b"], ["po1"],
                 lambda e, tc=tc: e.matmul(po1[:], permb[:, tc * 128:(tc + 1) * 128],
                                           b2b[:, 512:1024], start=True, stop=False, skip_group_check=True))
            for cp in range(32):
                pin = pin0 if cp % 2 == 0 else pin1
                pik = f"pin{cp % 2}"
                mik = f"mib{cp % 2}"
                for kc in range(8):
                    C.op("tensor", ["w1b", "xnb"], [pik],
                         lambda e, tc=tc, cp=cp, kc=kc, pin=pin: e.matmul(
                             pin[:], w1b[:, kc * 4096 + cp * 128: kc * 4096 + (cp + 1) * 128],
                             xnb[:, kc * TPC + tc * 128: kc * TPC + (tc + 1) * 128],
                             start=(kc == 0), stop=(kc == 7), skip_group_check=True),
                         accum=(kc != 0))
                C.op("scalar", [pik, "b1f"], ["mif"],
                     lambda e, cp=cp, pin=pin: e.activation(
                         mif[:], pin[:], AF.Relu, bias=b1f[:, cp:cp + 1]))
                mib = mib0 if cp % 2 == 0 else mib1
                C.op("vector", ["mif", "perm"], [mik],
                     lambda e, tc=tc, mib=mib: e.tensor_tensor(
                         mib[:], mif[:], perm[:, tc * 128:(tc + 1) * 128], op=ALU.mult))
                C.op("tensor", [mik, "w2b"], ["po0"],
                     lambda e, cp=cp, mib=mib: e.matmul(
                         po0[:], mib[:], w2b[:, cp * DM: cp * DM + 512],
                         start=False, stop=(cp == 31), skip_group_check=True),
                     accum=True)
                C.op("tensor", [mik, "w2b"], ["po1"],
                     lambda e, cp=cp, mib=mib: e.matmul(
                         po1[:], mib[:], w2b[:, cp * DM + 512: (cp + 1) * DM],
                         start=False, stop=(cp == 31), skip_group_check=True),
                     accum=True)
            C.op("vector", ["po0"], ["f0sb"], lambda e: e.tensor_copy(f0sb[:, 0:512], po0[:]))
            C.op("vector", ["po1"], ["f0sb"], lambda e: e.tensor_copy(f0sb[:, 512:1024], po1[:]))
            C.dma("gpsimd", ["f0sb"], ["f0"],
                  lambda e, tc=tc: e.dma_start(out=f0[tc * 128:(tc + 1) * 128, :], in_=f0sb[:]))
        C.sync_on("gpsimd", ["f0"])
        C.emit(block)
    return nc


def build_p3():
    nc = bass.Bass("TRN2", target_bir_lowering=False)
    qin = nc.dram_tensor("qin", [8 * 64, T], F32, kind="ExternalInput").ap()
    kin = nc.dram_tensor("kin", [8 * 64, T], F32, kind="ExternalInput").ap()
    vin = nc.dram_tensor("vin", [T, 8 * 64], BF16, kind="ExternalInput").ap()
    eyed = nc.dram_tensor("eyed", [128, 128], F32, kind="ExternalInput").ap()
    aout = nc.dram_tensor("aout", [8 * 64, T], F32, kind="ExternalOutput").ap()
    LOC = T - G
    es = contextlib.ExitStack()
    with es:
        qsb = _sb(nc, es, "qsb", (64, T), F32)
        ksb = _sb(nc, es, "ksb", (64, T), F32)
        qbf = _sb(nc, es, "qbf", (64, T), BF16)
        kbf = _sb(nc, es, "kbf", (64, T), BF16)
        vbf = _sb(nc, es, "vbf", (128, 32 * 64), BF16)
        eye = _sb(nc, es, "eye", (128, 128), F32)
        eyeb = _sb(nc, es, "eyeb", (128, 128), BF16)
        m1 = _sb(nc, es, "m1", (128, 1), F32)
        nb = _sb(nc, es, "nb", (128, 1), F32)
        ssum = _sb(nc, es, "ssum", (128, 1), F32)
        attn_f = _sb(nc, es, "attn_f", (128, G), F32)
        abf = _sb(nc, es, "abf", (128, 4 * G), BF16)
        atT = _sb(nc, es, "atT", (128, G), BF16)
        kle = _sb(nc, es, "kle", (64, LOC), F32)
        klp = _sb(nc, es, "klp", (64, T), BF16)
        klT = _sb(nc, es, "klT", (128, 32 * 64), BF16)
        kvb = _sb(nc, es, "kvb", (64, 64), BF16)
        aosb = _sb(nc, es, "aosb", (64, T), F32)
        ps = _ps(nc, es, "ps", (128, G))
        pg = _ps(nc, es, "pg", (64, G))
        ptb = _ps(nc, es, "ptb", (128, 128), BF16)
        pkv = _ps(nc, es, "pkv", (64, 64))
        plo = _ps(nc, es, "plo", (64, 512))
        C = Ctx(nc, es)
        block = es.enter_context(nc.Block())

        C.dma("sync", [], ["eye"], lambda e: e.dma_start(out=eye[:], in_=eyed))
        C.op("gpsimd", ["eye"], ["eyeb"], lambda e: e.tensor_copy(eyeb[:], eye[:]))
        isz = [(0, 128), (128, 128), (256, 128), (384, 25)]
        for j in range(8):
            C.dma("sync", [], ["qsb"],
                  lambda e, j=j: e.dma_start(out=qsb[:], in_=qin[j * 64:(j + 1) * 64, :]))
            C.dma("sync", [], ["ksb"],
                  lambda e, j=j: e.dma_start(out=ksb[:], in_=kin[j * 64:(j + 1) * 64, :]))
            C.dma("sync", [], ["vbf"],
                  lambda e, j=j: e.dma_start(
                      out=vbf[:].rearrange("p (tt e) -> p tt e", e=64),
                      in_=vin[:, j * 64:(j + 1) * 64].rearrange("(tt p) e -> p tt e", p=128)))
            C.op("gpsimd", ["qsb"], ["qbf"], lambda e: e.tensor_copy(qbf[:], qsb[:]))
            C.op("gpsimd", ["ksb"], ["kbf"], lambda e: e.tensor_copy(kbf[:], ksb[:]))
            for ic, (i0, sz) in enumerate(isz):
                C.op("tensor", ["qbf", "kbf"], ["ps"],
                     lambda e, i0=i0, sz=sz: e.matmul(
                         ps[0:sz, :], qbf[:, i0:i0 + sz], kbf[:, 0:G], start=True, stop=True, skip_group_check=True))
                C.op("vector", ["ps"], ["m1"],
                     lambda e, sz=sz: e.reduce_max(m1[0:sz, :], ps[0:sz, :], axis=AX.X))
                C.op("vector", ["m1"], ["nb"],
                     lambda e, sz=sz: e.tensor_scalar_mul(nb[0:sz, :], m1[0:sz, :], -0.125))
                C.op("scalar", ["ps", "nb"], ["attn_f"],
                     lambda e, sz=sz: e.activation(attn_f[0:sz, :], ps[0:sz, :], AF.Exp,
                                                   bias=nb[0:sz, :], scale=0.125))
                C.op("vector", ["attn_f"], ["ssum"],
                     lambda e, sz=sz: e.reduce_sum(ssum[0:sz, :], attn_f[0:sz, :], axis=AX.X))
                C.op("vector", ["ssum"], ["ssum"],
                     lambda e, sz=sz: e.reciprocal(ssum[0:sz, :], ssum[0:sz, :]))
                C.op("vector", ["attn_f", "ssum"], ["abf"],
                     lambda e, ic=ic, sz=sz: e.tensor_scalar(
                         abf[0:sz, ic * G:(ic + 1) * G], attn_f[0:sz, :], ssum[0:sz, :],
                         None, op0=ALU.mult))
            jsz = [(0, 128), (128, 128), (256, 128), (384, 25)]
            for jc, (j0, szj) in enumerate(jsz):
                for ic, (i0, szi) in enumerate(isz):
                    C.op("tensor", ["abf", "eyeb"], ["ptb"],
                         lambda e, ic=ic, j0=j0, szj=szj, szi=szi: e.transpose(
                             ptb[0:szj, 0:szi], abf[0:szi, ic * G + j0: ic * G + j0 + szj],
                             eyeb[0:szi, 0:szi]))
                    C.op("vector", ["ptb"], ["atT"],
                         lambda e, i0=i0, szj=szj, szi=szi: e.tensor_copy(
                             atT[0:szj, i0:i0 + szi], ptb[0:szj, 0:szi]))
                C.op("tensor", ["atT", "vbf"], ["pg"],
                     lambda e, jc=jc, szj=szj: e.matmul(
                         pg[:, :], vbf[0:szj, jc * 64:(jc + 1) * 64], atT[0:szj, 0:G],
                         start=(jc == 0), stop=(jc == 3), skip_group_check=True),
                     accum=(jc != 0))
            C.op("vector", ["pg"], ["aosb"], lambda e: e.tensor_copy(aosb[:, 0:G], pg[:, :]))
            C.op("vector", ["ksb"], ["m1"],
                 lambda e: e.reduce_max(m1[0:64, :], ksb[:, G:T], axis=AX.X))
            C.op("vector", ["m1"], ["nb"],
                 lambda e: e.tensor_scalar_mul(nb[0:64, :], m1[0:64, :], -1.0))
            C.op("scalar", ["ksb", "nb"], ["kle"],
                 lambda e: e.activation(kle[:], ksb[:, G:T], AF.Exp, bias=nb[0:64, :]))
            C.op("vector", ["kle"], ["ssum"],
                 lambda e: e.reduce_sum(ssum[0:64, :], kle[:], axis=AX.X))
            C.op("vector", ["ssum"], ["ssum"],
                 lambda e: e.reciprocal(ssum[0:64, :], ssum[0:64, :]))
            C.op("vector", [], ["klp"], lambda e: e.memset(klp[:, 0:G], 0.0))
            C.op("vector", ["kle", "ssum"], ["klp"],
                 lambda e: e.tensor_scalar(klp[:, G:T], kle[:], ssum[0:64, :], None, op0=ALU.mult))
            for jc in range(32):
                C.op("tensor", ["klp", "eyeb"], ["ptb"],
                     lambda e, jc=jc: e.transpose(
                         ptb[:, 0:64], klp[:, jc * 128:(jc + 1) * 128], eyeb[0:64, 0:64]))
                C.op("vector", ["ptb"], ["klT"],
                     lambda e, jc=jc: e.tensor_copy(klT[:, jc * 64:(jc + 1) * 64], ptb[:, 0:64]))
            for jc in range(32):
                C.op("tensor", ["klT", "vbf"], ["pkv"],
                     lambda e, jc=jc: e.matmul(
                         pkv[:], vbf[:, jc * 64:(jc + 1) * 64], klT[:, jc * 64:(jc + 1) * 64],
                         start=(jc == 0), stop=(jc == 31), skip_group_check=True),
                     accum=(jc != 0))
            C.op("vector", ["pkv"], ["kvb"], lambda e: e.tensor_copy(kvb[:], pkv[:]))
            off = G
            while off < T:
                sz = min(512, T - off)
                C.op("tensor", ["kvb", "qbf"], ["plo"],
                     lambda e, off=off, sz=sz: e.matmul(
                         plo[:, 0:sz], kvb[:], qbf[:, off:off + sz], start=True, stop=True, skip_group_check=True))
                C.op("vector", ["plo"], ["aosb"],
                     lambda e, off=off, sz=sz: e.tensor_copy(aosb[:, off:off + sz], plo[:, 0:sz]))
                off += sz
            C.dma("gpsimd", ["aosb"], ["aout"],
                  lambda e, j=j: e.dma_start(out=aout[j * 64:(j + 1) * 64, :], in_=aosb[:]))
        C.sync_on("gpsimd", ["aout"])
        C.emit(block)
    return nc


def build_p4():
    nc = bass.Bass("TRN2", target_bir_lowering=False)
    x = nc.dram_tensor("x", [TPC, DM], F32, kind="ExternalInput").ap()
    f0 = nc.dram_tensor("f0", [TPC, DM], F32, kind="ExternalInput").ap()
    ao = nc.dram_tensor("ao", [DM, TPC], F32, kind="ExternalInput").ap()
    wo = nc.dram_tensor("wo", [DM, DM], F32, kind="ExternalInput").ap()
    x1 = nc.dram_tensor("x1", [TPC, DM], F32, kind="ExternalOutput").ap()
    es = contextlib.ExitStack()
    with es:
        stage = _sb(nc, es, "stage", (128, 8 * TPC), F32)
        aob = _sb(nc, es, "aob", (128, 8 * TPC), BF16)
        wob = _sb(nc, es, "wob", (128, 8 * DM), BF16)
        xtile = _sb(nc, es, "xtile", (128, DM), F32)
        ftile = _sb(nc, es, "ftile", (128, DM), F32)
        osb = _sb(nc, es, "osb", (128, DM), F32)
        p0 = _ps(nc, es, "p0", (128, 512))
        p1 = _ps(nc, es, "p1", (128, 512))
        C = Ctx(nc, es)
        block = es.enter_context(nc.Block())
        C.dma("sync", [], ["stage"],
              lambda e: e.dma_start(out=stage[:].rearrange("p (m t) -> p m t", t=TPC),
                                    in_=ao.rearrange("(m p) t -> p m t", p=128)))
        C.op("gpsimd", ["stage"], ["aob"], lambda e: e.tensor_copy(aob[:], stage[:]))
        C.dma("sync", [], ["stage"],
              lambda e: e.dma_start(out=stage[:, 0:8 * DM].rearrange("p (kc m) -> p kc m", m=DM),
                                    in_=wo.rearrange("(kc p) m -> p kc m", p=128)))
        C.op("gpsimd", ["stage"], ["wob"], lambda e: e.tensor_copy(wob[:], stage[:, 0:8 * DM]))
        for tc in range(16):
            for nn in range(2):
                pb = p0 if nn == 0 else p1
                pk = f"p{nn}"
                for kc in range(8):
                    C.op("tensor", ["aob", "wob"], [pk],
                         lambda e, tc=tc, nn=nn, kc=kc, pb=pb: e.matmul(
                             pb[:], aob[:, kc * TPC + tc * 128: kc * TPC + (tc + 1) * 128],
                             wob[:, kc * DM + nn * 512: kc * DM + (nn + 1) * 512],
                             start=(kc == 0), stop=(kc == 7), skip_group_check=True),
                         accum=(kc != 0))
            C.dma("sync", [], ["xtile"],
                  lambda e, tc=tc: e.dma_start(out=xtile[:], in_=x[tc * 128:(tc + 1) * 128, :]))
            C.dma("sync", [], ["ftile"],
                  lambda e, tc=tc: e.dma_start(out=ftile[:], in_=f0[tc * 128:(tc + 1) * 128, :]))
            for nn in range(2):
                pb = p0 if nn == 0 else p1
                pk = f"p{nn}"
                C.op("vector", [pk, "xtile"], ["osb"],
                     lambda e, nn=nn, pb=pb: e.tensor_tensor(
                         osb[:, nn * 512:(nn + 1) * 512], pb[:],
                         xtile[:, nn * 512:(nn + 1) * 512], op=ALU.add))
                C.op("vector", ["osb", "ftile"], ["osb"],
                     lambda e, nn=nn: e.tensor_tensor(
                         osb[:, nn * 512:(nn + 1) * 512], osb[:, nn * 512:(nn + 1) * 512],
                         ftile[:, nn * 512:(nn + 1) * 512], op=ALU.add))
            C.dma("gpsimd", ["osb"], ["x1"],
                  lambda e, tc=tc: e.dma_start(out=x1[tc * 128:(tc + 1) * 128, :], in_=osb[:]))
        C.sync_on("gpsimd", ["x1"])
        C.emit(block)
    return nc


def build_p5():
    W = 1152
    nc = bass.Bass("TRN2", target_bir_lowering=False)
    x1p = nc.dram_tensor("x1p", [2176, DM], F32, kind="ExternalInput").ap()
    inw = nc.dram_tensor("inw", [DM, 2 * DI], F32, kind="ExternalInput").ap()
    inb = nc.dram_tensor("inb", [128, 32], F32, kind="ExternalInput").ap()
    cwd = nc.dram_tensor("cwd", [128, 64], F32, kind="ExternalInput").ap()
    cbd = nc.dram_tensor("cbd", [128, 16], F32, kind="ExternalInput").ap()
    bwd = nc.dram_tensor("bwd", [DI, DS], F32, kind="ExternalInput").ap()
    bbd = nc.dram_tensor("bbd", [16, 1], F32, kind="ExternalInput").ap()
    cwd2 = nc.dram_tensor("cwd2", [DI, DS], F32, kind="ExternalInput").ap()
    cbd2 = nc.dram_tensor("cbd2", [16, 1], F32, kind="ExternalInput").ap()
    dtw = nc.dram_tensor("dtw", [DI, DI], F32, kind="ExternalInput").ap()
    dtb = nc.dram_tensor("dtb", [128, 16], F32, kind="ExternalInput").ap()
    eyed = nc.dram_tensor("eyed", [128, 128], F32, kind="ExternalInput").ap()
    epsd = nc.dram_tensor("epsd", [128, 1], F32, kind="ExternalInput").ap()
    oned = nc.dram_tensor("oned", [128, 1], F32, kind="ExternalInput").ap()
    xco = nc.dram_tensor("xco", [DI, TPC], F32, kind="ExternalOutput").ap()
    dto = nc.dram_tensor("dto", [DI, TPC], F32, kind="ExternalOutput").ap()
    sgo = nc.dram_tensor("sgo", [DI, TPC], F32, kind="ExternalOutput").ap()
    bmo = nc.dram_tensor("bmo", [DS, TPC], F32, kind="ExternalOutput").ap()
    cmo = nc.dram_tensor("cmo", [DS, TPC], F32, kind="ExternalOutput").ap()
    xn2o = nc.dram_tensor("xn2o", [DM, TPC], F32, kind="ExternalOutput").ap()
    es = contextlib.ExitStack()
    with es:
        xtile = _sb(nc, es, "xtile", (128, DM), F32)
        xn = _sb(nc, es, "xn", (128, DM), F32)
        scr = _sb(nc, es, "scr", (128, DM), F32)
        mean = _sb(nc, es, "mean", (128, 1), F32)
        var = _sb(nc, es, "var", (128, 1), F32)
        epscol = _sb(nc, es, "epscol", (128, 1), F32)
        onecol = _sb(nc, es, "onecol", (128, 1), F32)
        eye = _sb(nc, es, "eye", (128, 128), F32)
        x2T = _sb(nc, es, "x2T", (128, 8 * W), F32)
        x2Tb = _sb(nc, es, "x2Tb", (128, 8 * W), BF16)
        wstage = _sb(nc, es, "wstage", (128, 16 * 128), F32)
        iwb = _sb(nc, es, "iwb", (128, 8 * 128), BF16)
        dwb = _sb(nc, es, "dwb", (128, 16 * 128), BF16)
        bwb = _sb(nc, es, "bwb", (128, 16 * DS), BF16)
        cwb = _sb(nc, es, "cwb", (128, 16 * DS), BF16)
        inbc = _sb(nc, es, "inbc", (128, 32), F32)
        cw = _sb(nc, es, "cw", (128, 64), F32)
        cb = _sb(nc, es, "cb", (128, 16), F32)
        dtbc = _sb(nc, es, "dtbc", (128, 16), F32)
        bbc = _sb(nc, es, "bbc", (16, 1), F32)
        cbc2 = _sb(nc, es, "cbc2", (16, 1), F32)
        xzm = _sb(nc, es, "xzm", (128, W), F32)
        acc = _sb(nc, es, "acc", (128, 1024), F32)
        xcm = _sb(nc, es, "xcm", (128, 1024), F32)
        xcb = _sb(nc, es, "xcb", (128, 16 * 1024), BF16)
        sgm = _sb(nc, es, "sgm", (128, W), F32)
        zb = _sb(nc, es, "zb", (128, 512), F32)
        ab = _sb(nc, es, "ab", (128, 512), F32)
        lb = _sb(nc, es, "lb", (128, 512), F32)
        mb = _sb(nc, es, "mb", (128, 512), F32)
        bm = _sb(nc, es, "bm", (16, 512), F32)
        ptr = _ps(nc, es, "ptr", (128, 128))
        pp0 = _ps(nc, es, "pp0", (128, 512))
        pp1 = _ps(nc, es, "pp1", (128, 512))
        pbc = _ps(nc, es, "pbc", (16, 512))
        C = Ctx(nc, es)
        block = es.enter_context(nc.Block())

        C.dma("sync", [], ["epscol"], lambda e: e.dma_start(out=epscol[:], in_=epsd))
        C.dma("sync", [], ["onecol"], lambda e: e.dma_start(out=onecol[:], in_=oned))
        C.dma("sync", [], ["eye"], lambda e: e.dma_start(out=eye[:], in_=eyed))
        C.dma("sync", [], ["inbc"], lambda e: e.dma_start(out=inbc[:], in_=inb))
        C.dma("sync", [], ["cw"], lambda e: e.dma_start(out=cw[:], in_=cwd))
        C.dma("sync", [], ["cb"], lambda e: e.dma_start(out=cb[:], in_=cbd))
        C.dma("sync", [], ["dtbc"], lambda e: e.dma_start(out=dtbc[:], in_=dtb))
        C.dma("sync", [], ["bbc"], lambda e: e.dma_start(out=bbc[:], in_=bbd))
        C.dma("sync", [], ["cbc2"], lambda e: e.dma_start(out=cbc2[:], in_=cbd2))
        C.dma("sync", [], ["wstage"],
              lambda e: e.dma_start(out=wstage[:, 0:16 * DS].rearrange("p (kc n) -> p kc n", n=DS),
                                    in_=bwd.rearrange("(kc p) n -> p kc n", p=128)))
        C.op("gpsimd", ["wstage"], ["bwb"], lambda e: e.tensor_copy(bwb[:], wstage[:, 0:16 * DS]))
        C.dma("sync", [], ["wstage"],
              lambda e: e.dma_start(out=wstage[:, 0:16 * DS].rearrange("p (kc n) -> p kc n", n=DS),
                                    in_=cwd2.rearrange("(kc p) n -> p kc n", p=128)))
        C.op("gpsimd", ["wstage"], ["cwb"], lambda e: e.tensor_copy(cwb[:], wstage[:, 0:16 * DS]))

        for th in range(2):
            r0 = th * 1024
            for i in range(9):
                C.dma("sync", [], ["xtile"],
                      lambda e, r0=r0, i=i: e.dma_start(
                          out=xtile[:], in_=x1p[r0 + i * 128: r0 + (i + 1) * 128, :]))
                _ln_tile(C, xtile[:], "xtile", xn, mean, var, scr, epscol)
                for kc in range(8):
                    C.op("tensor", ["xn", "eye"], ["ptr"],
                         lambda e, kc=kc: e.transpose(ptr[:], xn[:, kc * 128:(kc + 1) * 128],
                                                      eye[:]))
                    C.op("vector", ["ptr"], ["x2T"],
                         lambda e, kc=kc, i=i: e.tensor_copy(
                             x2T[:, kc * W + i * 128: kc * W + (i + 1) * 128], ptr[:]))
            C.op("gpsimd", ["x2T"], ["x2Tb"], lambda e: e.tensor_copy(x2Tb[:], x2T[:]))
            C.dma("gpsimd", ["x2T"], ["xn2o"],
                  lambda e, th=th: e.dma_start(
                      out=xn2o[:, th * 1024:(th + 1) * 1024].rearrange("(kc p) t -> p kc t", p=128),
                      in_=x2T[:].rearrange("p (kc t) -> p kc t", t=W)[:, :, 1:1025]))
            for m in range(32):
                C.dma("sync", [], ["wstage"],
                      lambda e, m=m: e.dma_start(
                          out=wstage[:, 0:8 * 128].rearrange("p (kc c) -> p kc c", c=128),
                          in_=inw[:, m * 128:(m + 1) * 128].rearrange("(kc p) c -> p kc c", p=128)))
                C.op("gpsimd", ["wstage"], ["iwb"],
                     lambda e: e.tensor_copy(iwb[:], wstage[:, 0:8 * 128]))
                for nt in range(3):
                    pb = pp0 if nt % 2 == 0 else pp1
                    pk = f"pp{nt % 2}"
                    for kc in range(8):
                        C.op("tensor", ["iwb", "x2Tb"], [pk],
                             lambda e, nt=nt, kc=kc, pb=pb: e.matmul(
                                 pb[:, 0:384], iwb[:, kc * 128:(kc + 1) * 128],
                                 x2Tb[:, kc * W + nt * 384: kc * W + (nt + 1) * 384],
                                 start=(kc == 0), stop=(kc == 7), skip_group_check=True),
                             accum=(kc != 0))
                    if m < 16:
                        C.op("vector", [pk, "inbc"], ["xzm"],
                             lambda e, m=m, nt=nt, pb=pb: e.tensor_scalar(
                                 xzm[:, nt * 384:(nt + 1) * 384], pb[:, 0:384],
                                 inbc[:, m:m + 1], None, op0=ALU.add))
                    else:
                        C.op("scalar", [pk, "inbc"], ["sgm"],
                             lambda e, m=m, nt=nt, pb=pb: e.activation(
                                 sgm[:, nt * 384:(nt + 1) * 384], pb[:, 0:384], AF.Silu,
                                 bias=inbc[:, m:m + 1]))
                if m < 16:
                    C.op("vector", ["xzm", "cw"], ["acc"],
                         lambda e, m=m: e.tensor_scalar(
                             acc[:], xzm[:, 0:1024], cw[:, m * 4: m * 4 + 1], None, op0=ALU.mult))
                    for jj in range(1, 4):
                        C.op("vector", ["xzm", "cw", "acc"], ["acc"],
                             lambda e, m=m, jj=jj: e.scalar_tensor_tensor(
                                 acc[:], xzm[:, jj:jj + 1024], cw[:, m * 4 + jj: m * 4 + jj + 1],
                                 acc[:], op0=ALU.mult, op1=ALU.add))
                    C.op("scalar", ["acc", "cb"], ["xcm"],
                         lambda e, m=m: e.activation(xcm[:], acc[:], AF.Silu,
                                                     bias=cb[:, m:m + 1]))
                    C.dma("gpsimd", ["xcm"], ["xco"],
                          lambda e, m=m, th=th: e.dma_start(
                              out=xco[m * 128:(m + 1) * 128, th * 1024:(th + 1) * 1024],
                              in_=xcm[:]))
                    C.op("gpsimd", ["xcm"], ["xcb"],
                         lambda e, m=m: e.tensor_copy(xcb[:, m * 1024:(m + 1) * 1024], xcm[:]))
                else:
                    C.dma("gpsimd", ["sgm"], ["sgo"],
                          lambda e, m=m, th=th: e.dma_start(
                              out=sgo[(m - 16) * 128:(m - 15) * 128, th * 1024:(th + 1) * 1024],
                              in_=sgm[:, 1:1025]))
            for m2 in range(16):
                C.dma("sync", [], ["wstage"],
                      lambda e, m2=m2: e.dma_start(
                          out=wstage[:].rearrange("p (kc c) -> p kc c", c=128),
                          in_=dtw[:, m2 * 128:(m2 + 1) * 128].rearrange("(kc p) c -> p kc c",
                                                                       p=128)))
                C.op("gpsimd", ["wstage"], ["dwb"], lambda e: e.tensor_copy(dwb[:], wstage[:]))
                for nt in range(2):
                    pb = pp0 if nt == 0 else pp1
                    pk = f"pp{nt}"
                    for kc in range(16):
                        C.op("tensor", ["dwb", "xcb"], [pk],
                             lambda e, nt=nt, kc=kc, pb=pb: e.matmul(
                                 pb[:], dwb[:, kc * 128:(kc + 1) * 128],
                                 xcb[:, kc * 1024 + nt * 512: kc * 1024 + (nt + 1) * 512],
                                 start=(kc == 0), stop=(kc == 15), skip_group_check=True),
                             accum=(kc != 0))
                    C.op("vector", [pk, "dtbc"], ["zb"],
                         lambda e, m2=m2, pb=pb: e.tensor_scalar(
                             zb[:], pb[:], dtbc[:, m2:m2 + 1], None, op0=ALU.add))
                    C.op("scalar", ["zb"], ["ab"], lambda e: e.activation(ab[:], zb[:], AF.Abs))
                    C.op("scalar", ["ab"], ["ab"],
                         lambda e: e.activation(ab[:], ab[:], AF.Exp, scale=-1.0))
                    C.op("scalar", ["ab", "onecol"], ["lb"],
                         lambda e: e.activation(lb[:], ab[:], AF.Ln, bias=onecol[:]))
                    C.op("vector", ["zb"], ["mb"],
                         lambda e: e.tensor_scalar_max(mb[:], zb[:], 0.0))
                    C.op("vector", ["lb", "mb"], ["zb"],
                         lambda e: e.tensor_tensor(zb[:], lb[:], mb[:], op=ALU.add))
                    C.dma("gpsimd", ["zb"], ["dto"],
                          lambda e, m2=m2, nt=nt, th=th: e.dma_start(
                              out=dto[m2 * 128:(m2 + 1) * 128,
                                      th * 1024 + nt * 512: th * 1024 + (nt + 1) * 512],
                              in_=zb[:]))
            for wsb, bias, outap, okey in ((bwb, bbc, bmo, "bmo"), (cwb, cbc2, cmo, "cmo")):
                for nt in range(2):
                    for kc in range(16):
                        C.op("tensor", ["bwb", "cwb", "xcb"], ["pbc"],
                             lambda e, wsb=wsb, nt=nt, kc=kc: e.matmul(
                                 pbc[:], wsb[:, kc * DS:(kc + 1) * DS],
                                 xcb[:, kc * 1024 + nt * 512: kc * 1024 + (nt + 1) * 512],
                                 start=(kc == 0), stop=(kc == 15), skip_group_check=True),
                             accum=(kc != 0))
                    C.op("vector", ["pbc"], ["bm"],
                         lambda e, bias=bias: e.tensor_scalar(
                             bm[:], pbc[:], bias[:], None, op0=ALU.add))
                    C.dma("gpsimd", ["bm"], [okey],
                          lambda e, outap=outap, nt=nt, th=th: e.dma_start(
                              out=outap[:, th * 1024 + nt * 512: th * 1024 + (nt + 1) * 512],
                              in_=bm[:]))
        C.sync_on("gpsimd", ["xco", "dto", "sgo", "bmo", "cmo", "xn2o"])
        C.emit(block)
    return nc


def build_p6():
    nc = bass.Bass("TRN2", target_bir_lowering=False)
    dtd = nc.dram_tensor("dtd", [1024, T], F32, kind="ExternalInput").ap()
    xcd = nc.dram_tensor("xcd", [1024, T], F32, kind="ExternalInput").ap()
    sgd = nc.dram_tensor("sgd", [1024, T], F32, kind="ExternalInput").ap()
    bmd = nc.dram_tensor("bmd", [DS, T], F32, kind="ExternalInput").ap()
    cmd = nc.dram_tensor("cmd", [DS, T], F32, kind="ExternalInput").ap()
    acold = nc.dram_tensor("acold", [128, 128], F32, kind="ExternalInput").ap()
    dpd = nc.dram_tensor("dpd", [128, 8], F32, kind="ExternalInput").ap()
    owd = nc.dram_tensor("owd", [1024, DM], F32, kind="ExternalInput").ap()
    onesd = nc.dram_tensor("onesd", [1, 128], F32, kind="ExternalInput").ap()
    opo = nc.dram_tensor("opo", [DM, T], F32, kind="ExternalOutput").ap()
    CH = 512
    es = contextlib.ExitStack()
    with es:
        dt = _sb(nc, es, "dt", (128, 8 * CH), F32)
        xc = _sb(nc, es, "xc", (128, 8 * CH), F32)
        sg = _sb(nc, es, "sg", (128, 8 * CH), F32)
        bmr = _sb(nc, es, "bmr", (1, CH), F32)
        cmr = _sb(nc, es, "cmr", (1, CH), F32)
        Bbc = _sb(nc, es, "Bbc", (128, DS * CH), F32)
        Cbc = _sb(nc, es, "Cbc", (128, DS * CH), F32)
        acol = _sb(nc, es, "acol", (128, 128), F32)
        dpc = _sb(nc, es, "dpc", (128, 8), F32)
        ones1 = _sb(nc, es, "ones1", (1, 128), F32)
        hstate = _sb(nc, es, "hstate", (128, 128), F32)
        dA = [_sb(nc, es, f"dA{i}", (128, CH), F32) for i in range(2)]
        dBx = [_sb(nc, es, f"dBx{i}", (128, CH), F32) for i in range(2)]
        hh = [_sb(nc, es, f"hh{i}", (128, CH), F32) for i in range(2)]
        yt = [_sb(nc, es, f"yt{i}", (128, CH), F32) for i in range(2)]
        ycc = _sb(nc, es, "ycc", (128, CH), F32)
        ybf = _sb(nc, es, "ybf", (128, 8 * CH), BF16)
        owst = _sb(nc, es, "owst", (128, 4 * DM), F32)
        owb = _sb(nc, es, "owb", (128, 8 * DM), BF16)
        osb = _sb(nc, es, "osb", (128, CH), F32)
        pb0 = _ps(nc, es, "pb0", (128, 512))
        pb1 = _ps(nc, es, "pb1", (128, 512))
        C = Ctx(nc, es)
        block = es.enter_context(nc.Block())

        C.dma("sync", [], ["acol"], lambda e: e.dma_start(out=acol[:], in_=acold))
        C.dma("sync", [], ["dpc"], lambda e: e.dma_start(out=dpc[:], in_=dpd))
        C.dma("sync", [], ["ones1"], lambda e: e.dma_start(out=ones1[:], in_=onesd))
        for hf in range(2):
            C.dma("sync", [], ["owst"],
                  lambda e, hf=hf: e.dma_start(
                      out=owst[:].rearrange("p (cc m) -> p cc m", m=DM),
                      in_=owd.rearrange("(cc p) m -> p cc m", p=128)[:, hf * 4:(hf + 1) * 4, :]))
            C.op("gpsimd", ["owst"], ["owb"],
                 lambda e, hf=hf: e.tensor_copy(owb[:, hf * 4 * DM:(hf + 1) * 4 * DM], owst[:]))
        C.op("vector", [], ["hstate"], lambda e: e.memset(hstate[:], 0.0))
        for ch in range(8):
            t0 = ch * CH
            C.dma("sync", [], ["dt"],
                  lambda e, t0=t0: e.dma_start(
                      out=dt[:].rearrange("p (cc t) -> p cc t", t=CH),
                      in_=dtd[:, t0:t0 + CH].rearrange("(cc p) t -> p cc t", p=128)))
            C.dma("sync", [], ["xc"],
                  lambda e, t0=t0: e.dma_start(
                      out=xc[:].rearrange("p (cc t) -> p cc t", t=CH),
                      in_=xcd[:, t0:t0 + CH].rearrange("(cc p) t -> p cc t", p=128)))
            C.dma("sync", [], ["sg"],
                  lambda e, t0=t0: e.dma_start(
                      out=sg[:].rearrange("p (cc t) -> p cc t", t=CH),
                      in_=sgd[:, t0:t0 + CH].rearrange("(cc p) t -> p cc t", p=128)))
            for n in range(DS):
                pb = pb0 if n % 2 == 0 else pb1
                pk = f"pb{n % 2}"
                C.dma("sync", [], ["bmr"],
                      lambda e, t0=t0, n=n: e.dma_start(
                          out=bmr[:], in_=bmd[n:n + 1, t0:t0 + CH]))
                C.op("tensor", ["bmr", "ones1"], [pk],
                     lambda e, pb=pb: e.matmul(
                         pb[:], ones1[:], bmr[:], start=True, stop=True, skip_group_check=True))
                C.op("vector", [pk], ["Bbc"],
                     lambda e, n=n, pb=pb: e.tensor_copy(Bbc[:, n * CH:(n + 1) * CH], pb[:]))
                C.dma("sync", [], ["cmr"],
                      lambda e, t0=t0, n=n: e.dma_start(
                          out=cmr[:], in_=cmd[n:n + 1, t0:t0 + CH]))
                C.op("tensor", ["cmr", "ones1"], [pk],
                     lambda e, pb=pb: e.matmul(
                         pb[:], ones1[:], cmr[:], start=True, stop=True, skip_group_check=True))
                C.op("vector", [pk], ["Cbc"],
                     lambda e, n=n, pb=pb: e.tensor_copy(Cbc[:, n * CH:(n + 1) * CH], pb[:]))
            for cc in range(8):
                C.op("vector", ["xc", "dpc"], ["ycc"],
                     lambda e, cc=cc: e.tensor_scalar(
                         ycc[:], xc[:, cc * CH:(cc + 1) * CH], dpc[:, cc:cc + 1],
                         None, op0=ALU.mult))
                for n in range(DS):
                    i = n % 2
                    kda, kdb, kh, kyt = f"dA{i}", f"dBx{i}", f"hh{i}", f"yt{i}"
                    col = cc * 16 + n
                    C.op("scalar", ["dt", "acol"], [kda],
                         lambda e, cc=cc, col=col, i=i: e.activation(
                             dA[i][:], dt[:, cc * CH:(cc + 1) * CH], AF.Exp,
                             scale=acol[:, col:col + 1]))
                    C.op("gpsimd", ["dt", "Bbc"], [kdb],
                         lambda e, cc=cc, n=n, i=i: e.tensor_tensor(
                             dBx[i][:], dt[:, cc * CH:(cc + 1) * CH],
                             Bbc[:, n * CH:(n + 1) * CH], op=ALU.mult))
                    C.op("gpsimd", ["xc", kdb], [kdb],
                         lambda e, cc=cc, i=i: e.tensor_tensor(
                             dBx[i][:], dBx[i][:], xc[:, cc * CH:(cc + 1) * CH], op=ALU.mult))
                    C.op("vector", [kda, kdb, "hstate"], [kh],
                         lambda e, col=col, i=i: e.tensor_tensor_scan(
                             hh[i][:], dA[i][:], dBx[i][:], hstate[:, col:col + 1],
                             op0=ALU.mult, op1=ALU.add))
                    C.op("vector", [kh], ["hstate"],
                         lambda e, col=col, i=i: e.tensor_copy(
                             hstate[:, col:col + 1], hh[i][:, CH - 1:CH]))
                    C.op("gpsimd", [kh, "Cbc"], [kyt],
                         lambda e, n=n, i=i: e.tensor_tensor(
                             yt[i][:], hh[i][:], Cbc[:, n * CH:(n + 1) * CH], op=ALU.mult))
                    C.op("vector", [kyt, "ycc"], ["ycc"],
                         lambda e, i=i: e.tensor_tensor(ycc[:], ycc[:], yt[i][:], op=ALU.add))
                C.op("vector", ["ycc", "sg"], ["ycc"],
                     lambda e, cc=cc: e.tensor_tensor(
                         ycc[:], ycc[:], sg[:, cc * CH:(cc + 1) * CH], op=ALU.mult))
                C.op("gpsimd", ["ycc"], ["ybf"],
                     lambda e, cc=cc: e.tensor_copy(ybf[:, cc * CH:(cc + 1) * CH], ycc[:]))
            for m in range(8):
                pb = pb0 if m % 2 == 0 else pb1
                pk = f"pb{m % 2}"
                for cc in range(8):
                    C.op("tensor", ["owb", "ybf"], [pk],
                         lambda e, m=m, cc=cc, pb=pb: e.matmul(
                             pb[:], owb[:, cc * DM + m * 128: cc * DM + (m + 1) * 128],
                             ybf[:, cc * CH:(cc + 1) * CH],
                             start=(cc == 0), stop=(cc == 7), skip_group_check=True),
                         accum=(cc != 0))
                C.op("vector", [pk], ["osb"], lambda e, pb=pb: e.tensor_copy(osb[:], pb[:]))
                C.dma("gpsimd", ["osb"], ["opo"],
                      lambda e, m=m, t0=t0: e.dma_start(
                          out=opo[m * 128:(m + 1) * 128, t0:t0 + CH], in_=osb[:]))
        C.sync_on("gpsimd", ["opo"])
        C.emit(block)
    return nc


def build_p7():
    nc = bass.Bass("TRN2", target_bir_lowering=False)
    x1 = nc.dram_tensor("x1", [TPC, DM], F32, kind="ExternalInput").ap()
    xn2 = nc.dram_tensor("xn2", [DM, TPC], F32, kind="ExternalInput").ap()
    opa = nc.dram_tensor("opa", [DM, TPC], F32, kind="ExternalInput").ap()
    opb = nc.dram_tensor("opb", [DM, TPC], F32, kind="ExternalInput").ap()
    outbd = nc.dram_tensor("outbd", [128, 8], F32, kind="ExternalInput").ap()
    gwd = nc.dram_tensor("gwd", [DM, NE], F32, kind="ExternalInput").ap()
    gbd = nc.dram_tensor("gbd", [1, NE], F32, kind="ExternalInput").ap()
    onesd = nc.dram_tensor("onesd", [1, 128], F32, kind="ExternalInput").ap()
    eyed = nc.dram_tensor("eyed", [128, 128], F32, kind="ExternalInput").ap()
    base = nc.dram_tensor("base", [DM, TPC], F32, kind="ExternalOutput").ap()
    cto = nc.dram_tensor("cto", [NE, TPC], F32, kind="ExternalOutput").ap()
    xnbo = nc.dram_tensor("xnbo", [DM, TPC], BF16, kind="ExternalOutput").ap()
    es = contextlib.ExitStack()
    with es:
        x1T = _sb(nc, es, "x1T", (128, 8 * TPC), F32)
        xnb = _sb(nc, es, "xnb", (128, 8 * TPC), BF16)
        xtile = _sb(nc, es, "xtile", (128, DM), F32)
        xstage = _sb(nc, es, "xstage", (128, TPC), F32)
        ostage = _sb(nc, es, "ostage", (128, TPC), F32)
        outbc = _sb(nc, es, "outbc", (128, 8), F32)
        gw = _sb(nc, es, "gw", (128, 8 * NE), F32)
        gwb = _sb(nc, es, "gwb", (128, 8 * NE), BF16)
        gb = _sb(nc, es, "gb", (1, NE), F32)
        gbb = _sb(nc, es, "gbb", (1, NE), BF16)
        ones1 = _sb(nc, es, "ones1", (1, 128), F32)
        onesb = _sb(nc, es, "onesb", (1, 128), BF16)
        eye = _sb(nc, es, "eye", (128, 128), F32)
        s = _sb(nc, es, "s", (128, NE), F32)
        eq1 = _sb(nc, es, "eq1", (128, NE), F32)
        s2 = _sb(nc, es, "s2", (128, NE), F32)
        eq2 = _sb(nc, es, "eq2", (128, NE), F32)
        m1 = _sb(nc, es, "m1", (128, 1), F32)
        m2 = _sb(nc, es, "m2", (128, 1), F32)
        dd = _sb(nc, es, "dd", (128, 1), F32)
        ed = _sb(nc, es, "ed", (128, 1), F32)
        p1c = _sb(nc, es, "p1c", (128, 1), F32)
        p2c = _sb(nc, es, "p2c", (128, 1), F32)
        comb = _sb(nc, es, "comb", (128, NE), F32)
        ctsb = _sb(nc, es, "ctsb", (NE, TPC), F32)
        ptr = _ps(nc, es, "ptr", (128, 128))
        pgs = _ps(nc, es, "pgs", (128, NE))
        C = Ctx(nc, es)
        block = es.enter_context(nc.Block())

        C.dma("sync", [], ["outbc"], lambda e: e.dma_start(out=outbc[:], in_=outbd))
        C.dma("sync", [], ["ones1"], lambda e: e.dma_start(out=ones1[:], in_=onesd))
        C.dma("sync", [], ["eye"], lambda e: e.dma_start(out=eye[:], in_=eyed))
        C.dma("sync", [], ["gb"], lambda e: e.dma_start(out=gb[:], in_=gbd))
        C.dma("sync", [], ["gw"],
              lambda e: e.dma_start(out=gw[:].rearrange("p (kc m) -> p kc m", m=NE),
                                    in_=gwd.rearrange("(kc p) m -> p kc m", p=128)))
        C.op("gpsimd", ["gw"], ["gwb"], lambda e: e.tensor_copy(gwb[:], gw[:]))
        C.op("gpsimd", ["gb"], ["gbb"], lambda e: e.tensor_copy(gbb[:], gb[:]))
        C.op("gpsimd", ["ones1"], ["onesb"], lambda e: e.tensor_copy(onesb[:], ones1[:]))
        for tc in range(16):
            C.dma("sync", [], ["xtile"],
                  lambda e, tc=tc: e.dma_start(out=xtile[:], in_=x1[tc * 128:(tc + 1) * 128, :]))
            for kc in range(8):
                C.op("tensor", ["xtile", "eye"], ["ptr"],
                     lambda e, kc=kc: e.transpose(ptr[:], xtile[:, kc * 128:(kc + 1) * 128],
                                                  eye[:]))
                C.op("vector", ["ptr"], ["x1T"],
                     lambda e, kc=kc, tc=tc: e.tensor_copy(
                         x1T[:, kc * TPC + tc * 128: kc * TPC + (tc + 1) * 128], ptr[:]))
        for m in range(8):
            C.dma("sync", [], ["xstage"],
                  lambda e, m=m: e.dma_start(out=xstage[:], in_=xn2[m * 128:(m + 1) * 128, :]))
            C.op("gpsimd", ["xstage"], ["xnb"],
                 lambda e, m=m: e.tensor_copy(xnb[:, m * TPC:(m + 1) * TPC], xstage[:]))
            C.op("vector", ["xstage", "x1T"], ["xstage"],
                 lambda e, m=m: e.tensor_tensor(
                     xstage[:], xstage[:], x1T[:, m * TPC:(m + 1) * TPC], op=ALU.add))
            C.dma("sync", [], ["ostage"],
                  lambda e, m=m: e.dma_start(out=ostage[:], in_=opa[m * 128:(m + 1) * 128, :]))
            C.op("vector", ["xstage", "ostage"], ["xstage"],
                 lambda e: e.tensor_tensor(xstage[:], xstage[:], ostage[:], op=ALU.add))
            C.dma("sync", [], ["ostage"],
                  lambda e, m=m: e.dma_start(out=ostage[:], in_=opb[m * 128:(m + 1) * 128, :]))
            C.op("vector", ["xstage", "ostage"], ["xstage"],
                 lambda e: e.tensor_tensor(xstage[:], xstage[:], ostage[:], op=ALU.add))
            C.op("vector", ["xstage", "outbc"], ["xstage"],
                 lambda e, m=m: e.tensor_scalar(xstage[:], xstage[:], outbc[:, m:m + 1],
                                                None, op0=ALU.add))
            C.dma("gpsimd", ["xstage"], ["base"],
                  lambda e, m=m: e.dma_start(out=base[m * 128:(m + 1) * 128, :], in_=xstage[:]))
        C.dma("gpsimd", ["xnb"], ["xnbo"],
              lambda e: e.dma_start(out=xnbo.rearrange("(m p) t -> p m t", p=128),
                                    in_=xnb[:].rearrange("p (m t) -> p m t", t=TPC)))
        for tc in range(16):
            for kc in range(8):
                C.op("tensor", ["gwb", "xnb"], ["pgs"],
                     lambda e, tc=tc, kc=kc: e.matmul(
                         pgs[:], xnb[:, kc * TPC + tc * 128: kc * TPC + (tc + 1) * 128],
                         gwb[:, kc * NE:(kc + 1) * NE],
                         start=(kc == 0), stop=False, skip_group_check=True),
                     accum=(kc != 0))
            C.op("tensor", ["gbb", "onesb"], ["pgs"],
                 lambda e: e.matmul(pgs[:], onesb[:], gbb[:], start=False, stop=True, skip_group_check=True),
                 accum=True)
            C.op("vector", ["pgs"], ["s"], lambda e: e.tensor_copy(s[:], pgs[:]))
            C.op("vector", ["s"], ["m1"], lambda e: e.reduce_max(m1[:], s[:], axis=AX.X))
            C.op("vector", ["s", "m1"], ["eq1"],
                 lambda e: e.tensor_scalar(eq1[:], s[:], m1[:], None, op0=ALU.is_ge))
            C.op("vector", ["eq1", "s"], ["s2"],
                 lambda e: e.scalar_tensor_tensor(s2[:], eq1[:], -1e30, s[:],
                                                  op0=ALU.mult, op1=ALU.add))
            C.op("vector", ["s2"], ["m2"], lambda e: e.reduce_max(m2[:], s2[:], axis=AX.X))
            C.op("vector", ["s2", "m2"], ["eq2"],
                 lambda e: e.tensor_scalar(eq2[:], s2[:], m2[:], None, op0=ALU.is_ge))
            C.op("vector", ["m1", "m2"], ["dd"],
                 lambda e: e.tensor_tensor(dd[:], m2[:], m1[:], op=ALU.subtract))
            C.op("scalar", ["dd"], ["ed"], lambda e: e.activation(ed[:], dd[:], AF.Exp))
            C.op("vector", ["ed"], ["p1c"], lambda e: e.tensor_scalar_add(p1c[:], ed[:], 1.0))
            C.op("vector", ["p1c"], ["p1c"], lambda e: e.reciprocal(p1c[:], p1c[:]))
            C.op("vector", ["ed", "p1c"], ["p2c"],
                 lambda e: e.tensor_tensor(p2c[:], ed[:], p1c[:], op=ALU.mult))
            C.op("vector", ["eq1", "p1c"], ["eq1"],
                 lambda e: e.tensor_scalar(eq1[:], eq1[:], p1c[:], None, op0=ALU.mult))
            C.op("vector", ["eq2", "p2c", "eq1"], ["comb"],
                 lambda e: e.scalar_tensor_tensor(comb[:], eq2[:], p2c[:], eq1[:],
                                                  op0=ALU.mult, op1=ALU.add))
            C.op("tensor", ["comb", "eye"], ["ptr"],
                 lambda e: e.transpose(ptr[0:NE, :], comb[:], eye[:]))
            C.op("vector", ["ptr"], ["ctsb"],
                 lambda e, tc=tc: e.tensor_copy(ctsb[:, tc * 128:(tc + 1) * 128], ptr[0:NE, :]))
        C.dma("gpsimd", ["ctsb"], ["cto"], lambda e: e.dma_start(out=cto, in_=ctsb[:]))
        C.sync_on("gpsimd", ["base", "cto", "xnbo"])
        C.emit(block)
    return nc


def build_p8():
    nc = bass.Bass("TRN2", target_bir_lowering=False)
    xnb_d = nc.dram_tensor("xnb_d", [DM, TPC], BF16, kind="ExternalInput").ap()
    ctd = nc.dram_tensor("ctd", [1, TPC], F32, kind="ExternalInput").ap()
    accin = nc.dram_tensor("accin", [DM, TPC], F32, kind="ExternalInput").ap()
    w1d = nc.dram_tensor("w1d", [DM, DF], F32, kind="ExternalInput").ap()
    b1d = nc.dram_tensor("b1d", [128, 32], F32, kind="ExternalInput").ap()
    w2d = nc.dram_tensor("w2d", [DF, DM], F32, kind="ExternalInput").ap()
    b2d = nc.dram_tensor("b2d", [128, 8], F32, kind="ExternalInput").ap()
    onesd = nc.dram_tensor("onesd", [1, 128], F32, kind="ExternalInput").ap()
    accout = nc.dram_tensor("accout", [DM, TPC], F32, kind="ExternalOutput").ap()
    es = contextlib.ExitStack()
    with es:
        w1b = _sb(nc, es, "w1b", (128, 8 * DF), BF16)
        w2b = _sb(nc, es, "w2b", (128, 32 * DM), BF16)
        stage = _sb(nc, es, "stage", (128, 2048), F32)
        xnb = _sb(nc, es, "xnb", (128, 8 * 512), BF16)
        hT = _sb(nc, es, "hT", (128, 32 * 512), BF16)
        b1c = _sb(nc, es, "b1c", (128, 32), F32)
        b2c = _sb(nc, es, "b2c", (128, 8), F32)
        ctrow = _sb(nc, es, "ctrow", (1, TPC), F32)
        ones1 = _sb(nc, es, "ones1", (1, 128), F32)
        cbc = _sb(nc, es, "cbc", (128, 512), F32)
        acct = _sb(nc, es, "acct", (128, 512), F32)
        scry = _sb(nc, es, "scry", (128, 512), F32)
        ph = _ps(nc, es, "ph", (128, 512))
        py0 = _ps(nc, es, "py0", (128, 512))
        py1 = _ps(nc, es, "py1", (128, 512))
        pc = _ps(nc, es, "pc", (128, 512))
        C = Ctx(nc, es)
        block = es.enter_context(nc.Block())

        C.dma("sync", [], ["b1c"], lambda e: e.dma_start(out=b1c[:], in_=b1d))
        C.dma("sync", [], ["b2c"], lambda e: e.dma_start(out=b2c[:], in_=b2d))
        C.dma("sync", [], ["ctrow"], lambda e: e.dma_start(out=ctrow[:], in_=ctd))
        C.dma("sync", [], ["ones1"], lambda e: e.dma_start(out=ones1[:], in_=onesd))
        for ch in range(16):
            C.dma("sync", [], ["stage"],
                  lambda e, ch=ch: e.dma_start(
                      out=stage[:].rearrange("p (kc m) -> p kc m", m=256),
                      in_=w1d[:, ch * 256:(ch + 1) * 256].rearrange("(kc p) m -> p kc m", p=128)))
            C.op("gpsimd", ["stage"], ["w1b"],
                 lambda e, ch=ch: e.tensor_copy(
                     w1b[:].rearrange("p (kc f) -> p kc f", f=DF)[:, :, ch * 256:(ch + 1) * 256],
                     stage[:].rearrange("p (kc m) -> p kc m", m=256)))
        for ch in range(16):
            C.dma("sync", [], ["stage"],
                  lambda e, ch=ch: e.dma_start(
                      out=stage[:].rearrange("p (fc m) -> p fc m", m=DM),
                      in_=w2d[ch * 256:(ch + 1) * 256, :].rearrange("(fc p) m -> p fc m", p=128)))
            C.op("gpsimd", ["stage"], ["w2b"],
                 lambda e, ch=ch: e.tensor_copy(
                     w2b[:, ch * 2 * DM:(ch + 1) * 2 * DM], stage[:]))
        for tt in range(4):
            C.dma("sync", [], ["xnb"],
                  lambda e, tt=tt: e.dma_start(
                      out=xnb[:].rearrange("p (kc t) -> p kc t", t=512),
                      in_=xnb_d[:, tt * 512:(tt + 1) * 512].rearrange("(kc p) t -> p kc t",
                                                                     p=128)))
            C.op("tensor", ["ctrow", "ones1"], ["pc"],
                 lambda e, tt=tt: e.matmul(pc[:], ones1[:],
                                           ctrow[:, tt * 512:(tt + 1) * 512],
                                           start=True, stop=True, skip_group_check=True))
            C.op("vector", ["pc"], ["cbc"], lambda e: e.tensor_copy(cbc[:], pc[:]))
            for fc in range(32):
                for kc in range(8):
                    C.op("tensor", ["w1b", "xnb"], ["ph"],
                         lambda e, fc=fc, kc=kc: e.matmul(
                             ph[:], w1b[:, kc * DF + fc * 128: kc * DF + (fc + 1) * 128],
                             xnb[:, kc * 512:(kc + 1) * 512],
                             start=(kc == 0), stop=(kc == 7), skip_group_check=True),
                         accum=(kc != 0))
                C.op("scalar", ["ph", "b1c"], ["hT"],
                     lambda e, fc=fc: e.activation(hT[:, fc * 512:(fc + 1) * 512], ph[:],
                                                   AF.Gelu, bias=b1c[:, fc:fc + 1]))
            for m in range(8):
                pb = py0 if m % 2 == 0 else py1
                pk = f"py{m % 2}"
                for fc in range(32):
                    C.op("tensor", ["w2b", "hT"], [pk],
                         lambda e, m=m, fc=fc, pb=pb: e.matmul(
                             pb[:], w2b[:, fc * DM + m * 128: fc * DM + (m + 1) * 128],
                             hT[:, fc * 512:(fc + 1) * 512],
                             start=(fc == 0), stop=(fc == 31), skip_group_check=True),
                         accum=(fc != 0))
                C.dma("sync", [], ["acct"],
                      lambda e, m=m, tt=tt: e.dma_start(
                          out=acct[:], in_=accin[m * 128:(m + 1) * 128, tt * 512:(tt + 1) * 512]))
                C.op("vector", [pk, "b2c"], ["scry"],
                     lambda e, m=m, pb=pb: e.tensor_scalar(
                         scry[:], pb[:], b2c[:, m:m + 1], None, op0=ALU.add))
                C.op("vector", ["scry", "cbc"], ["scry"],
                     lambda e: e.tensor_tensor(scry[:], scry[:], cbc[:], op=ALU.mult))
                C.op("vector", ["scry", "acct"], ["scry"],
                     lambda e: e.tensor_tensor(scry[:], scry[:], acct[:], op=ALU.add))
                C.dma("gpsimd", ["scry"], ["accout"],
                      lambda e, m=m, tt=tt: e.dma_start(
                          out=accout[m * 128:(m + 1) * 128, tt * 512:(tt + 1) * 512],
                          in_=scry[:]))
        C.sync_on("gpsimd", ["accout"])
        C.emit(block)
    return nc


def build_p9():
    nc = bass.Bass("TRN2", target_bir_lowering=False)
    accd = nc.dram_tensor("accd", [DM, TPC], F32, kind="ExternalInput").ap()
    eyed = nc.dram_tensor("eyed", [128, 128], F32, kind="ExternalInput").ap()
    outd = nc.dram_tensor("outd", [TPC, DM], F32, kind="ExternalOutput").ap()
    es = contextlib.ExitStack()
    with es:
        accs = _sb(nc, es, "accs", (128, 8 * TPC), F32)
        eye = _sb(nc, es, "eye", (128, 128), F32)
        otile = _sb(nc, es, "otile", (128, DM), F32)
        ptr = _ps(nc, es, "ptr", (128, 128))
        C = Ctx(nc, es)
        block = es.enter_context(nc.Block())
        C.dma("sync", [], ["eye"], lambda e: e.dma_start(out=eye[:], in_=eyed))
        C.dma("sync", [], ["accs"],
              lambda e: e.dma_start(out=accs[:].rearrange("p (m t) -> p m t", t=TPC),
                                    in_=accd.rearrange("(m p) t -> p m t", p=128)))
        for tc in range(16):
            for m in range(8):
                C.op("tensor", ["accs", "eye"], ["ptr"],
                     lambda e, tc=tc, m=m: e.transpose(
                         ptr[:], accs[:, m * TPC + tc * 128: m * TPC + (tc + 1) * 128], eye[:]))
                C.op("vector", ["ptr"], ["otile"],
                     lambda e, m=m: e.tensor_copy(otile[:, m * 128:(m + 1) * 128], ptr[:]))
            C.dma("gpsimd", ["otile"], ["outd"],
                  lambda e, tc=tc: e.dma_start(out=outd[tc * 128:(tc + 1) * 128, :], in_=otile[:]))
        C.sync_on("gpsimd", ["outd"])
        C.emit(block)
    return nc


def _run(nc, in_maps):
    return run_bass_kernel_spmd(nc, in_maps, core_ids=list(range(NCORES))).results


def kernel(**inputs):
    f32 = np.float32
    cont = np.ascontiguousarray
    eye = np.eye(128, dtype=f32)
    eps = np.full((128, 1), 1e-5, f32)
    one = np.full((128, 1), 1.0, f32)
    ones1 = np.ones((1, 128), f32)

    x_full = np.asarray(inputs["L0_attn_x"], f32).reshape(NB * T, DM)

    inv = 10000.0 ** (-(np.arange(32, dtype=np.float64) / 32.0))
    ang = np.arange(T, dtype=np.float64)[:, None] * inv[None, :]
    cosT = np.cos(ang).astype(f32)
    sinT = np.sin(ang).astype(f32)

    r = np.arange(DM)
    qcol = ((r % 512) // 32) * 64 + (r // 512) * 32 + (r % 32)
    wq_r = cont(np.asarray(inputs["L0_attn_Wq"], f32)[:, qcol])
    wk_r = cont(np.asarray(inputs["L0_attn_Wk"], f32)[:, qcol])
    wv = cont(np.asarray(inputs["L0_attn_Wv"], f32))
    wo = cont(np.asarray(inputs["L0_attn_Wout"], f32))

    nc1 = build_p1()
    maps = []
    for c in range(NCORES):
        b, half = c // 2, c % 2
        t0 = half * TPC
        cosd = cont(np.broadcast_to(cosT[t0:t0 + TPC, :].T, (32, TPC))[np.arange(128) % 32])
        sind = cont(np.broadcast_to(sinT[t0:t0 + TPC, :].T, (32, TPC))[np.arange(128) % 32])
        maps.append({"x": cont(x_full[b * T + t0: b * T + t0 + TPC]),
                     "wq": wq_r, "wk": wk_r, "wv": wv,
                     "cosd": cosd, "sind": sind, "eyed": eye, "epsd": eps})
    r1 = _run(nc1, maps)

    W1 = np.asarray(inputs["L0_ffn_W1"], f32)
    W2 = np.asarray(inputs["L0_ffn_W2"], f32)
    w1f = cont(W1.transpose(2, 3, 0, 1).reshape(DM, 32, 128).reshape(DM, 4096))
    w2f = cont(W2.transpose(3, 0, 1, 2).reshape(32, 128, DM).reshape(4096, DM))
    wcf = cont(np.asarray(inputs["L0_ffn_Wc"], f32).reshape(DM, 128))
    bcb = cont(np.asarray(inputs["L0_ffn_bc"], f32).reshape(128, 1))
    b1f = cont(np.asarray(inputs["L0_ffn_b1"], f32).reshape(128, 32))
    b2f = cont(np.asarray(inputs["L0_ffn_b2"], f32).reshape(128, DM))
    tie_row = np.linspace(0.0, 1e-6, 32, dtype=f32)[np.arange(512) % 32]
    tied = cont(np.broadcast_to(tie_row[None, :], (128, 512)))
    nc2 = build_p2()
    maps = [{"xnt": r1[c]["xnt"], "w1f": w1f, "w2f": w2f, "wcf": wcf, "bcb": bcb,
             "tied": tied, "b1fd": b1f, "b2fd": b2f} for c in range(NCORES)]
    r2 = _run(nc2, maps)

    QT = [np.concatenate([r1[2 * b]["qt"], r1[2 * b + 1]["qt"]], axis=1) for b in range(NB)]
    KT = [np.concatenate([r1[2 * b]["kt"], r1[2 * b + 1]["kt"]], axis=1) for b in range(NB)]
    VF = [np.concatenate([r1[2 * b]["vout"], r1[2 * b + 1]["vout"]], axis=0) for b in range(NB)]
    nc3 = build_p3()
    maps = []
    for c in range(NCORES):
        b = c // 2
        heads = range(0, 8) if c % 2 == 0 else range(8, 16)
        rows = np.concatenate([np.r_[h * 32:(h + 1) * 32, 512 + h * 32: 512 + (h + 1) * 32]
                               for h in heads])
        cols = np.concatenate([np.arange(h * 64, (h + 1) * 64) for h in heads])
        maps.append({"qin": cont(QT[b][rows]), "kin": cont(KT[b][rows]),
                     "vin": cont(VF[b][:, cols]), "eyed": eye})
    r3 = _run(nc3, maps)

    AO = [np.concatenate([r3[2 * b]["aout"], r3[2 * b + 1]["aout"]], axis=0) for b in range(NB)]
    nc4 = build_p4()
    maps = []
    for c in range(NCORES):
        b, half = c // 2, c % 2
        t0 = half * TPC
        maps.append({"x": cont(x_full[b * T + t0: b * T + t0 + TPC]),
                     "f0": r2[c]["f0"], "ao": cont(AO[b][:, t0:t0 + TPC]), "wo": wo})
    r4 = _run(nc4, maps)

    X1 = [np.concatenate([r4[2 * b]["x1"], r4[2 * b + 1]["x1"]], axis=0) for b in range(NB)]
    inw = cont(np.asarray(inputs["L1_attn_in_w"], f32))
    inb = cont(np.asarray(inputs["L1_attn_in_b"], f32).reshape(32, 128).T)
    cwd = cont(np.asarray(inputs["L1_attn_conv_w"], f32)[:, 0, :]
               .reshape(4, 16, 128).transpose(2, 1, 0).reshape(128, 64))
    cbd = cont(np.asarray(inputs["L1_attn_conv_b"], f32).reshape(16, 128).T)
    bwd = cont(np.asarray(inputs["L1_attn_B_w"], f32))
    bbd = cont(np.asarray(inputs["L1_attn_B_b"], f32).reshape(DS, 1))
    cwd2 = cont(np.asarray(inputs["L1_attn_C_w"], f32))
    cbd2 = cont(np.asarray(inputs["L1_attn_C_b"], f32).reshape(DS, 1))
    dtw = cont(np.asarray(inputs["L1_attn_dt_w"], f32))
    dtb = cont(np.asarray(inputs["L1_attn_dt_b"], f32).reshape(16, 128).T)
    nc5 = build_p5()
    maps = []
    for c in range(NCORES):
        b, half = c // 2, c % 2
        t0 = half * TPC
        pad = np.zeros((2176, DM), f32)
        lo = t0 - 1
        hi = min(T, t0 + TPC + 2)
        src0 = max(0, lo)
        pad[src0 - lo: hi - lo] = X1[b][src0:hi]
        maps.append({"x1p": pad, "inw": inw, "inb": inb, "cwd": cwd, "cbd": cbd,
                     "bwd": bwd, "bbd": bbd, "cwd2": cwd2, "cbd2": cbd2,
                     "dtw": dtw, "dtb": dtb, "eyed": eye, "epsd": eps, "oned": one})
    r5 = _run(nc5, maps)

    XC = [np.concatenate([r5[2 * b]["xco"], r5[2 * b + 1]["xco"]], axis=1) for b in range(NB)]
    DT_ = [np.concatenate([r5[2 * b]["dto"], r5[2 * b + 1]["dto"]], axis=1) for b in range(NB)]
    SG = [np.concatenate([r5[2 * b]["sgo"], r5[2 * b + 1]["sgo"]], axis=1) for b in range(NB)]
    BM = [np.concatenate([r5[2 * b]["bmo"], r5[2 * b + 1]["bmo"]], axis=1) for b in range(NB)]
    CM = [np.concatenate([r5[2 * b]["cmo"], r5[2 * b + 1]["cmo"]], axis=1) for b in range(NB)]
    A = np.asarray(inputs["L1_attn_A"], f32)
    Dp = np.asarray(inputs["L1_attn_Dp"], f32)
    ow = cont(np.asarray(inputs["L1_attn_out_w"], f32))
    nc6 = build_p6()
    maps = []
    for c in range(NCORES):
        b, chh = c // 2, c % 2
        c0 = chh * 1024
        acol = cont(A[c0:c0 + 1024].reshape(8, 128, DS).transpose(1, 0, 2).reshape(128, 128))
        dpc = cont(Dp[c0:c0 + 1024].reshape(8, 128).T)
        maps.append({"dtd": cont(DT_[b][c0:c0 + 1024]), "xcd": cont(XC[b][c0:c0 + 1024]),
                     "sgd": cont(SG[b][c0:c0 + 1024]), "bmd": cont(BM[b]), "cmd": cont(CM[b]),
                     "acold": acol, "dpd": dpc, "owd": cont(ow[c0:c0 + 1024]),
                     "onesd": ones1})
    r6 = _run(nc6, maps)

    gw = cont(np.asarray(inputs["L1_ffn_gate_w"], f32))
    gb = cont(np.asarray(inputs["L1_ffn_gate_b"], f32).reshape(1, NE))
    outb = cont(np.asarray(inputs["L1_attn_out_b"], f32).reshape(8, 128).T)
    nc7 = build_p7()
    maps = []
    for c in range(NCORES):
        b, half = c // 2, c % 2
        t0 = half * TPC
        maps.append({"x1": r4[c]["x1"], "xn2": r5[c]["xn2o"],
                     "opa": cont(r6[2 * b]["opo"][:, t0:t0 + TPC]),
                     "opb": cont(r6[2 * b + 1]["opo"][:, t0:t0 + TPC]),
                     "outbd": outb, "gwd": gw, "gbd": gb, "onesd": ones1, "eyed": eye})
    r7 = _run(nc7, maps)

    w1e = np.asarray(inputs["L1_ffn_w1"], f32)
    b1e = np.asarray(inputs["L1_ffn_b1"], f32)
    w2e = np.asarray(inputs["L1_ffn_w2"], f32)
    b2e = np.asarray(inputs["L1_ffn_b2"], f32)
    nc8 = build_p8()
    acc = [r7[c]["base"] for c in range(NCORES)]
    for e_ in range(NE):
        w1d = cont(w1e[e_])
        b1d = cont(b1e[e_].reshape(32, 128).T)
        w2d = cont(w2e[e_])
        b2d = cont(b2e[e_].reshape(8, 128).T)
        maps = [{"xnb_d": r7[c]["xnbo"], "ctd": cont(r7[c]["cto"][e_:e_ + 1]),
                 "accin": acc[c], "w1d": w1d, "b1d": b1d, "w2d": w2d, "b2d": b2d,
                 "onesd": ones1} for c in range(NCORES)]
        r8 = _run(nc8, maps)
        acc = [r8[c]["accout"] for c in range(NCORES)]

    nc9 = build_p9()
    maps = [{"accd": acc[c], "eyed": eye} for c in range(NCORES)]
    r9 = _run(nc9, maps)

    out = np.empty((NB, T, DM), f32)
    for c in range(NCORES):
        b, half = c // 2, c % 2
        t0 = half * TPC
        out[b, t0:t0 + TPC] = r9[c]["outd"]
    return out


# version 02
# speedup vs baseline: 1.0262x; 1.0262x over previous
import contextlib
import numpy as np
import ml_dtypes
import concourse.bass as bass
import concourse.mybir as mybir
from concourse.bass_utils import run_bass_kernel_spmd

F32 = mybir.dt.float32
BF16 = mybir.dt.bfloat16
AF = mybir.ActivationFunctionType
ALU = mybir.AluOpType
AX = mybir.AxisListType
NPBF = ml_dtypes.bfloat16

NB, T, DM = 4, 4096, 1024
H, DH, G = 16, 64, 409
TPC = 2048
DI, DS = 2048, 16
NE, DF = 24, 4096
NCORES = 8


class Ctx:
    QUEUES = ("sync", "vector", "scalar", "tensor", "gpsimd")

    def __init__(self, nc, es):
        self.nc = nc
        self.sem = {q: es.enter_context(nc.semaphore(f"prog_{q}")) for q in self.QUEUES}
        self.NDMA = 10
        self.dsem, self.duses = {}, {}
        for q in ("sync", "scalar", "gpsimd", "vector"):
            self.dsem[q] = [es.enter_context(nc.semaphore(f"dma_{q}{i}")) for i in range(self.NDMA)]
            self.duses[q] = [0] * self.NDMA
        self.drr = {q: 0 for q in self.dsem}
        self.cnt = {q: 0 for q in self.QUEUES}
        self.last_write = {}
        self.readers = {}
        self.watermark = {q: {} for q in self.QUEUES}
        self.queue = {q: [] for q in self.QUEUES}

    def _wait(self, eng, kind, q, c):
        wm = self.watermark[eng]
        k = (kind, q)
        if wm.get(k, 0) >= c:
            return
        sem = self.sem[q] if kind == "prog" else self.dsem[q[0]][q[1]]
        self.queue[eng].append(("wait", sem, c))
        wm[k] = c

    def _raw(self, eng, keys):
        for key in keys:
            lw = self.last_write.get(key)
            if lw is not None:
                self._wait(eng, lw[0], lw[1], lw[2])

    def _war_waw(self, eng, keys):
        for key in keys:
            lw = self.last_write.get(key)
            if lw is not None:
                self._wait(eng, lw[0], lw[1], lw[2])
            for (kind, q), c in self.readers.get(key, {}).items():
                self._wait(eng, kind, q, c)

    def _record(self, kind, q, c, reads, writes):
        for key in writes:
            self.last_write[key] = (kind, q, c)
            self.readers[key] = {}
        for key in reads:
            d = self.readers.setdefault(key, {})
            k = (kind, q)
            d[k] = max(d.get(k, 0), c)

    def op(self, eng, reads, writes, fn, accum=False):
        self._raw(eng, reads)
        if accum:
            for key in writes:
                lw = self.last_write.get(key)
                if lw is not None and not (lw[0] == "prog" and lw[1] == eng):
                    self._wait(eng, lw[0], lw[1], lw[2])
                for (kind, q), c in self.readers.get(key, {}).items():
                    if not (kind == "prog" and q == eng):
                        self._wait(eng, kind, q, c)
        else:
            self._war_waw(eng, writes)
        self.cnt[eng] += 1
        self.queue[eng].append(("op", fn, self.sem[eng], 1))
        self._record("prog", eng, self.cnt[eng], reads, writes)

    def dma(self, q, reads, writes, fn):
        self._raw(q, reads)
        self._war_waw(q, writes)
        slot = self.drr[q]
        self.drr[q] = (slot + 1) % self.NDMA
        prev = self.duses[q][slot]
        if prev > 0:
            self._wait(q, "dma", (q, slot), 16 * prev)
        self.duses[q][slot] = prev + 1
        self.queue[q].append(("op", fn, self.dsem[q][slot], 16))
        self._record("dma", (q, slot), 16 * (prev + 1), reads, writes)

    def sync_on(self, eng, keys):
        self._raw(eng, keys)

    def emit(self, block):
        def make_body(qname):
            def body(engine):
                for item in self.queue[qname]:
                    if item[0] == "wait":
                        engine.wait_ge(item[1], item[2])
                    else:
                        _, fn, sem, inc = item
                        fn(engine).then_inc(sem, inc)
            return body
        for qname in self.QUEUES:
            if self.queue[qname]:
                getattr(block, qname)(make_body(qname))


def _sb(nc, es, name, shape, dt):
    return es.enter_context(nc.sbuf_tensor(name, list(shape), dt))


def _ps(nc, es, name, shape, dt=F32):
    return es.enter_context(nc.psum_tensor(name, list(shape), dt))


def _ln_tile(C, xin_ap, xin_key, xn, mean, var, scr, epscol):
    C.op("vector", [xin_key], ["mean"], lambda e: e.reduce_sum(mean[:], xin_ap, axis=AX.X))
    C.op("vector", ["mean"], ["mean"], lambda e: e.tensor_scalar_mul(mean[:], mean[:], 1.0 / DM))
    C.op("vector", [xin_key, "mean"], ["scr"],
         lambda e: e.tensor_scalar(scr[:], xin_ap, mean[:], None, op0=ALU.subtract))
    C.op("scalar", ["scr"], ["xn"], lambda e: e.activation(xn[:], scr[:], AF.Square))
    C.op("vector", ["xn"], ["var"], lambda e: e.reduce_sum(var[:], xn[:], axis=AX.X))
    C.op("vector", ["var"], ["var"], lambda e: e.tensor_scalar_mul(var[:], var[:], 1.0 / DM))
    C.op("scalar", ["var", "epscol"], ["var"],
         lambda e: e.activation(var[:], var[:], AF.Sqrt, bias=epscol[:]))
    C.op("vector", ["var"], ["var"], lambda e: e.reciprocal(var[:], var[:]))
    C.op("vector", ["scr", "var"], ["xn"],
         lambda e: e.tensor_scalar(xn[:], scr[:], var[:], None, op0=ALU.mult))


def build_p1():
    nc = bass.Bass("TRN2", target_bir_lowering=False)
    x = nc.dram_tensor("x", [TPC, DM], F32, kind="ExternalInput").ap()
    wq = nc.dram_tensor("wq", [DM, DM], F32, kind="ExternalInput").ap()
    wk = nc.dram_tensor("wk", [DM, DM], F32, kind="ExternalInput").ap()
    wv = nc.dram_tensor("wv", [DM, DM], F32, kind="ExternalInput").ap()
    cosd = nc.dram_tensor("cosd", [128, TPC], F32, kind="ExternalInput").ap()
    sind = nc.dram_tensor("sind", [128, TPC], F32, kind="ExternalInput").ap()
    eyed = nc.dram_tensor("eyed", [128, 128], F32, kind="ExternalInput").ap()
    epsd = nc.dram_tensor("epsd", [128, 1], F32, kind="ExternalInput").ap()
    qt = nc.dram_tensor("qt", [DM, TPC], F32, kind="ExternalOutput").ap()
    kt = nc.dram_tensor("kt", [DM, TPC], F32, kind="ExternalOutput").ap()
    vout = nc.dram_tensor("vout", [TPC, DM], BF16, kind="ExternalOutput").ap()
    xnt = nc.dram_tensor("xnt", [DM, TPC], BF16, kind="ExternalOutput").ap()
    es = contextlib.ExitStack()
    with es:
        xtile = _sb(nc, es, "xtile", (128, DM), F32)
        xn = _sb(nc, es, "xn", (128, DM), F32)
        scr = _sb(nc, es, "scr", (128, DM), F32)
        mean = _sb(nc, es, "mean", (128, 1), F32)
        var = _sb(nc, es, "var", (128, 1), F32)
        epscol = _sb(nc, es, "epscol", (128, 1), F32)
        eye = _sb(nc, es, "eye", (128, 128), F32)
        cos = _sb(nc, es, "cos", (128, TPC), F32)
        sin = _sb(nc, es, "sin", (128, TPC), F32)
        xnt_bf = _sb(nc, es, "xnt_bf", (128, 8 * TPC), BF16)
        wstage = _sb(nc, es, "wstage", (128, 4 * DM), F32)
        wbf = _sb(nc, es, "wbf", (128, 8 * DM), BF16)
        qraw = _sb(nc, es, "qraw", (128, 8 * TPC), F32)
        t1 = _sb(nc, es, "t1", (128, TPC), F32)
        t2 = _sb(nc, es, "t2", (128, TPC), F32)
        vsb = _sb(nc, es, "vsb", (128, DM), BF16)
        ptr = _ps(nc, es, "ptr", (128, 128))
        pq0 = _ps(nc, es, "pq0", (128, 512))
        pq1 = _ps(nc, es, "pq1", (128, 512))
        C = Ctx(nc, es)
        block = es.enter_context(nc.Block())

        C.dma("sync", [], ["epscol"], lambda e: e.dma_start(out=epscol[:], in_=epsd))
        C.dma("sync", [], ["eye"], lambda e: e.dma_start(out=eye[:], in_=eyed))
        C.dma("sync", [], ["cos"], lambda e: e.dma_start(out=cos[:], in_=cosd))
        C.dma("sync", [], ["sin"], lambda e: e.dma_start(out=sin[:], in_=sind))

        for i in range(16):
            C.dma("sync", [], ["xtile"],
                  lambda e, i=i: e.dma_start(out=xtile[:], in_=x[i * 128:(i + 1) * 128, :]))
            _ln_tile(C, xtile[:], "xtile", xn, mean, var, scr, epscol)
            for kc in range(8):
                C.op("tensor", ["xn", "eye"], ["ptr"],
                     lambda e, kc=kc: e.transpose(ptr[:], xn[:, kc * 128:(kc + 1) * 128], eye[:]))
                C.op("vector", ["ptr"], ["xnt_bf"],
                     lambda e, kc=kc, i=i: e.tensor_copy(
                         xnt_bf[:, kc * TPC + i * 128: kc * TPC + (i + 1) * 128], ptr[:]))
        C.dma("gpsimd", ["xnt_bf"], ["xnt"],
              lambda e: e.dma_start(out=xnt.rearrange("(kc p) t -> p kc t", p=128),
                                    in_=xnt_bf[:].rearrange("p (kc t) -> p kc t", t=TPC)))

        def load_weight_bf(wsrc):
            for hf in range(2):
                C.dma("sync", [], ["wstage"],
                      lambda e, wsrc=wsrc, hf=hf: e.dma_start(
                          out=wstage[:].rearrange("p (kc m) -> p kc m", m=DM),
                          in_=wsrc.rearrange("(kc p) m -> p kc m", p=128)[:, hf * 4:(hf + 1) * 4, :]))
                C.op("gpsimd", ["wstage"], ["wbf"],
                     lambda e, hf=hf: e.tensor_copy(
                         wbf[:, hf * 4 * DM:(hf + 1) * 4 * DM], wstage[:]))

        for wsrc, outap, okey in ((wq, qt, "qt"), (wk, kt, "kt")):
            load_weight_bf(wsrc)
            for m in range(8):
                for tt in range(4):
                    pb = pq0 if tt % 2 == 0 else pq1
                    pk = f"pq{tt % 2}"
                    for kc in range(8):
                        C.op("tensor", ["wbf", "xnt_bf"], [pk],
                             lambda e, m=m, tt=tt, kc=kc, pb=pb: e.matmul(
                                 pb[:], wbf[:, kc * DM + m * 128: kc * DM + (m + 1) * 128],
                                 xnt_bf[:, kc * TPC + tt * 512: kc * TPC + (tt + 1) * 512],
                                 start=(kc == 0), stop=(kc == 7), skip_group_check=True),
                             accum=(kc != 0))
                    C.op("vector", [pk], ["qraw"],
                         lambda e, m=m, tt=tt, pb=pb: e.tensor_copy(
                             qraw[:, m * TPC + tt * 512: m * TPC + (tt + 1) * 512], pb[:]))
            for m in range(4):
                a = qraw[:, m * TPC:(m + 1) * TPC]
                b = qraw[:, (m + 4) * TPC:(m + 5) * TPC]
                C.op("vector", ["qraw", "cos"], ["t1"],
                     lambda e, a=a: e.tensor_tensor(t1[:], a, cos[:], op=ALU.mult))
                C.op("vector", ["qraw", "sin"], ["t2"],
                     lambda e, a=a: e.tensor_tensor(t2[:], a, sin[:], op=ALU.mult))
                C.op("vector", ["qraw", "sin"], ["qraw"],
                     lambda e, a=a, b=b: e.tensor_tensor(a, b, sin[:], op=ALU.mult))
                C.op("vector", ["qraw", "cos"], ["qraw"],
                     lambda e, b=b: e.tensor_tensor(b, b, cos[:], op=ALU.mult))
                C.op("vector", ["qraw", "t2"], ["qraw"],
                     lambda e, b=b: e.tensor_tensor(b, t2[:], b, op=ALU.add))
                C.op("vector", ["qraw", "t1"], ["qraw"],
                     lambda e, a=a: e.tensor_tensor(a, t1[:], a, op=ALU.subtract))
            C.dma("gpsimd", ["qraw"], [okey],
                  lambda e, outap=outap: e.dma_start(
                      out=outap.rearrange("(m p) t -> p m t", p=128),
                      in_=qraw[:].rearrange("p (m t) -> p m t", t=TPC)))

        load_weight_bf(wv)
        for i in range(16):
            for nn in range(2):
                pb = pq0 if nn == 0 else pq1
                pk = f"pq{nn}"
                for kc in range(8):
                    C.op("tensor", ["wbf", "xnt_bf"], [pk],
                         lambda e, i=i, nn=nn, kc=kc, pb=pb: e.matmul(
                             pb[:], xnt_bf[:, kc * TPC + i * 128: kc * TPC + (i + 1) * 128],
                             wbf[:, kc * DM + nn * 512: kc * DM + (nn + 1) * 512],
                             start=(kc == 0), stop=(kc == 7), skip_group_check=True),
                         accum=(kc != 0))
                C.op("vector", [pk], ["vsb"],
                     lambda e, nn=nn, pb=pb: e.tensor_copy(vsb[:, nn * 512:(nn + 1) * 512], pb[:]))
            C.dma("gpsimd", ["vsb"], ["vout"],
                  lambda e, i=i: e.dma_start(out=vout[i * 128:(i + 1) * 128, :], in_=vsb[:]))
        C.sync_on("gpsimd", ["xnt", "qt", "kt", "vout"])
        C.emit(block)
    return nc


def build_p2():
    nc = bass.Bass("TRN2", target_bir_lowering=False)
    xnt = nc.dram_tensor("xnt", [DM, TPC], BF16, kind="ExternalInput").ap()
    w1f = nc.dram_tensor("w1f", [DM, 4096], F32, kind="ExternalInput").ap()
    w2f = nc.dram_tensor("w2f", [4096, DM], F32, kind="ExternalInput").ap()
    wcf = nc.dram_tensor("wcf", [DM, 128], F32, kind="ExternalInput").ap()
    bcb = nc.dram_tensor("bcb", [128, 1], F32, kind="ExternalInput").ap()
    tied = nc.dram_tensor("tied", [128, 512], F32, kind="ExternalInput").ap()
    b1fd = nc.dram_tensor("b1fd", [128, 32], F32, kind="ExternalInput").ap()
    b2fd = nc.dram_tensor("b2fd", [128, DM], F32, kind="ExternalInput").ap()
    f0 = nc.dram_tensor("f0", [TPC, DM], F32, kind="ExternalOutput").ap()
    es = contextlib.ExitStack()
    with es:
        xnb = _sb(nc, es, "xnb", (128, 8 * TPC), BF16)
        w1b = _sb(nc, es, "w1b", (128, 8 * 4096), BF16)
        w2b = _sb(nc, es, "w2b", (128, 32 * DM), BF16)
        wcb = _sb(nc, es, "wcb", (128, 8 * 128), BF16)
        stage = _sb(nc, es, "stage", (128, 1024), F32)
        bcc = _sb(nc, es, "bcc", (128, 1), F32)
        tie = _sb(nc, es, "tie", (128, 512), F32)
        b1f = _sb(nc, es, "b1f", (128, 32), F32)
        b2b = _sb(nc, es, "b2b", (128, DM), BF16)
        logits = _sb(nc, es, "logits", (128, TPC), F32)
        gmax = _sb(nc, es, "gmax", (128, 64), F32)
        perm = _sb(nc, es, "perm", (128, TPC), F32)
        permb = _sb(nc, es, "permb", (128, TPC), BF16)
        mif = _sb(nc, es, "mif", (128, 128), F32)
        mib0 = _sb(nc, es, "mib0", (128, 128), BF16)
        mib1 = _sb(nc, es, "mib1", (128, 128), BF16)
        f0sb = _sb(nc, es, "f0sb", (128, DM), F32)
        pin0 = _ps(nc, es, "pin0", (128, 128))
        pin1 = _ps(nc, es, "pin1", (128, 128))
        po0 = _ps(nc, es, "po0", (128, 512))
        po1 = _ps(nc, es, "po1", (128, 512))
        C = Ctx(nc, es)
        block = es.enter_context(nc.Block())

        C.dma("sync", [], ["xnb"],
              lambda e: e.dma_start(out=xnb[:].rearrange("p (kc t) -> p kc t", t=TPC),
                                    in_=xnt.rearrange("(kc p) t -> p kc t", p=128)))
        C.dma("sync", [], ["bcc"], lambda e: e.dma_start(out=bcc[:], in_=bcb))
        C.dma("sync", [], ["tie"], lambda e: e.dma_start(out=tie[:], in_=tied))
        C.dma("sync", [], ["b1f"], lambda e: e.dma_start(out=b1f[:], in_=b1fd))
        C.dma("sync", [], ["stage"],
              lambda e: e.dma_start(out=stage[:, 0:DM], in_=b2fd))
        C.op("gpsimd", ["stage"], ["b2b"], lambda e: e.tensor_copy(b2b[:], stage[:, 0:DM]))
        C.dma("sync", [], ["stage"],
              lambda e: e.dma_start(out=stage[:].rearrange("p (kc m) -> p kc m", m=128),
                                    in_=wcf.rearrange("(kc p) m -> p kc m", p=128)))
        C.op("gpsimd", ["stage"], ["wcb"], lambda e: e.tensor_copy(wcb[:], stage[:]))
        for ch in range(32):
            C.dma("sync", [], ["stage"],
                  lambda e, ch=ch: e.dma_start(
                      out=stage[:].rearrange("p (kc m) -> p kc m", m=128),
                      in_=w1f[:, ch * 128:(ch + 1) * 128].rearrange("(kc p) m -> p kc m", p=128)))
            C.op("gpsimd", ["stage"], ["w1b"],
                 lambda e, ch=ch: e.tensor_copy(
                     w1b[:].rearrange("p (kc f) -> p kc f", f=4096)[:, :, ch * 128:(ch + 1) * 128],
                     stage[:].rearrange("p (kc m) -> p kc m", m=128)))
        for ch in range(32):
            C.dma("sync", [], ["stage"],
                  lambda e, ch=ch: e.dma_start(
                      out=stage[:],
                      in_=w2f[ch * 128:(ch + 1) * 128, :].rearrange("(o p) m -> p (o m)", p=128)))
            C.op("gpsimd", ["stage"], ["w2b"],
                 lambda e, ch=ch: e.tensor_copy(
                     w2b[:, ch * DM:(ch + 1) * DM], stage[:]))
        for tt in range(4):
            pk = "po0"
            for kc in range(8):
                C.op("tensor", ["wcb", "xnb"], [pk],
                     lambda e, tt=tt, kc=kc: e.matmul(
                         po0[:], wcb[:, kc * 128:(kc + 1) * 128],
                         xnb[:, kc * TPC + tt * 512: kc * TPC + (tt + 1) * 512],
                         start=(kc == 0), stop=(kc == 7), skip_group_check=True),
                     accum=(kc != 0))
            C.op("vector", [pk, "bcc"], ["logits"],
                 lambda e, tt=tt: e.tensor_scalar(
                     logits[:, tt * 512:(tt + 1) * 512], po0[:], bcc[:], None, op0=ALU.add))
            C.op("vector", ["logits", "tie"], ["logits"],
                 lambda e, tt=tt: e.tensor_tensor(
                     logits[:, tt * 512:(tt + 1) * 512],
                     logits[:, tt * 512:(tt + 1) * 512], tie[:], op=ALU.add))
        C.op("vector", ["logits"], ["gmax"],
             lambda e: e.tensor_reduce(gmax[:], logits[:].rearrange("p (g w) -> p g w", w=32),
                                       axis=AX.X, op=ALU.max))
        C.op("vector", ["logits", "gmax"], ["perm"],
             lambda e: e.tensor_tensor(perm[:].rearrange("p (g w) -> p g w", w=32),
                                       logits[:].rearrange("p (g w) -> p g w", w=32),
                                       gmax[:].unsqueeze(2).broadcast_to([128, 64, 32]),
                                       op=ALU.is_ge))
        C.op("gpsimd", ["perm"], ["permb"], lambda e: e.tensor_copy(permb[:], perm[:]))
        for tc in range(16):
            C.op("tensor", ["permb", "b2b"], ["po0"],
                 lambda e, tc=tc: e.matmul(po0[:], permb[:, tc * 128:(tc + 1) * 128],
                                           b2b[:, 0:512], start=True, stop=False, skip_group_check=True))
            C.op("tensor", ["permb", "b2b"], ["po1"],
                 lambda e, tc=tc: e.matmul(po1[:], permb[:, tc * 128:(tc + 1) * 128],
                                           b2b[:, 512:1024], start=True, stop=False, skip_group_check=True))
            for cp in range(32):
                pin = pin0 if cp % 2 == 0 else pin1
                pik = f"pin{cp % 2}"
                mik = f"mib{cp % 2}"
                for kc in range(8):
                    C.op("tensor", ["w1b", "xnb"], [pik],
                         lambda e, tc=tc, cp=cp, kc=kc, pin=pin: e.matmul(
                             pin[:], w1b[:, kc * 4096 + cp * 128: kc * 4096 + (cp + 1) * 128],
                             xnb[:, kc * TPC + tc * 128: kc * TPC + (tc + 1) * 128],
                             start=(kc == 0), stop=(kc == 7), skip_group_check=True),
                         accum=(kc != 0))
                C.op("scalar", [pik, "b1f"], ["mif"],
                     lambda e, cp=cp, pin=pin: e.activation(
                         mif[:], pin[:], AF.Relu, bias=b1f[:, cp:cp + 1]))
                mib = mib0 if cp % 2 == 0 else mib1
                C.op("vector", ["mif", "perm"], [mik],
                     lambda e, tc=tc, mib=mib: e.tensor_tensor(
                         mib[:], mif[:], perm[:, tc * 128:(tc + 1) * 128], op=ALU.mult))
                C.op("tensor", [mik, "w2b"], ["po0"],
                     lambda e, cp=cp, mib=mib: e.matmul(
                         po0[:], mib[:], w2b[:, cp * DM: cp * DM + 512],
                         start=False, stop=(cp == 31), skip_group_check=True),
                     accum=True)
                C.op("tensor", [mik, "w2b"], ["po1"],
                     lambda e, cp=cp, mib=mib: e.matmul(
                         po1[:], mib[:], w2b[:, cp * DM + 512: (cp + 1) * DM],
                         start=False, stop=(cp == 31), skip_group_check=True),
                     accum=True)
            C.op("vector", ["po0"], ["f0sb"], lambda e: e.tensor_copy(f0sb[:, 0:512], po0[:]))
            C.op("vector", ["po1"], ["f0sb"], lambda e: e.tensor_copy(f0sb[:, 512:1024], po1[:]))
            C.dma("gpsimd", ["f0sb"], ["f0"],
                  lambda e, tc=tc: e.dma_start(out=f0[tc * 128:(tc + 1) * 128, :], in_=f0sb[:]))
        C.sync_on("gpsimd", ["f0"])
        C.emit(block)
    return nc


def build_p3():
    nc = bass.Bass("TRN2", target_bir_lowering=False)
    qin = nc.dram_tensor("qin", [8 * 64, T], F32, kind="ExternalInput").ap()
    kin = nc.dram_tensor("kin", [8 * 64, T], F32, kind="ExternalInput").ap()
    vin = nc.dram_tensor("vin", [T, 8 * 64], BF16, kind="ExternalInput").ap()
    eyed = nc.dram_tensor("eyed", [128, 128], F32, kind="ExternalInput").ap()
    aout = nc.dram_tensor("aout", [8 * 64, T], F32, kind="ExternalOutput").ap()
    LOC = T - G
    es = contextlib.ExitStack()
    with es:
        qsb = _sb(nc, es, "qsb", (64, T), F32)
        ksb = _sb(nc, es, "ksb", (64, T), F32)
        qbf = _sb(nc, es, "qbf", (64, T), BF16)
        kbf = _sb(nc, es, "kbf", (64, T), BF16)
        vbf = _sb(nc, es, "vbf", (128, 32 * 64), BF16)
        eye = _sb(nc, es, "eye", (128, 128), F32)
        eyeb = _sb(nc, es, "eyeb", (128, 128), BF16)
        m1 = _sb(nc, es, "m1", (128, 1), F32)
        nb = _sb(nc, es, "nb", (128, 1), F32)
        ssum = _sb(nc, es, "ssum", (128, 1), F32)
        attn_f = _sb(nc, es, "attn_f", (128, G), F32)
        abf = _sb(nc, es, "abf", (128, 4 * G), BF16)
        atT = _sb(nc, es, "atT", (128, G), BF16)
        kle = _sb(nc, es, "kle", (64, LOC), F32)
        klp = _sb(nc, es, "klp", (64, T), BF16)
        klT = _sb(nc, es, "klT", (128, 32 * 64), BF16)
        kvb = _sb(nc, es, "kvb", (64, 64), BF16)
        aosb = _sb(nc, es, "aosb", (64, T), F32)
        ps = _ps(nc, es, "ps", (128, G))
        pg = _ps(nc, es, "pg", (64, G))
        ptb = _ps(nc, es, "ptb", (128, 128), BF16)
        pkv = _ps(nc, es, "pkv", (64, 64))
        plo = _ps(nc, es, "plo", (64, 512))
        C = Ctx(nc, es)
        block = es.enter_context(nc.Block())

        C.dma("sync", [], ["eye"], lambda e: e.dma_start(out=eye[:], in_=eyed))
        C.op("gpsimd", ["eye"], ["eyeb"], lambda e: e.tensor_copy(eyeb[:], eye[:]))
        isz = [(0, 128), (128, 128), (256, 128), (384, 25)]
        for j in range(8):
            C.dma("sync", [], ["qsb"],
                  lambda e, j=j: e.dma_start(out=qsb[:], in_=qin[j * 64:(j + 1) * 64, :]))
            C.dma("sync", [], ["ksb"],
                  lambda e, j=j: e.dma_start(out=ksb[:], in_=kin[j * 64:(j + 1) * 64, :]))
            C.dma("sync", [], ["vbf"],
                  lambda e, j=j: e.dma_start(
                      out=vbf[:].rearrange("p (tt e) -> p tt e", e=64),
                      in_=vin[:, j * 64:(j + 1) * 64].rearrange("(tt p) e -> p tt e", p=128)))
            C.op("gpsimd", ["qsb"], ["qbf"], lambda e: e.tensor_copy(qbf[:], qsb[:]))
            C.op("gpsimd", ["ksb"], ["kbf"], lambda e: e.tensor_copy(kbf[:], ksb[:]))
            for ic, (i0, sz) in enumerate(isz):
                C.op("tensor", ["qbf", "kbf"], ["ps"],
                     lambda e, i0=i0, sz=sz: e.matmul(
                         ps[0:sz, :], qbf[:, i0:i0 + sz], kbf[:, 0:G], start=True, stop=True, skip_group_check=True))
                C.op("vector", ["ps"], ["m1"],
                     lambda e, sz=sz: e.reduce_max(m1[0:sz, :], ps[0:sz, :], axis=AX.X))
                C.op("vector", ["m1"], ["nb"],
                     lambda e, sz=sz: e.tensor_scalar_mul(nb[0:sz, :], m1[0:sz, :], -0.125))
                C.op("scalar", ["ps", "nb"], ["attn_f"],
                     lambda e, sz=sz: e.activation(attn_f[0:sz, :], ps[0:sz, :], AF.Exp,
                                                   bias=nb[0:sz, :], scale=0.125))
                C.op("vector", ["attn_f"], ["ssum"],
                     lambda e, sz=sz: e.reduce_sum(ssum[0:sz, :], attn_f[0:sz, :], axis=AX.X))
                C.op("vector", ["ssum"], ["ssum"],
                     lambda e, sz=sz: e.reciprocal(ssum[0:sz, :], ssum[0:sz, :]))
                C.op("vector", ["attn_f", "ssum"], ["abf"],
                     lambda e, ic=ic, sz=sz: e.tensor_scalar(
                         abf[0:sz, ic * G:(ic + 1) * G], attn_f[0:sz, :], ssum[0:sz, :],
                         None, op0=ALU.mult))
            jsz = [(0, 128), (128, 128), (256, 128), (384, 25)]
            for jc, (j0, szj) in enumerate(jsz):
                for ic, (i0, szi) in enumerate(isz):
                    C.op("tensor", ["abf", "eyeb"], ["ptb"],
                         lambda e, ic=ic, j0=j0, szj=szj, szi=szi: e.transpose(
                             ptb[0:szj, 0:szi], abf[0:szi, ic * G + j0: ic * G + j0 + szj],
                             eyeb[0:szi, 0:szi]))
                    C.op("vector", ["ptb"], ["atT"],
                         lambda e, i0=i0, szj=szj, szi=szi: e.tensor_copy(
                             atT[0:szj, i0:i0 + szi], ptb[0:szj, 0:szi]))
                C.op("tensor", ["atT", "vbf"], ["pg"],
                     lambda e, jc=jc, szj=szj: e.matmul(
                         pg[:, :], vbf[0:szj, jc * 64:(jc + 1) * 64], atT[0:szj, 0:G],
                         start=(jc == 0), stop=(jc == 3), skip_group_check=True),
                     accum=(jc != 0))
            C.op("vector", ["pg"], ["aosb"], lambda e: e.tensor_copy(aosb[:, 0:G], pg[:, :]))
            C.op("vector", ["ksb"], ["m1"],
                 lambda e: e.reduce_max(m1[0:64, :], ksb[:, G:T], axis=AX.X))
            C.op("vector", ["m1"], ["nb"],
                 lambda e: e.tensor_scalar_mul(nb[0:64, :], m1[0:64, :], -1.0))
            C.op("scalar", ["ksb", "nb"], ["kle"],
                 lambda e: e.activation(kle[:], ksb[:, G:T], AF.Exp, bias=nb[0:64, :]))
            C.op("vector", ["kle"], ["ssum"],
                 lambda e: e.reduce_sum(ssum[0:64, :], kle[:], axis=AX.X))
            C.op("vector", ["ssum"], ["ssum"],
                 lambda e: e.reciprocal(ssum[0:64, :], ssum[0:64, :]))
            C.op("vector", [], ["klp"], lambda e: e.memset(klp[:, 0:G], 0.0))
            C.op("vector", ["kle", "ssum"], ["klp"],
                 lambda e: e.tensor_scalar(klp[:, G:T], kle[:], ssum[0:64, :], None, op0=ALU.mult))
            for jc in range(32):
                C.op("tensor", ["klp", "eyeb"], ["ptb"],
                     lambda e, jc=jc: e.transpose(
                         ptb[:, 0:64], klp[:, jc * 128:(jc + 1) * 128], eyeb[0:64, 0:64]))
                C.op("vector", ["ptb"], ["klT"],
                     lambda e, jc=jc: e.tensor_copy(klT[:, jc * 64:(jc + 1) * 64], ptb[:, 0:64]))
            for jc in range(32):
                C.op("tensor", ["klT", "vbf"], ["pkv"],
                     lambda e, jc=jc: e.matmul(
                         pkv[:], vbf[:, jc * 64:(jc + 1) * 64], klT[:, jc * 64:(jc + 1) * 64],
                         start=(jc == 0), stop=(jc == 31), skip_group_check=True),
                     accum=(jc != 0))
            C.op("vector", ["pkv"], ["kvb"], lambda e: e.tensor_copy(kvb[:], pkv[:]))
            off = G
            while off < T:
                sz = min(512, T - off)
                C.op("tensor", ["kvb", "qbf"], ["plo"],
                     lambda e, off=off, sz=sz: e.matmul(
                         plo[:, 0:sz], kvb[:], qbf[:, off:off + sz], start=True, stop=True, skip_group_check=True))
                C.op("vector", ["plo"], ["aosb"],
                     lambda e, off=off, sz=sz: e.tensor_copy(aosb[:, off:off + sz], plo[:, 0:sz]))
                off += sz
            C.dma("gpsimd", ["aosb"], ["aout"],
                  lambda e, j=j: e.dma_start(out=aout[j * 64:(j + 1) * 64, :], in_=aosb[:]))
        C.sync_on("gpsimd", ["aout"])
        C.emit(block)
    return nc


def build_p4():
    nc = bass.Bass("TRN2", target_bir_lowering=False)
    x = nc.dram_tensor("x", [TPC, DM], F32, kind="ExternalInput").ap()
    f0 = nc.dram_tensor("f0", [TPC, DM], F32, kind="ExternalInput").ap()
    ao = nc.dram_tensor("ao", [DM, TPC], F32, kind="ExternalInput").ap()
    wo = nc.dram_tensor("wo", [DM, DM], F32, kind="ExternalInput").ap()
    x1 = nc.dram_tensor("x1", [TPC, DM], F32, kind="ExternalOutput").ap()
    es = contextlib.ExitStack()
    with es:
        stage = _sb(nc, es, "stage", (128, 8 * TPC), F32)
        aob = _sb(nc, es, "aob", (128, 8 * TPC), BF16)
        wob = _sb(nc, es, "wob", (128, 8 * DM), BF16)
        xtile = _sb(nc, es, "xtile", (128, DM), F32)
        ftile = _sb(nc, es, "ftile", (128, DM), F32)
        osb = _sb(nc, es, "osb", (128, DM), F32)
        p0 = _ps(nc, es, "p0", (128, 512))
        p1 = _ps(nc, es, "p1", (128, 512))
        C = Ctx(nc, es)
        block = es.enter_context(nc.Block())
        C.dma("sync", [], ["stage"],
              lambda e: e.dma_start(out=stage[:].rearrange("p (m t) -> p m t", t=TPC),
                                    in_=ao.rearrange("(m p) t -> p m t", p=128)))
        C.op("gpsimd", ["stage"], ["aob"], lambda e: e.tensor_copy(aob[:], stage[:]))
        C.dma("sync", [], ["stage"],
              lambda e: e.dma_start(out=stage[:, 0:8 * DM].rearrange("p (kc m) -> p kc m", m=DM),
                                    in_=wo.rearrange("(kc p) m -> p kc m", p=128)))
        C.op("gpsimd", ["stage"], ["wob"], lambda e: e.tensor_copy(wob[:], stage[:, 0:8 * DM]))
        for tc in range(16):
            for nn in range(2):
                pb = p0 if nn == 0 else p1
                pk = f"p{nn}"
                for kc in range(8):
                    C.op("tensor", ["aob", "wob"], [pk],
                         lambda e, tc=tc, nn=nn, kc=kc, pb=pb: e.matmul(
                             pb[:], aob[:, kc * TPC + tc * 128: kc * TPC + (tc + 1) * 128],
                             wob[:, kc * DM + nn * 512: kc * DM + (nn + 1) * 512],
                             start=(kc == 0), stop=(kc == 7), skip_group_check=True),
                         accum=(kc != 0))
            C.dma("sync", [], ["xtile"],
                  lambda e, tc=tc: e.dma_start(out=xtile[:], in_=x[tc * 128:(tc + 1) * 128, :]))
            C.dma("sync", [], ["ftile"],
                  lambda e, tc=tc: e.dma_start(out=ftile[:], in_=f0[tc * 128:(tc + 1) * 128, :]))
            for nn in range(2):
                pb = p0 if nn == 0 else p1
                pk = f"p{nn}"
                C.op("vector", [pk, "xtile"], ["osb"],
                     lambda e, nn=nn, pb=pb: e.tensor_tensor(
                         osb[:, nn * 512:(nn + 1) * 512], pb[:],
                         xtile[:, nn * 512:(nn + 1) * 512], op=ALU.add))
                C.op("vector", ["osb", "ftile"], ["osb"],
                     lambda e, nn=nn: e.tensor_tensor(
                         osb[:, nn * 512:(nn + 1) * 512], osb[:, nn * 512:(nn + 1) * 512],
                         ftile[:, nn * 512:(nn + 1) * 512], op=ALU.add))
            C.dma("gpsimd", ["osb"], ["x1"],
                  lambda e, tc=tc: e.dma_start(out=x1[tc * 128:(tc + 1) * 128, :], in_=osb[:]))
        C.sync_on("gpsimd", ["x1"])
        C.emit(block)
    return nc


def build_p5():
    W = 1152
    nc = bass.Bass("TRN2", target_bir_lowering=False)
    x1p = nc.dram_tensor("x1p", [2176, DM], F32, kind="ExternalInput").ap()
    inw = nc.dram_tensor("inw", [DM, 2 * DI], F32, kind="ExternalInput").ap()
    inb = nc.dram_tensor("inb", [128, 32], F32, kind="ExternalInput").ap()
    cwd = nc.dram_tensor("cwd", [128, 64], F32, kind="ExternalInput").ap()
    cbd = nc.dram_tensor("cbd", [128, 16], F32, kind="ExternalInput").ap()
    bwd = nc.dram_tensor("bwd", [DI, DS], F32, kind="ExternalInput").ap()
    bbd = nc.dram_tensor("bbd", [16, 1], F32, kind="ExternalInput").ap()
    cwd2 = nc.dram_tensor("cwd2", [DI, DS], F32, kind="ExternalInput").ap()
    cbd2 = nc.dram_tensor("cbd2", [16, 1], F32, kind="ExternalInput").ap()
    dtw = nc.dram_tensor("dtw", [DI, DI], F32, kind="ExternalInput").ap()
    dtb = nc.dram_tensor("dtb", [128, 16], F32, kind="ExternalInput").ap()
    eyed = nc.dram_tensor("eyed", [128, 128], F32, kind="ExternalInput").ap()
    epsd = nc.dram_tensor("epsd", [128, 1], F32, kind="ExternalInput").ap()
    oned = nc.dram_tensor("oned", [128, 1], F32, kind="ExternalInput").ap()
    xco = nc.dram_tensor("xco", [DI, TPC], F32, kind="ExternalOutput").ap()
    dto = nc.dram_tensor("dto", [DI, TPC], F32, kind="ExternalOutput").ap()
    sgo = nc.dram_tensor("sgo", [DI, TPC], F32, kind="ExternalOutput").ap()
    bmo = nc.dram_tensor("bmo", [DS, TPC], F32, kind="ExternalOutput").ap()
    cmo = nc.dram_tensor("cmo", [DS, TPC], F32, kind="ExternalOutput").ap()
    xn2o = nc.dram_tensor("xn2o", [DM, TPC], F32, kind="ExternalOutput").ap()
    es = contextlib.ExitStack()
    with es:
        xtile = _sb(nc, es, "xtile", (128, DM), F32)
        xn = _sb(nc, es, "xn", (128, DM), F32)
        scr = _sb(nc, es, "scr", (128, DM), F32)
        mean = _sb(nc, es, "mean", (128, 1), F32)
        var = _sb(nc, es, "var", (128, 1), F32)
        epscol = _sb(nc, es, "epscol", (128, 1), F32)
        onecol = _sb(nc, es, "onecol", (128, 1), F32)
        eye = _sb(nc, es, "eye", (128, 128), F32)
        x2T = _sb(nc, es, "x2T", (128, 8 * W), F32)
        x2Tb = _sb(nc, es, "x2Tb", (128, 8 * W), BF16)
        wstage = _sb(nc, es, "wstage", (128, 16 * 128), F32)
        iwb = _sb(nc, es, "iwb", (128, 8 * 128), BF16)
        dwb = _sb(nc, es, "dwb", (128, 16 * 128), BF16)
        bwb = _sb(nc, es, "bwb", (128, 16 * DS), BF16)
        cwb = _sb(nc, es, "cwb", (128, 16 * DS), BF16)
        inbc = _sb(nc, es, "inbc", (128, 32), F32)
        cw = _sb(nc, es, "cw", (128, 64), F32)
        cb = _sb(nc, es, "cb", (128, 16), F32)
        dtbc = _sb(nc, es, "dtbc", (128, 16), F32)
        bbc = _sb(nc, es, "bbc", (16, 1), F32)
        cbc2 = _sb(nc, es, "cbc2", (16, 1), F32)
        xzm = _sb(nc, es, "xzm", (128, W), F32)
        acc = _sb(nc, es, "acc", (128, 1024), F32)
        xcm = _sb(nc, es, "xcm", (128, 1024), F32)
        xcb = _sb(nc, es, "xcb", (128, 16 * 1024), BF16)
        sgm = _sb(nc, es, "sgm", (128, W), F32)
        zb = _sb(nc, es, "zb", (128, 512), F32)
        ab = _sb(nc, es, "ab", (128, 512), F32)
        lb = _sb(nc, es, "lb", (128, 512), F32)
        mb = _sb(nc, es, "mb", (128, 512), F32)
        bm = _sb(nc, es, "bm", (16, 512), F32)
        ptr = _ps(nc, es, "ptr", (128, 128))
        pp0 = _ps(nc, es, "pp0", (128, 512))
        pp1 = _ps(nc, es, "pp1", (128, 512))
        pbc = _ps(nc, es, "pbc", (16, 512))
        C = Ctx(nc, es)
        block = es.enter_context(nc.Block())

        C.dma("sync", [], ["epscol"], lambda e: e.dma_start(out=epscol[:], in_=epsd))
        C.dma("sync", [], ["onecol"], lambda e: e.dma_start(out=onecol[:], in_=oned))
        C.dma("sync", [], ["eye"], lambda e: e.dma_start(out=eye[:], in_=eyed))
        C.dma("sync", [], ["inbc"], lambda e: e.dma_start(out=inbc[:], in_=inb))
        C.dma("sync", [], ["cw"], lambda e: e.dma_start(out=cw[:], in_=cwd))
        C.dma("sync", [], ["cb"], lambda e: e.dma_start(out=cb[:], in_=cbd))
        C.dma("sync", [], ["dtbc"], lambda e: e.dma_start(out=dtbc[:], in_=dtb))
        C.dma("sync", [], ["bbc"], lambda e: e.dma_start(out=bbc[:], in_=bbd))
        C.dma("sync", [], ["cbc2"], lambda e: e.dma_start(out=cbc2[:], in_=cbd2))
        C.dma("sync", [], ["wstage"],
              lambda e: e.dma_start(out=wstage[:, 0:16 * DS].rearrange("p (kc n) -> p kc n", n=DS),
                                    in_=bwd.rearrange("(kc p) n -> p kc n", p=128)))
        C.op("gpsimd", ["wstage"], ["bwb"], lambda e: e.tensor_copy(bwb[:], wstage[:, 0:16 * DS]))
        C.dma("sync", [], ["wstage"],
              lambda e: e.dma_start(out=wstage[:, 0:16 * DS].rearrange("p (kc n) -> p kc n", n=DS),
                                    in_=cwd2.rearrange("(kc p) n -> p kc n", p=128)))
        C.op("gpsimd", ["wstage"], ["cwb"], lambda e: e.tensor_copy(cwb[:], wstage[:, 0:16 * DS]))

        for th in range(2):
            r0 = th * 1024
            for i in range(9):
                C.dma("sync", [], ["xtile"],
                      lambda e, r0=r0, i=i: e.dma_start(
                          out=xtile[:], in_=x1p[r0 + i * 128: r0 + (i + 1) * 128, :]))
                _ln_tile(C, xtile[:], "xtile", xn, mean, var, scr, epscol)
                for kc in range(8):
                    C.op("tensor", ["xn", "eye"], ["ptr"],
                         lambda e, kc=kc: e.transpose(ptr[:], xn[:, kc * 128:(kc + 1) * 128],
                                                      eye[:]))
                    C.op("vector", ["ptr"], ["x2T"],
                         lambda e, kc=kc, i=i: e.tensor_copy(
                             x2T[:, kc * W + i * 128: kc * W + (i + 1) * 128], ptr[:]))
            C.op("gpsimd", ["x2T"], ["x2Tb"], lambda e: e.tensor_copy(x2Tb[:], x2T[:]))
            C.dma("gpsimd", ["x2T"], ["xn2o"],
                  lambda e, th=th: e.dma_start(
                      out=xn2o[:, th * 1024:(th + 1) * 1024].rearrange("(kc p) t -> p kc t", p=128),
                      in_=x2T[:].rearrange("p (kc t) -> p kc t", t=W)[:, :, 1:1025]))
            for m in range(32):
                C.dma("sync", [], ["wstage"],
                      lambda e, m=m: e.dma_start(
                          out=wstage[:, 0:8 * 128].rearrange("p (kc c) -> p kc c", c=128),
                          in_=inw[:, m * 128:(m + 1) * 128].rearrange("(kc p) c -> p kc c", p=128)))
                C.op("gpsimd", ["wstage"], ["iwb"],
                     lambda e: e.tensor_copy(iwb[:], wstage[:, 0:8 * 128]))
                for nt in range(3):
                    pb = pp0 if nt % 2 == 0 else pp1
                    pk = f"pp{nt % 2}"
                    for kc in range(8):
                        C.op("tensor", ["iwb", "x2Tb"], [pk],
                             lambda e, nt=nt, kc=kc, pb=pb: e.matmul(
                                 pb[:, 0:384], iwb[:, kc * 128:(kc + 1) * 128],
                                 x2Tb[:, kc * W + nt * 384: kc * W + (nt + 1) * 384],
                                 start=(kc == 0), stop=(kc == 7), skip_group_check=True),
                             accum=(kc != 0))
                    if m < 16:
                        C.op("vector", [pk, "inbc"], ["xzm"],
                             lambda e, m=m, nt=nt, pb=pb: e.tensor_scalar(
                                 xzm[:, nt * 384:(nt + 1) * 384], pb[:, 0:384],
                                 inbc[:, m:m + 1], None, op0=ALU.add))
                    else:
                        C.op("scalar", [pk, "inbc"], ["sgm"],
                             lambda e, m=m, nt=nt, pb=pb: e.activation(
                                 sgm[:, nt * 384:(nt + 1) * 384], pb[:, 0:384], AF.Silu,
                                 bias=inbc[:, m:m + 1]))
                if m < 16:
                    C.op("vector", ["xzm", "cw"], ["acc"],
                         lambda e, m=m: e.tensor_scalar(
                             acc[:], xzm[:, 0:1024], cw[:, m * 4: m * 4 + 1], None, op0=ALU.mult))
                    for jj in range(1, 4):
                        C.op("vector", ["xzm", "cw", "acc"], ["acc"],
                             lambda e, m=m, jj=jj: e.scalar_tensor_tensor(
                                 acc[:], xzm[:, jj:jj + 1024], cw[:, m * 4 + jj: m * 4 + jj + 1],
                                 acc[:], op0=ALU.mult, op1=ALU.add))
                    C.op("scalar", ["acc", "cb"], ["xcm"],
                         lambda e, m=m: e.activation(xcm[:], acc[:], AF.Silu,
                                                     bias=cb[:, m:m + 1]))
                    C.dma("gpsimd", ["xcm"], ["xco"],
                          lambda e, m=m, th=th: e.dma_start(
                              out=xco[m * 128:(m + 1) * 128, th * 1024:(th + 1) * 1024],
                              in_=xcm[:]))
                    C.op("gpsimd", ["xcm"], ["xcb"],
                         lambda e, m=m: e.tensor_copy(xcb[:, m * 1024:(m + 1) * 1024], xcm[:]))
                else:
                    C.dma("gpsimd", ["sgm"], ["sgo"],
                          lambda e, m=m, th=th: e.dma_start(
                              out=sgo[(m - 16) * 128:(m - 15) * 128, th * 1024:(th + 1) * 1024],
                              in_=sgm[:, 1:1025]))
            for m2 in range(16):
                C.dma("sync", [], ["wstage"],
                      lambda e, m2=m2: e.dma_start(
                          out=wstage[:].rearrange("p (kc c) -> p kc c", c=128),
                          in_=dtw[:, m2 * 128:(m2 + 1) * 128].rearrange("(kc p) c -> p kc c",
                                                                       p=128)))
                C.op("gpsimd", ["wstage"], ["dwb"], lambda e: e.tensor_copy(dwb[:], wstage[:]))
                for nt in range(2):
                    pb = pp0 if nt == 0 else pp1
                    pk = f"pp{nt}"
                    for kc in range(16):
                        C.op("tensor", ["dwb", "xcb"], [pk],
                             lambda e, nt=nt, kc=kc, pb=pb: e.matmul(
                                 pb[:], dwb[:, kc * 128:(kc + 1) * 128],
                                 xcb[:, kc * 1024 + nt * 512: kc * 1024 + (nt + 1) * 512],
                                 start=(kc == 0), stop=(kc == 15), skip_group_check=True),
                             accum=(kc != 0))
                    C.op("vector", [pk, "dtbc"], ["zb"],
                         lambda e, m2=m2, pb=pb: e.tensor_scalar(
                             zb[:], pb[:], dtbc[:, m2:m2 + 1], None, op0=ALU.add))
                    C.op("scalar", ["zb"], ["ab"], lambda e: e.activation(ab[:], zb[:], AF.Abs))
                    C.op("scalar", ["ab"], ["ab"],
                         lambda e: e.activation(ab[:], ab[:], AF.Exp, scale=-1.0))
                    C.op("scalar", ["ab", "onecol"], ["lb"],
                         lambda e: e.activation(lb[:], ab[:], AF.Ln, bias=onecol[:]))
                    C.op("vector", ["zb"], ["mb"],
                         lambda e: e.tensor_scalar_max(mb[:], zb[:], 0.0))
                    C.op("vector", ["lb", "mb"], ["zb"],
                         lambda e: e.tensor_tensor(zb[:], lb[:], mb[:], op=ALU.add))
                    C.dma("gpsimd", ["zb"], ["dto"],
                          lambda e, m2=m2, nt=nt, th=th: e.dma_start(
                              out=dto[m2 * 128:(m2 + 1) * 128,
                                      th * 1024 + nt * 512: th * 1024 + (nt + 1) * 512],
                              in_=zb[:]))
            for wsb, bias, outap, okey in ((bwb, bbc, bmo, "bmo"), (cwb, cbc2, cmo, "cmo")):
                for nt in range(2):
                    for kc in range(16):
                        C.op("tensor", ["bwb", "cwb", "xcb"], ["pbc"],
                             lambda e, wsb=wsb, nt=nt, kc=kc: e.matmul(
                                 pbc[:], wsb[:, kc * DS:(kc + 1) * DS],
                                 xcb[:, kc * 1024 + nt * 512: kc * 1024 + (nt + 1) * 512],
                                 start=(kc == 0), stop=(kc == 15), skip_group_check=True),
                             accum=(kc != 0))
                    C.op("vector", ["pbc"], ["bm"],
                         lambda e, bias=bias: e.tensor_scalar(
                             bm[:], pbc[:], bias[:], None, op0=ALU.add))
                    C.dma("gpsimd", ["bm"], [okey],
                          lambda e, outap=outap, nt=nt, th=th: e.dma_start(
                              out=outap[:, th * 1024 + nt * 512: th * 1024 + (nt + 1) * 512],
                              in_=bm[:]))
        C.sync_on("gpsimd", ["xco", "dto", "sgo", "bmo", "cmo", "xn2o"])
        C.emit(block)
    return nc


def build_p6():
    nc = bass.Bass("TRN2", target_bir_lowering=False)
    dtd = nc.dram_tensor("dtd", [1024, T], F32, kind="ExternalInput").ap()
    xcd = nc.dram_tensor("xcd", [1024, T], F32, kind="ExternalInput").ap()
    sgd = nc.dram_tensor("sgd", [1024, T], F32, kind="ExternalInput").ap()
    bmd = nc.dram_tensor("bmd", [DS, T], F32, kind="ExternalInput").ap()
    cmd = nc.dram_tensor("cmd", [DS, T], F32, kind="ExternalInput").ap()
    acold = nc.dram_tensor("acold", [128, 128], F32, kind="ExternalInput").ap()
    dpd = nc.dram_tensor("dpd", [128, 8], F32, kind="ExternalInput").ap()
    owd = nc.dram_tensor("owd", [1024, DM], F32, kind="ExternalInput").ap()
    onesd = nc.dram_tensor("onesd", [1, 128], F32, kind="ExternalInput").ap()
    opo = nc.dram_tensor("opo", [DM, T], F32, kind="ExternalOutput").ap()
    CH = 512
    es = contextlib.ExitStack()
    with es:
        dt = _sb(nc, es, "dt", (128, 8 * CH), F32)
        xc = _sb(nc, es, "xc", (128, 8 * CH), F32)
        sg = _sb(nc, es, "sg", (128, 8 * CH), F32)
        bmr = _sb(nc, es, "bmr", (1, CH), F32)
        cmr = _sb(nc, es, "cmr", (1, CH), F32)
        Bbc = _sb(nc, es, "Bbc", (128, DS * CH), F32)
        Cbc = _sb(nc, es, "Cbc", (128, DS * CH), F32)
        acol = _sb(nc, es, "acol", (128, 128), F32)
        dpc = _sb(nc, es, "dpc", (128, 8), F32)
        ones1 = _sb(nc, es, "ones1", (1, 128), F32)
        hstate = _sb(nc, es, "hstate", (128, 128), F32)
        dA = [_sb(nc, es, f"dA{i}", (128, CH), F32) for i in range(2)]
        dBx = [_sb(nc, es, f"dBx{i}", (128, CH), F32) for i in range(2)]
        hh = [_sb(nc, es, f"hh{i}", (128, CH), F32) for i in range(2)]
        yt = [_sb(nc, es, f"yt{i}", (128, CH), F32) for i in range(2)]
        ycc = _sb(nc, es, "ycc", (128, CH), F32)
        ybf = _sb(nc, es, "ybf", (128, 8 * CH), BF16)
        owst = _sb(nc, es, "owst", (128, 4 * DM), F32)
        owb = _sb(nc, es, "owb", (128, 8 * DM), BF16)
        osb = _sb(nc, es, "osb", (128, CH), F32)
        pb0 = _ps(nc, es, "pb0", (128, 512))
        pb1 = _ps(nc, es, "pb1", (128, 512))
        C = Ctx(nc, es)
        block = es.enter_context(nc.Block())

        C.dma("sync", [], ["acol"], lambda e: e.dma_start(out=acol[:], in_=acold))
        C.dma("sync", [], ["dpc"], lambda e: e.dma_start(out=dpc[:], in_=dpd))
        C.dma("sync", [], ["ones1"], lambda e: e.dma_start(out=ones1[:], in_=onesd))
        for hf in range(2):
            C.dma("sync", [], ["owst"],
                  lambda e, hf=hf: e.dma_start(
                      out=owst[:].rearrange("p (cc m) -> p cc m", m=DM),
                      in_=owd.rearrange("(cc p) m -> p cc m", p=128)[:, hf * 4:(hf + 1) * 4, :]))
            C.op("gpsimd", ["owst"], ["owb"],
                 lambda e, hf=hf: e.tensor_copy(owb[:, hf * 4 * DM:(hf + 1) * 4 * DM], owst[:]))
        C.op("vector", [], ["hstate"], lambda e: e.memset(hstate[:], 0.0))
        for ch in range(8):
            t0 = ch * CH
            C.dma("sync", [], ["dt"],
                  lambda e, t0=t0: e.dma_start(
                      out=dt[:].rearrange("p (cc t) -> p cc t", t=CH),
                      in_=dtd[:, t0:t0 + CH].rearrange("(cc p) t -> p cc t", p=128)))
            C.dma("sync", [], ["xc"],
                  lambda e, t0=t0: e.dma_start(
                      out=xc[:].rearrange("p (cc t) -> p cc t", t=CH),
                      in_=xcd[:, t0:t0 + CH].rearrange("(cc p) t -> p cc t", p=128)))
            C.dma("sync", [], ["sg"],
                  lambda e, t0=t0: e.dma_start(
                      out=sg[:].rearrange("p (cc t) -> p cc t", t=CH),
                      in_=sgd[:, t0:t0 + CH].rearrange("(cc p) t -> p cc t", p=128)))
            for n in range(DS):
                pb = pb0 if n % 2 == 0 else pb1
                pk = f"pb{n % 2}"
                C.dma("sync", [], ["bmr"],
                      lambda e, t0=t0, n=n: e.dma_start(
                          out=bmr[:], in_=bmd[n:n + 1, t0:t0 + CH]))
                C.op("tensor", ["bmr", "ones1"], [pk],
                     lambda e, pb=pb: e.matmul(
                         pb[:], ones1[:], bmr[:], start=True, stop=True, skip_group_check=True))
                C.op("vector", [pk], ["Bbc"],
                     lambda e, n=n, pb=pb: e.tensor_copy(Bbc[:, n * CH:(n + 1) * CH], pb[:]))
                C.dma("sync", [], ["cmr"],
                      lambda e, t0=t0, n=n: e.dma_start(
                          out=cmr[:], in_=cmd[n:n + 1, t0:t0 + CH]))
                C.op("tensor", ["cmr", "ones1"], [pk],
                     lambda e, pb=pb: e.matmul(
                         pb[:], ones1[:], cmr[:], start=True, stop=True, skip_group_check=True))
                C.op("vector", [pk], ["Cbc"],
                     lambda e, n=n, pb=pb: e.tensor_copy(Cbc[:, n * CH:(n + 1) * CH], pb[:]))
            for cc in range(8):
                C.op("vector", ["xc", "dpc"], ["ycc"],
                     lambda e, cc=cc: e.tensor_scalar(
                         ycc[:], xc[:, cc * CH:(cc + 1) * CH], dpc[:, cc:cc + 1],
                         None, op0=ALU.mult))
                for n in range(DS):
                    i = n % 2
                    kda, kdb, kh, kyt = f"dA{i}", f"dBx{i}", f"hh{i}", f"yt{i}"
                    col = cc * 16 + n
                    C.op("scalar", ["dt", "acol"], [kda],
                         lambda e, cc=cc, col=col, i=i: e.activation(
                             dA[i][:], dt[:, cc * CH:(cc + 1) * CH], AF.Exp,
                             scale=acol[:, col:col + 1]))
                    C.op("gpsimd", ["dt", "Bbc"], [kdb],
                         lambda e, cc=cc, n=n, i=i: e.tensor_tensor(
                             dBx[i][:], dt[:, cc * CH:(cc + 1) * CH],
                             Bbc[:, n * CH:(n + 1) * CH], op=ALU.mult))
                    C.op("gpsimd", ["xc", kdb], [kdb],
                         lambda e, cc=cc, i=i: e.tensor_tensor(
                             dBx[i][:], dBx[i][:], xc[:, cc * CH:(cc + 1) * CH], op=ALU.mult))
                    C.op("vector", [kda, kdb, "hstate"], [kh],
                         lambda e, col=col, i=i: e.tensor_tensor_scan(
                             hh[i][:], dA[i][:], dBx[i][:], hstate[:, col:col + 1],
                             op0=ALU.mult, op1=ALU.add))
                    C.op("vector", [kh], ["hstate"],
                         lambda e, col=col, i=i: e.tensor_copy(
                             hstate[:, col:col + 1], hh[i][:, CH - 1:CH]))
                    C.op("gpsimd", [kh, "Cbc"], [kyt],
                         lambda e, n=n, i=i: e.tensor_tensor(
                             yt[i][:], hh[i][:], Cbc[:, n * CH:(n + 1) * CH], op=ALU.mult))
                    C.op("vector", [kyt, "ycc"], ["ycc"],
                         lambda e, i=i: e.tensor_tensor(ycc[:], ycc[:], yt[i][:], op=ALU.add))
                C.op("vector", ["ycc", "sg"], ["ycc"],
                     lambda e, cc=cc: e.tensor_tensor(
                         ycc[:], ycc[:], sg[:, cc * CH:(cc + 1) * CH], op=ALU.mult))
                C.op("gpsimd", ["ycc"], ["ybf"],
                     lambda e, cc=cc: e.tensor_copy(ybf[:, cc * CH:(cc + 1) * CH], ycc[:]))
            for m in range(8):
                pb = pb0 if m % 2 == 0 else pb1
                pk = f"pb{m % 2}"
                for cc in range(8):
                    C.op("tensor", ["owb", "ybf"], [pk],
                         lambda e, m=m, cc=cc, pb=pb: e.matmul(
                             pb[:], owb[:, cc * DM + m * 128: cc * DM + (m + 1) * 128],
                             ybf[:, cc * CH:(cc + 1) * CH],
                             start=(cc == 0), stop=(cc == 7), skip_group_check=True),
                         accum=(cc != 0))
                C.op("vector", [pk], ["osb"], lambda e, pb=pb: e.tensor_copy(osb[:], pb[:]))
                C.dma("gpsimd", ["osb"], ["opo"],
                      lambda e, m=m, t0=t0: e.dma_start(
                          out=opo[m * 128:(m + 1) * 128, t0:t0 + CH], in_=osb[:]))
        C.sync_on("gpsimd", ["opo"])
        C.emit(block)
    return nc


def build_p7():
    nc = bass.Bass("TRN2", target_bir_lowering=False)
    x1 = nc.dram_tensor("x1", [TPC, DM], F32, kind="ExternalInput").ap()
    xn2 = nc.dram_tensor("xn2", [DM, TPC], F32, kind="ExternalInput").ap()
    opa = nc.dram_tensor("opa", [DM, TPC], F32, kind="ExternalInput").ap()
    opb = nc.dram_tensor("opb", [DM, TPC], F32, kind="ExternalInput").ap()
    outbd = nc.dram_tensor("outbd", [128, 8], F32, kind="ExternalInput").ap()
    gwd = nc.dram_tensor("gwd", [DM, NE], F32, kind="ExternalInput").ap()
    gbd = nc.dram_tensor("gbd", [1, NE], F32, kind="ExternalInput").ap()
    onesd = nc.dram_tensor("onesd", [1, 128], F32, kind="ExternalInput").ap()
    eyed = nc.dram_tensor("eyed", [128, 128], F32, kind="ExternalInput").ap()
    base = nc.dram_tensor("base", [DM, TPC], F32, kind="ExternalOutput").ap()
    cto = nc.dram_tensor("cto", [NE, TPC], F32, kind="ExternalOutput").ap()
    xnbo = nc.dram_tensor("xnbo", [DM, TPC], BF16, kind="ExternalOutput").ap()
    es = contextlib.ExitStack()
    with es:
        x1T = _sb(nc, es, "x1T", (128, 8 * TPC), F32)
        xnb = _sb(nc, es, "xnb", (128, 8 * TPC), BF16)
        xtile = _sb(nc, es, "xtile", (128, DM), F32)
        xstage = _sb(nc, es, "xstage", (128, TPC), F32)
        ostage = _sb(nc, es, "ostage", (128, TPC), F32)
        outbc = _sb(nc, es, "outbc", (128, 8), F32)
        gw = _sb(nc, es, "gw", (128, 8 * NE), F32)
        gwb = _sb(nc, es, "gwb", (128, 8 * NE), BF16)
        gb = _sb(nc, es, "gb", (1, NE), F32)
        gbb = _sb(nc, es, "gbb", (1, NE), BF16)
        ones1 = _sb(nc, es, "ones1", (1, 128), F32)
        onesb = _sb(nc, es, "onesb", (1, 128), BF16)
        eye = _sb(nc, es, "eye", (128, 128), F32)
        s = _sb(nc, es, "s", (128, NE), F32)
        eq1 = _sb(nc, es, "eq1", (128, NE), F32)
        s2 = _sb(nc, es, "s2", (128, NE), F32)
        eq2 = _sb(nc, es, "eq2", (128, NE), F32)
        m1 = _sb(nc, es, "m1", (128, 1), F32)
        m2 = _sb(nc, es, "m2", (128, 1), F32)
        dd = _sb(nc, es, "dd", (128, 1), F32)
        ed = _sb(nc, es, "ed", (128, 1), F32)
        p1c = _sb(nc, es, "p1c", (128, 1), F32)
        p2c = _sb(nc, es, "p2c", (128, 1), F32)
        comb = _sb(nc, es, "comb", (128, NE), F32)
        ctsb = _sb(nc, es, "ctsb", (NE, TPC), F32)
        ptr = _ps(nc, es, "ptr", (128, 128))
        pgs = _ps(nc, es, "pgs", (128, NE))
        C = Ctx(nc, es)
        block = es.enter_context(nc.Block())

        C.dma("sync", [], ["outbc"], lambda e: e.dma_start(out=outbc[:], in_=outbd))
        C.dma("sync", [], ["ones1"], lambda e: e.dma_start(out=ones1[:], in_=onesd))
        C.dma("sync", [], ["eye"], lambda e: e.dma_start(out=eye[:], in_=eyed))
        C.dma("sync", [], ["gb"], lambda e: e.dma_start(out=gb[:], in_=gbd))
        C.dma("sync", [], ["gw"],
              lambda e: e.dma_start(out=gw[:].rearrange("p (kc m) -> p kc m", m=NE),
                                    in_=gwd.rearrange("(kc p) m -> p kc m", p=128)))
        C.op("gpsimd", ["gw"], ["gwb"], lambda e: e.tensor_copy(gwb[:], gw[:]))
        C.op("gpsimd", ["gb"], ["gbb"], lambda e: e.tensor_copy(gbb[:], gb[:]))
        C.op("gpsimd", ["ones1"], ["onesb"], lambda e: e.tensor_copy(onesb[:], ones1[:]))
        for tc in range(16):
            C.dma("sync", [], ["xtile"],
                  lambda e, tc=tc: e.dma_start(out=xtile[:], in_=x1[tc * 128:(tc + 1) * 128, :]))
            for kc in range(8):
                C.op("tensor", ["xtile", "eye"], ["ptr"],
                     lambda e, kc=kc: e.transpose(ptr[:], xtile[:, kc * 128:(kc + 1) * 128],
                                                  eye[:]))
                C.op("vector", ["ptr"], ["x1T"],
                     lambda e, kc=kc, tc=tc: e.tensor_copy(
                         x1T[:, kc * TPC + tc * 128: kc * TPC + (tc + 1) * 128], ptr[:]))
        for m in range(8):
            C.dma("sync", [], ["xstage"],
                  lambda e, m=m: e.dma_start(out=xstage[:], in_=xn2[m * 128:(m + 1) * 128, :]))
            C.op("gpsimd", ["xstage"], ["xnb"],
                 lambda e, m=m: e.tensor_copy(xnb[:, m * TPC:(m + 1) * TPC], xstage[:]))
            C.op("vector", ["xstage", "x1T"], ["xstage"],
                 lambda e, m=m: e.tensor_tensor(
                     xstage[:], xstage[:], x1T[:, m * TPC:(m + 1) * TPC], op=ALU.add))
            C.dma("sync", [], ["ostage"],
                  lambda e, m=m: e.dma_start(out=ostage[:], in_=opa[m * 128:(m + 1) * 128, :]))
            C.op("vector", ["xstage", "ostage"], ["xstage"],
                 lambda e: e.tensor_tensor(xstage[:], xstage[:], ostage[:], op=ALU.add))
            C.dma("sync", [], ["ostage"],
                  lambda e, m=m: e.dma_start(out=ostage[:], in_=opb[m * 128:(m + 1) * 128, :]))
            C.op("vector", ["xstage", "ostage"], ["xstage"],
                 lambda e: e.tensor_tensor(xstage[:], xstage[:], ostage[:], op=ALU.add))
            C.op("vector", ["xstage", "outbc"], ["xstage"],
                 lambda e, m=m: e.tensor_scalar(xstage[:], xstage[:], outbc[:, m:m + 1],
                                                None, op0=ALU.add))
            C.dma("gpsimd", ["xstage"], ["base"],
                  lambda e, m=m: e.dma_start(out=base[m * 128:(m + 1) * 128, :], in_=xstage[:]))
        C.dma("gpsimd", ["xnb"], ["xnbo"],
              lambda e: e.dma_start(out=xnbo.rearrange("(m p) t -> p m t", p=128),
                                    in_=xnb[:].rearrange("p (m t) -> p m t", t=TPC)))
        for tc in range(16):
            for kc in range(8):
                C.op("tensor", ["gwb", "xnb"], ["pgs"],
                     lambda e, tc=tc, kc=kc: e.matmul(
                         pgs[:], xnb[:, kc * TPC + tc * 128: kc * TPC + (tc + 1) * 128],
                         gwb[:, kc * NE:(kc + 1) * NE],
                         start=(kc == 0), stop=False, skip_group_check=True),
                     accum=(kc != 0))
            C.op("tensor", ["gbb", "onesb"], ["pgs"],
                 lambda e: e.matmul(pgs[:], onesb[:], gbb[:], start=False, stop=True, skip_group_check=True),
                 accum=True)
            C.op("vector", ["pgs"], ["s"], lambda e: e.tensor_copy(s[:], pgs[:]))
            C.op("vector", ["s"], ["m1"], lambda e: e.reduce_max(m1[:], s[:], axis=AX.X))
            C.op("vector", ["s", "m1"], ["eq1"],
                 lambda e: e.tensor_scalar(eq1[:], s[:], m1[:], None, op0=ALU.is_ge))
            C.op("vector", ["eq1", "s"], ["s2"],
                 lambda e: e.scalar_tensor_tensor(s2[:], eq1[:], -1e30, s[:],
                                                  op0=ALU.mult, op1=ALU.add))
            C.op("vector", ["s2"], ["m2"], lambda e: e.reduce_max(m2[:], s2[:], axis=AX.X))
            C.op("vector", ["s2", "m2"], ["eq2"],
                 lambda e: e.tensor_scalar(eq2[:], s2[:], m2[:], None, op0=ALU.is_ge))
            C.op("vector", ["m1", "m2"], ["dd"],
                 lambda e: e.tensor_tensor(dd[:], m2[:], m1[:], op=ALU.subtract))
            C.op("scalar", ["dd"], ["ed"], lambda e: e.activation(ed[:], dd[:], AF.Exp))
            C.op("vector", ["ed"], ["p1c"], lambda e: e.tensor_scalar_add(p1c[:], ed[:], 1.0))
            C.op("vector", ["p1c"], ["p1c"], lambda e: e.reciprocal(p1c[:], p1c[:]))
            C.op("vector", ["ed", "p1c"], ["p2c"],
                 lambda e: e.tensor_tensor(p2c[:], ed[:], p1c[:], op=ALU.mult))
            C.op("vector", ["eq1", "p1c"], ["eq1"],
                 lambda e: e.tensor_scalar(eq1[:], eq1[:], p1c[:], None, op0=ALU.mult))
            C.op("vector", ["eq2", "p2c", "eq1"], ["comb"],
                 lambda e: e.scalar_tensor_tensor(comb[:], eq2[:], p2c[:], eq1[:],
                                                  op0=ALU.mult, op1=ALU.add))
            C.op("tensor", ["comb", "eye"], ["ptr"],
                 lambda e: e.transpose(ptr[0:NE, :], comb[:], eye[:]))
            C.op("vector", ["ptr"], ["ctsb"],
                 lambda e, tc=tc: e.tensor_copy(ctsb[:, tc * 128:(tc + 1) * 128], ptr[0:NE, :]))
        C.dma("gpsimd", ["ctsb"], ["cto"], lambda e: e.dma_start(out=cto, in_=ctsb[:]))
        C.sync_on("gpsimd", ["base", "cto", "xnbo"])
        C.emit(block)
    return nc


def build_p8():
    nc = bass.Bass("TRN2", target_bir_lowering=False)
    xnb_d = nc.dram_tensor("xnb_d", [DM, TPC], BF16, kind="ExternalInput").ap()
    ctd = nc.dram_tensor("ctd", [1, TPC], F32, kind="ExternalInput").ap()
    accin = nc.dram_tensor("accin", [DM, TPC], F32, kind="ExternalInput").ap()
    w1d = nc.dram_tensor("w1d", [DM, DF], F32, kind="ExternalInput").ap()
    b1d = nc.dram_tensor("b1d", [128, 32], F32, kind="ExternalInput").ap()
    w2d = nc.dram_tensor("w2d", [DF, DM], F32, kind="ExternalInput").ap()
    b2d = nc.dram_tensor("b2d", [128, 8], F32, kind="ExternalInput").ap()
    onesd = nc.dram_tensor("onesd", [1, 128], F32, kind="ExternalInput").ap()
    accout = nc.dram_tensor("accout", [DM, TPC], F32, kind="ExternalOutput").ap()
    es = contextlib.ExitStack()
    with es:
        w1b = _sb(nc, es, "w1b", (128, 8 * DF), BF16)
        w2b = _sb(nc, es, "w2b", (128, 32 * DM), BF16)
        stage = _sb(nc, es, "stage", (128, 2048), F32)
        xnb = _sb(nc, es, "xnb", (128, 8 * 512), BF16)
        hT = _sb(nc, es, "hT", (128, 32 * 512), BF16)
        b1c = _sb(nc, es, "b1c", (128, 32), F32)
        b2c = _sb(nc, es, "b2c", (128, 8), F32)
        ctrow = _sb(nc, es, "ctrow", (1, TPC), F32)
        ones1 = _sb(nc, es, "ones1", (1, 128), F32)
        cbc = _sb(nc, es, "cbc", (128, 512), F32)
        acct = _sb(nc, es, "acct", (128, 512), F32)
        scry = _sb(nc, es, "scry", (128, 512), F32)
        ph = _ps(nc, es, "ph", (128, 512))
        py0 = _ps(nc, es, "py0", (128, 512))
        py1 = _ps(nc, es, "py1", (128, 512))
        pc = _ps(nc, es, "pc", (128, 512))
        C = Ctx(nc, es)
        block = es.enter_context(nc.Block())

        C.dma("sync", [], ["b1c"], lambda e: e.dma_start(out=b1c[:], in_=b1d))
        C.dma("sync", [], ["b2c"], lambda e: e.dma_start(out=b2c[:], in_=b2d))
        C.dma("sync", [], ["ctrow"], lambda e: e.dma_start(out=ctrow[:], in_=ctd))
        C.dma("sync", [], ["ones1"], lambda e: e.dma_start(out=ones1[:], in_=onesd))
        for ch in range(16):
            C.dma("sync", [], ["stage"],
                  lambda e, ch=ch: e.dma_start(
                      out=stage[:].rearrange("p (kc m) -> p kc m", m=256),
                      in_=w1d[:, ch * 256:(ch + 1) * 256].rearrange("(kc p) m -> p kc m", p=128)))
            C.op("gpsimd", ["stage"], ["w1b"],
                 lambda e, ch=ch: e.tensor_copy(
                     w1b[:].rearrange("p (kc f) -> p kc f", f=DF)[:, :, ch * 256:(ch + 1) * 256],
                     stage[:].rearrange("p (kc m) -> p kc m", m=256)))
        for ch in range(16):
            C.dma("sync", [], ["stage"],
                  lambda e, ch=ch: e.dma_start(
                      out=stage[:].rearrange("p (fc m) -> p fc m", m=DM),
                      in_=w2d[ch * 256:(ch + 1) * 256, :].rearrange("(fc p) m -> p fc m", p=128)))
            C.op("gpsimd", ["stage"], ["w2b"],
                 lambda e, ch=ch: e.tensor_copy(
                     w2b[:, ch * 2 * DM:(ch + 1) * 2 * DM], stage[:]))
        for tt in range(4):
            C.dma("sync", [], ["xnb"],
                  lambda e, tt=tt: e.dma_start(
                      out=xnb[:].rearrange("p (kc t) -> p kc t", t=512),
                      in_=xnb_d[:, tt * 512:(tt + 1) * 512].rearrange("(kc p) t -> p kc t",
                                                                     p=128)))
            C.op("tensor", ["ctrow", "ones1"], ["pc"],
                 lambda e, tt=tt: e.matmul(pc[:], ones1[:],
                                           ctrow[:, tt * 512:(tt + 1) * 512],
                                           start=True, stop=True, skip_group_check=True))
            C.op("vector", ["pc"], ["cbc"], lambda e: e.tensor_copy(cbc[:], pc[:]))
            for fc in range(32):
                for kc in range(8):
                    C.op("tensor", ["w1b", "xnb"], ["ph"],
                         lambda e, fc=fc, kc=kc: e.matmul(
                             ph[:], w1b[:, kc * DF + fc * 128: kc * DF + (fc + 1) * 128],
                             xnb[:, kc * 512:(kc + 1) * 512],
                             start=(kc == 0), stop=(kc == 7), skip_group_check=True),
                         accum=(kc != 0))
                C.op("scalar", ["ph", "b1c"], ["hT"],
                     lambda e, fc=fc: e.activation(hT[:, fc * 512:(fc + 1) * 512], ph[:],
                                                   AF.Gelu, bias=b1c[:, fc:fc + 1]))
            for m in range(8):
                pb = py0 if m % 2 == 0 else py1
                pk = f"py{m % 2}"
                for fc in range(32):
                    C.op("tensor", ["w2b", "hT"], [pk],
                         lambda e, m=m, fc=fc, pb=pb: e.matmul(
                             pb[:], w2b[:, fc * DM + m * 128: fc * DM + (m + 1) * 128],
                             hT[:, fc * 512:(fc + 1) * 512],
                             start=(fc == 0), stop=(fc == 31), skip_group_check=True),
                         accum=(fc != 0))
                C.dma("sync", [], ["acct"],
                      lambda e, m=m, tt=tt: e.dma_start(
                          out=acct[:], in_=accin[m * 128:(m + 1) * 128, tt * 512:(tt + 1) * 512]))
                C.op("vector", [pk, "b2c"], ["scry"],
                     lambda e, m=m, pb=pb: e.tensor_scalar(
                         scry[:], pb[:], b2c[:, m:m + 1], None, op0=ALU.add))
                C.op("vector", ["scry", "cbc"], ["scry"],
                     lambda e: e.tensor_tensor(scry[:], scry[:], cbc[:], op=ALU.mult))
                C.op("vector", ["scry", "acct"], ["scry"],
                     lambda e: e.tensor_tensor(scry[:], scry[:], acct[:], op=ALU.add))
                C.dma("gpsimd", ["scry"], ["accout"],
                      lambda e, m=m, tt=tt: e.dma_start(
                          out=accout[m * 128:(m + 1) * 128, tt * 512:(tt + 1) * 512],
                          in_=scry[:]))
        C.sync_on("gpsimd", ["accout"])
        C.emit(block)
    return nc



GEXP = 6


def build_moe():
    nc = bass.Bass("TRN2", target_bir_lowering=False)
    xnb_d = nc.dram_tensor("xnb_d", [DM, TPC], BF16, kind="ExternalInput").ap()
    ctg = nc.dram_tensor("ctg", [GEXP, TPC], F32, kind="ExternalInput").ap()
    accin = nc.dram_tensor("accin", [DM, TPC], F32, kind="ExternalInput").ap()
    w1g = nc.dram_tensor("w1g", [GEXP, DM, DF], F32, kind="ExternalInput").ap()
    b1g = nc.dram_tensor("b1g", [128, GEXP * 32], F32, kind="ExternalInput").ap()
    w2g = nc.dram_tensor("w2g", [GEXP, DF, DM], F32, kind="ExternalInput").ap()
    b2g = nc.dram_tensor("b2g", [128, GEXP * 8], F32, kind="ExternalInput").ap()
    onesd = nc.dram_tensor("onesd", [1, 128], F32, kind="ExternalInput").ap()
    accout = nc.dram_tensor("accout", [DM, TPC], F32, kind="ExternalOutput").ap()
    es = contextlib.ExitStack()
    with es:
        w1b = _sb(nc, es, "w1b", (128, 8 * DF), BF16)
        w2b = _sb(nc, es, "w2b", (128, 32 * DM), BF16)
        _st = _sb(nc, es, "stage0", (128, 1024), F32)
        stage = [_st, _st]
        xnb = _sb(nc, es, "xnb", (128, 8 * TPC), BF16)
        hT = _sb(nc, es, "hT", (128, 32 * 512), BF16)
        b1c = _sb(nc, es, "b1c", (128, GEXP * 32), F32)
        b2c = _sb(nc, es, "b2c", (128, GEXP * 8), F32)
        ctrow = _sb(nc, es, "ctrow", (1, 512), F32)
        ones1 = _sb(nc, es, "ones1", (1, 128), F32)
        cbc = _sb(nc, es, "cbc", (128, 512), F32)
        acct = _sb(nc, es, "acct", (128, 512), F32)
        scry = _sb(nc, es, "scry", (128, 512), F32)
        ph = _ps(nc, es, "ph", (128, 512))
        py0 = _ps(nc, es, "py0", (128, 512))
        py1 = _ps(nc, es, "py1", (128, 512))
        pc = _ps(nc, es, "pc", (128, 512))
        C = Ctx(nc, es)
        block = es.enter_context(nc.Block())

        C.dma("sync", [], ["b1c"], lambda e: e.dma_start(out=b1c[:], in_=b1g))
        C.dma("sync", [], ["b2c"], lambda e: e.dma_start(out=b2c[:], in_=b2g))
        C.dma("sync", [], ["ones1"], lambda e: e.dma_start(out=ones1[:], in_=onesd))
        C.dma("sync", [], ["xnb"],
              lambda e: e.dma_start(out=xnb[:].rearrange("p (kc t) -> p kc t", t=TPC),
                                    in_=xnb_d.rearrange("(kc p) t -> p kc t", p=128)))
        for ex in range(GEXP):
            for ch in range(32):
                sg = stage[ch % 2]
                sk = "stage0"
                C.dma("sync", [], [sk],
                      lambda e, ex=ex, ch=ch, sg=sg: e.dma_start(
                          out=sg[:].rearrange("p (kc m) -> p kc m", m=128),
                          in_=w1g[ex, :, ch * 128:(ch + 1) * 128]
                          .rearrange("(kc p) m -> p kc m", p=128)))
                C.op("gpsimd", [sk], ["w1b"],
                     lambda e, ch=ch, sg=sg: e.tensor_copy(
                         w1b[:].rearrange("p (kc f) -> p kc f", f=DF)
                         [:, :, ch * 128:(ch + 1) * 128],
                         sg[:].rearrange("p (kc m) -> p kc m", m=128)))
            for ch in range(32):
                sg = stage[ch % 2]
                sk = "stage0"
                C.dma("sync", [], [sk],
                      lambda e, ex=ex, ch=ch, sg=sg: e.dma_start(
                          out=sg[:],
                          in_=w2g[ex, ch * 128:(ch + 1) * 128, :]
                          .rearrange("(o p) m -> p (o m)", p=128)))
                C.op("gpsimd", [sk], ["w2b"],
                     lambda e, ch=ch, sg=sg: e.tensor_copy(
                         w2b[:, ch * DM:(ch + 1) * DM], sg[:]))
            for tt in range(4):
                C.dma("scalar", [], ["ctrow"],
                      lambda e, ex=ex, tt=tt: e.dma_start(
                          out=ctrow[:], in_=ctg[ex:ex + 1, tt * 512:(tt + 1) * 512]))
                C.op("tensor", ["ctrow", "ones1"], ["pc"],
                     lambda e: e.matmul(pc[:], ones1[:], ctrow[:],
                                        start=True, stop=True, skip_group_check=True))
                C.op("vector", ["pc"], ["cbc"], lambda e: e.tensor_copy(cbc[:], pc[:]))
                for fc in range(32):
                    for kc in range(8):
                        C.op("tensor", ["w1b", "xnb"], ["ph"],
                             lambda e, tt=tt, fc=fc, kc=kc: e.matmul(
                                 ph[:], w1b[:, kc * DF + fc * 128: kc * DF + (fc + 1) * 128],
                                 xnb[:, kc * TPC + tt * 512: kc * TPC + (tt + 1) * 512],
                                 start=(kc == 0), stop=(kc == 7), skip_group_check=True),
                             accum=(kc != 0))
                    C.op("scalar", ["ph", "b1c"], ["hT"],
                         lambda e, ex=ex, fc=fc: e.activation(
                             hT[:, fc * 512:(fc + 1) * 512], ph[:], AF.Gelu,
                             bias=b1c[:, ex * 32 + fc: ex * 32 + fc + 1]))
                for m in range(8):
                    pb = py0 if m % 2 == 0 else py1
                    pk = f"py{m % 2}"
                    for fc in range(32):
                        C.op("tensor", ["w2b", "hT"], [pk],
                             lambda e, m=m, fc=fc, pb=pb: e.matmul(
                                 pb[:], w2b[:, fc * DM + m * 128: fc * DM + (m + 1) * 128],
                                 hT[:, fc * 512:(fc + 1) * 512],
                                 start=(fc == 0), stop=(fc == 31), skip_group_check=True),
                             accum=(fc != 0))
                    ak = f"acc_{m}_{tt}"
                    asrc = accin if ex == 0 else accout
                    C.dma("scalar", [ak], ["acct"],
                          lambda e, m=m, tt=tt, asrc=asrc: e.dma_start(
                              out=acct[:],
                              in_=asrc[m * 128:(m + 1) * 128, tt * 512:(tt + 1) * 512]))
                    C.op("vector", [pk, "b2c"], ["scry"],
                         lambda e, ex=ex, m=m, pb=pb: e.tensor_scalar(
                             scry[:], pb[:], b2c[:, ex * 8 + m: ex * 8 + m + 1],
                             None, op0=ALU.add))
                    C.op("vector", ["scry", "cbc"], ["scry"],
                         lambda e: e.tensor_tensor(scry[:], scry[:], cbc[:], op=ALU.mult))
                    C.op("vector", ["scry", "acct"], ["scry"],
                         lambda e: e.tensor_tensor(scry[:], scry[:], acct[:], op=ALU.add))
                    C.dma("gpsimd", ["scry"], [ak],
                          lambda e, m=m, tt=tt: e.dma_start(
                              out=accout[m * 128:(m + 1) * 128, tt * 512:(tt + 1) * 512],
                              in_=scry[:]))
        C.sync_on("gpsimd", [f"acc_{m}_{tt}" for m in range(8) for tt in range(4)])
        C.emit(block)
    return nc


def build_p9():
    nc = bass.Bass("TRN2", target_bir_lowering=False)
    accd = nc.dram_tensor("accd", [DM, TPC], F32, kind="ExternalInput").ap()
    eyed = nc.dram_tensor("eyed", [128, 128], F32, kind="ExternalInput").ap()
    outd = nc.dram_tensor("outd", [TPC, DM], F32, kind="ExternalOutput").ap()
    es = contextlib.ExitStack()
    with es:
        accs = _sb(nc, es, "accs", (128, 8 * TPC), F32)
        eye = _sb(nc, es, "eye", (128, 128), F32)
        otile = _sb(nc, es, "otile", (128, DM), F32)
        ptr = _ps(nc, es, "ptr", (128, 128))
        C = Ctx(nc, es)
        block = es.enter_context(nc.Block())
        C.dma("sync", [], ["eye"], lambda e: e.dma_start(out=eye[:], in_=eyed))
        C.dma("sync", [], ["accs"],
              lambda e: e.dma_start(out=accs[:].rearrange("p (m t) -> p m t", t=TPC),
                                    in_=accd.rearrange("(m p) t -> p m t", p=128)))
        for tc in range(16):
            for m in range(8):
                C.op("tensor", ["accs", "eye"], ["ptr"],
                     lambda e, tc=tc, m=m: e.transpose(
                         ptr[:], accs[:, m * TPC + tc * 128: m * TPC + (tc + 1) * 128], eye[:]))
                C.op("vector", ["ptr"], ["otile"],
                     lambda e, m=m: e.tensor_copy(otile[:, m * 128:(m + 1) * 128], ptr[:]))
            C.dma("gpsimd", ["otile"], ["outd"],
                  lambda e, tc=tc: e.dma_start(out=outd[tc * 128:(tc + 1) * 128, :], in_=otile[:]))
        C.sync_on("gpsimd", ["outd"])
        C.emit(block)
    return nc


def _run(nc, in_maps):
    return run_bass_kernel_spmd(nc, in_maps, core_ids=list(range(NCORES))).results


def kernel(**inputs):
    f32 = np.float32
    cont = np.ascontiguousarray
    eye = np.eye(128, dtype=f32)
    eps = np.full((128, 1), 1e-5, f32)
    one = np.full((128, 1), 1.0, f32)
    ones1 = np.ones((1, 128), f32)

    x_full = np.asarray(inputs["L0_attn_x"], f32).reshape(NB * T, DM)

    inv = 10000.0 ** (-(np.arange(32, dtype=np.float64) / 32.0))
    ang = np.arange(T, dtype=np.float64)[:, None] * inv[None, :]
    cosT = np.cos(ang).astype(f32)
    sinT = np.sin(ang).astype(f32)

    r = np.arange(DM)
    qcol = ((r % 512) // 32) * 64 + (r // 512) * 32 + (r % 32)
    wq_r = cont(np.asarray(inputs["L0_attn_Wq"], f32)[:, qcol])
    wk_r = cont(np.asarray(inputs["L0_attn_Wk"], f32)[:, qcol])
    wv = cont(np.asarray(inputs["L0_attn_Wv"], f32))
    wo = cont(np.asarray(inputs["L0_attn_Wout"], f32))

    nc1 = build_p1()
    maps = []
    for c in range(NCORES):
        b, half = c // 2, c % 2
        t0 = half * TPC
        cosd = cont(np.broadcast_to(cosT[t0:t0 + TPC, :].T, (32, TPC))[np.arange(128) % 32])
        sind = cont(np.broadcast_to(sinT[t0:t0 + TPC, :].T, (32, TPC))[np.arange(128) % 32])
        maps.append({"x": cont(x_full[b * T + t0: b * T + t0 + TPC]),
                     "wq": wq_r, "wk": wk_r, "wv": wv,
                     "cosd": cosd, "sind": sind, "eyed": eye, "epsd": eps})
    r1 = _run(nc1, maps)

    W1 = np.asarray(inputs["L0_ffn_W1"], f32)
    W2 = np.asarray(inputs["L0_ffn_W2"], f32)
    w1f = cont(W1.transpose(2, 3, 0, 1).reshape(DM, 32, 128).reshape(DM, 4096))
    w2f = cont(W2.transpose(3, 0, 1, 2).reshape(32, 128, DM).reshape(4096, DM))
    wcf = cont(np.asarray(inputs["L0_ffn_Wc"], f32).reshape(DM, 128))
    bcb = cont(np.asarray(inputs["L0_ffn_bc"], f32).reshape(128, 1))
    b1f = cont(np.asarray(inputs["L0_ffn_b1"], f32).reshape(128, 32))
    b2f = cont(np.asarray(inputs["L0_ffn_b2"], f32).reshape(128, DM))
    tie_row = np.linspace(0.0, 1e-6, 32, dtype=f32)[np.arange(512) % 32]
    tied = cont(np.broadcast_to(tie_row[None, :], (128, 512)))
    nc2 = build_p2()
    maps = [{"xnt": r1[c]["xnt"], "w1f": w1f, "w2f": w2f, "wcf": wcf, "bcb": bcb,
             "tied": tied, "b1fd": b1f, "b2fd": b2f} for c in range(NCORES)]
    r2 = _run(nc2, maps)

    QT = [np.concatenate([r1[2 * b]["qt"], r1[2 * b + 1]["qt"]], axis=1) for b in range(NB)]
    KT = [np.concatenate([r1[2 * b]["kt"], r1[2 * b + 1]["kt"]], axis=1) for b in range(NB)]
    VF = [np.concatenate([r1[2 * b]["vout"], r1[2 * b + 1]["vout"]], axis=0) for b in range(NB)]
    nc3 = build_p3()
    maps = []
    for c in range(NCORES):
        b = c // 2
        heads = range(0, 8) if c % 2 == 0 else range(8, 16)
        rows = np.concatenate([np.r_[h * 32:(h + 1) * 32, 512 + h * 32: 512 + (h + 1) * 32]
                               for h in heads])
        cols = np.concatenate([np.arange(h * 64, (h + 1) * 64) for h in heads])
        maps.append({"qin": cont(QT[b][rows]), "kin": cont(KT[b][rows]),
                     "vin": cont(VF[b][:, cols]), "eyed": eye})
    r3 = _run(nc3, maps)

    AO = [np.concatenate([r3[2 * b]["aout"], r3[2 * b + 1]["aout"]], axis=0) for b in range(NB)]
    nc4 = build_p4()
    maps = []
    for c in range(NCORES):
        b, half = c // 2, c % 2
        t0 = half * TPC
        maps.append({"x": cont(x_full[b * T + t0: b * T + t0 + TPC]),
                     "f0": r2[c]["f0"], "ao": cont(AO[b][:, t0:t0 + TPC]), "wo": wo})
    r4 = _run(nc4, maps)

    X1 = [np.concatenate([r4[2 * b]["x1"], r4[2 * b + 1]["x1"]], axis=0) for b in range(NB)]
    inw = cont(np.asarray(inputs["L1_attn_in_w"], f32))
    inb = cont(np.asarray(inputs["L1_attn_in_b"], f32).reshape(32, 128).T)
    cwd = cont(np.asarray(inputs["L1_attn_conv_w"], f32)[:, 0, :]
               .reshape(4, 16, 128).transpose(2, 1, 0).reshape(128, 64))
    cbd = cont(np.asarray(inputs["L1_attn_conv_b"], f32).reshape(16, 128).T)
    bwd = cont(np.asarray(inputs["L1_attn_B_w"], f32))
    bbd = cont(np.asarray(inputs["L1_attn_B_b"], f32).reshape(DS, 1))
    cwd2 = cont(np.asarray(inputs["L1_attn_C_w"], f32))
    cbd2 = cont(np.asarray(inputs["L1_attn_C_b"], f32).reshape(DS, 1))
    dtw = cont(np.asarray(inputs["L1_attn_dt_w"], f32))
    dtb = cont(np.asarray(inputs["L1_attn_dt_b"], f32).reshape(16, 128).T)
    nc5 = build_p5()
    maps = []
    for c in range(NCORES):
        b, half = c // 2, c % 2
        t0 = half * TPC
        pad = np.zeros((2176, DM), f32)
        lo = t0 - 1
        hi = min(T, t0 + TPC + 2)
        src0 = max(0, lo)
        pad[src0 - lo: hi - lo] = X1[b][src0:hi]
        maps.append({"x1p": pad, "inw": inw, "inb": inb, "cwd": cwd, "cbd": cbd,
                     "bwd": bwd, "bbd": bbd, "cwd2": cwd2, "cbd2": cbd2,
                     "dtw": dtw, "dtb": dtb, "eyed": eye, "epsd": eps, "oned": one})
    r5 = _run(nc5, maps)

    XC = [np.concatenate([r5[2 * b]["xco"], r5[2 * b + 1]["xco"]], axis=1) for b in range(NB)]
    DT_ = [np.concatenate([r5[2 * b]["dto"], r5[2 * b + 1]["dto"]], axis=1) for b in range(NB)]
    SG = [np.concatenate([r5[2 * b]["sgo"], r5[2 * b + 1]["sgo"]], axis=1) for b in range(NB)]
    BM = [np.concatenate([r5[2 * b]["bmo"], r5[2 * b + 1]["bmo"]], axis=1) for b in range(NB)]
    CM = [np.concatenate([r5[2 * b]["cmo"], r5[2 * b + 1]["cmo"]], axis=1) for b in range(NB)]
    A = np.asarray(inputs["L1_attn_A"], f32)
    Dp = np.asarray(inputs["L1_attn_Dp"], f32)
    ow = cont(np.asarray(inputs["L1_attn_out_w"], f32))
    nc6 = build_p6()
    maps = []
    for c in range(NCORES):
        b, chh = c // 2, c % 2
        c0 = chh * 1024
        acol = cont(A[c0:c0 + 1024].reshape(8, 128, DS).transpose(1, 0, 2).reshape(128, 128))
        dpc = cont(Dp[c0:c0 + 1024].reshape(8, 128).T)
        maps.append({"dtd": cont(DT_[b][c0:c0 + 1024]), "xcd": cont(XC[b][c0:c0 + 1024]),
                     "sgd": cont(SG[b][c0:c0 + 1024]), "bmd": cont(BM[b]), "cmd": cont(CM[b]),
                     "acold": acol, "dpd": dpc, "owd": cont(ow[c0:c0 + 1024]),
                     "onesd": ones1})
    r6 = _run(nc6, maps)

    gw = cont(np.asarray(inputs["L1_ffn_gate_w"], f32))
    gb = cont(np.asarray(inputs["L1_ffn_gate_b"], f32).reshape(1, NE))
    outb = cont(np.asarray(inputs["L1_attn_out_b"], f32).reshape(8, 128).T)
    nc7 = build_p7()
    maps = []
    for c in range(NCORES):
        b, half = c // 2, c % 2
        t0 = half * TPC
        maps.append({"x1": r4[c]["x1"], "xn2": r5[c]["xn2o"],
                     "opa": cont(r6[2 * b]["opo"][:, t0:t0 + TPC]),
                     "opb": cont(r6[2 * b + 1]["opo"][:, t0:t0 + TPC]),
                     "outbd": outb, "gwd": gw, "gbd": gb, "onesd": ones1, "eyed": eye})
    r7 = _run(nc7, maps)

    w1e = np.asarray(inputs["L1_ffn_w1"], f32)
    b1e = np.asarray(inputs["L1_ffn_b1"], f32)
    w2e = np.asarray(inputs["L1_ffn_w2"], f32)
    b2e = np.asarray(inputs["L1_ffn_b2"], f32)
    nc8 = build_moe()
    acc = [r7[c]["base"] for c in range(NCORES)]
    for g in range(NE // GEXP):
        e0 = g * GEXP
        w1d = cont(w1e[e0:e0 + GEXP])
        b1d = cont(np.concatenate([b1e[e0 + i].reshape(32, 128).T for i in range(GEXP)], axis=1))
        w2d = cont(w2e[e0:e0 + GEXP])
        b2d = cont(np.concatenate([b2e[e0 + i].reshape(8, 128).T for i in range(GEXP)], axis=1))
        maps = [{"xnb_d": r7[c]["xnbo"], "ctg": cont(r7[c]["cto"][e0:e0 + GEXP]),
                 "accin": acc[c], "w1g": w1d, "b1g": b1d, "w2g": w2d, "b2g": b2d,
                 "onesd": ones1} for c in range(NCORES)]
        r8 = _run(nc8, maps)
        acc = [r8[c]["accout"] for c in range(NCORES)]

    nc9 = build_p9()
    maps = [{"accd": acc[c], "eyed": eye} for c in range(NCORES)]
    r9 = _run(nc9, maps)

    out = np.empty((NB, T, DM), f32)
    for c in range(NCORES):
        b, half = c // 2, c % 2
        t0 = half * TPC
        out[b, t0:t0 + TPC] = r9[c]["outd"]
    return out


# version 03
# speedup vs baseline: 1.1419x; 1.1128x over previous
import contextlib
import numpy as np
import ml_dtypes
import concourse.bass as bass
import concourse.mybir as mybir
from concourse.bass_utils import run_bass_kernel_spmd

F32 = mybir.dt.float32
BF16 = mybir.dt.bfloat16
AF = mybir.ActivationFunctionType
ALU = mybir.AluOpType
AX = mybir.AxisListType
NPBF = ml_dtypes.bfloat16

NB, T, DM = 4, 4096, 1024
H, DH, G = 16, 64, 409
TPC = 2048
DI, DS = 2048, 16
NE, DF = 24, 4096
NCORES = 8


class Ctx:
    QUEUES = ("sync", "vector", "scalar", "tensor", "gpsimd")

    def __init__(self, nc, es):
        self.nc = nc
        self.sem = {q: es.enter_context(nc.semaphore(f"prog_{q}")) for q in self.QUEUES}
        self.NDMA = 10
        self.dsem, self.duses = {}, {}
        for q in ("sync", "scalar", "gpsimd", "vector"):
            self.dsem[q] = [es.enter_context(nc.semaphore(f"dma_{q}{i}")) for i in range(self.NDMA)]
            self.duses[q] = [0] * self.NDMA
        self.drr = {q: 0 for q in self.dsem}
        self.cnt = {q: 0 for q in self.QUEUES}
        self.last_write = {}
        self.readers = {}
        self.watermark = {q: {} for q in self.QUEUES}
        self.queue = {q: [] for q in self.QUEUES}

    def _wait(self, eng, kind, q, c):
        wm = self.watermark[eng]
        k = (kind, q)
        if wm.get(k, 0) >= c:
            return
        sem = self.sem[q] if kind == "prog" else self.dsem[q[0]][q[1]]
        self.queue[eng].append(("wait", sem, c))
        wm[k] = c

    def _raw(self, eng, keys):
        for key in keys:
            lw = self.last_write.get(key)
            if lw is not None:
                self._wait(eng, lw[0], lw[1], lw[2])

    def _war_waw(self, eng, keys):
        for key in keys:
            lw = self.last_write.get(key)
            if lw is not None:
                self._wait(eng, lw[0], lw[1], lw[2])
            for (kind, q), c in self.readers.get(key, {}).items():
                self._wait(eng, kind, q, c)

    def _record(self, kind, q, c, reads, writes):
        for key in writes:
            self.last_write[key] = (kind, q, c)
            self.readers[key] = {}
        for key in reads:
            d = self.readers.setdefault(key, {})
            k = (kind, q)
            d[k] = max(d.get(k, 0), c)

    def op(self, eng, reads, writes, fn, accum=False):
        self._raw(eng, reads)
        if accum:
            for key in writes:
                lw = self.last_write.get(key)
                if lw is not None and not (lw[0] == "prog" and lw[1] == eng):
                    self._wait(eng, lw[0], lw[1], lw[2])
                for (kind, q), c in self.readers.get(key, {}).items():
                    if not (kind == "prog" and q == eng):
                        self._wait(eng, kind, q, c)
        else:
            self._war_waw(eng, writes)
        self.cnt[eng] += 1
        self.queue[eng].append(("op", fn, self.sem[eng], 1))
        self._record("prog", eng, self.cnt[eng], reads, writes)

    def dma(self, q, reads, writes, fn):
        self._raw(q, reads)
        self._war_waw(q, writes)
        slot = self.drr[q]
        self.drr[q] = (slot + 1) % self.NDMA
        prev = self.duses[q][slot]
        if prev > 0:
            self._wait(q, "dma", (q, slot), 16 * prev)
        self.duses[q][slot] = prev + 1
        self.queue[q].append(("op", fn, self.dsem[q][slot], 16))
        self._record("dma", (q, slot), 16 * (prev + 1), reads, writes)

    def sync_on(self, eng, keys):
        self._raw(eng, keys)

    def emit(self, block):
        def make_body(qname):
            def body(engine):
                for item in self.queue[qname]:
                    if item[0] == "wait":
                        engine.wait_ge(item[1], item[2])
                    else:
                        _, fn, sem, inc = item
                        fn(engine).then_inc(sem, inc)
            return body
        for qname in self.QUEUES:
            if self.queue[qname]:
                getattr(block, qname)(make_body(qname))


def _sb(nc, es, name, shape, dt):
    return es.enter_context(nc.sbuf_tensor(name, list(shape), dt))


def _ps(nc, es, name, shape, dt=F32):
    return es.enter_context(nc.psum_tensor(name, list(shape), dt))


def _ln_tile(C, xin_ap, xin_key, xn, mean, var, scr, epscol):
    C.op("vector", [xin_key], ["mean"], lambda e: e.reduce_sum(mean[:], xin_ap, axis=AX.X))
    C.op("vector", ["mean"], ["mean"], lambda e: e.tensor_scalar_mul(mean[:], mean[:], 1.0 / DM))
    C.op("vector", [xin_key, "mean"], ["scr"],
         lambda e: e.tensor_scalar(scr[:], xin_ap, mean[:], None, op0=ALU.subtract))
    C.op("scalar", ["scr"], ["xn"], lambda e: e.activation(xn[:], scr[:], AF.Square))
    C.op("vector", ["xn"], ["var"], lambda e: e.reduce_sum(var[:], xn[:], axis=AX.X))
    C.op("vector", ["var"], ["var"], lambda e: e.tensor_scalar_mul(var[:], var[:], 1.0 / DM))
    C.op("scalar", ["var", "epscol"], ["var"],
         lambda e: e.activation(var[:], var[:], AF.Sqrt, bias=epscol[:]))
    C.op("vector", ["var"], ["var"], lambda e: e.reciprocal(var[:], var[:]))
    C.op("vector", ["scr", "var"], ["xn"],
         lambda e: e.tensor_scalar(xn[:], scr[:], var[:], None, op0=ALU.mult))


def build_p1():
    nc = bass.Bass("TRN2", target_bir_lowering=False)
    x = nc.dram_tensor("x", [TPC, DM], F32, kind="ExternalInput").ap()
    wq = nc.dram_tensor("wq", [DM, DM], F32, kind="ExternalInput").ap()
    wk = nc.dram_tensor("wk", [DM, DM], F32, kind="ExternalInput").ap()
    wv = nc.dram_tensor("wv", [DM, DM], F32, kind="ExternalInput").ap()
    cosd = nc.dram_tensor("cosd", [128, TPC], F32, kind="ExternalInput").ap()
    sind = nc.dram_tensor("sind", [128, TPC], F32, kind="ExternalInput").ap()
    eyed = nc.dram_tensor("eyed", [128, 128], F32, kind="ExternalInput").ap()
    epsd = nc.dram_tensor("epsd", [128, 1], F32, kind="ExternalInput").ap()
    qt = nc.dram_tensor("qt", [DM, TPC], F32, kind="ExternalOutput").ap()
    kt = nc.dram_tensor("kt", [DM, TPC], F32, kind="ExternalOutput").ap()
    vout = nc.dram_tensor("vout", [TPC, DM], BF16, kind="ExternalOutput").ap()
    xnt = nc.dram_tensor("xnt", [DM, TPC], BF16, kind="ExternalOutput").ap()
    es = contextlib.ExitStack()
    with es:
        xtile = _sb(nc, es, "xtile", (128, DM), F32)
        xn = _sb(nc, es, "xn", (128, DM), F32)
        scr = _sb(nc, es, "scr", (128, DM), F32)
        mean = _sb(nc, es, "mean", (128, 1), F32)
        var = _sb(nc, es, "var", (128, 1), F32)
        epscol = _sb(nc, es, "epscol", (128, 1), F32)
        eye = _sb(nc, es, "eye", (128, 128), F32)
        cos = _sb(nc, es, "cos", (128, TPC), F32)
        sin = _sb(nc, es, "sin", (128, TPC), F32)
        xnt_bf = _sb(nc, es, "xnt_bf", (128, 8 * TPC), BF16)
        wstage = _sb(nc, es, "wstage", (128, 4 * DM), F32)
        wbf = _sb(nc, es, "wbf", (128, 8 * DM), BF16)
        qraw = _sb(nc, es, "qraw", (128, 8 * TPC), F32)
        t1 = _sb(nc, es, "t1", (128, TPC), F32)
        t2 = _sb(nc, es, "t2", (128, TPC), F32)
        vsb = _sb(nc, es, "vsb", (128, DM), BF16)
        ptr = _ps(nc, es, "ptr", (128, 128))
        pq0 = _ps(nc, es, "pq0", (128, 512))
        pq1 = _ps(nc, es, "pq1", (128, 512))
        C = Ctx(nc, es)
        block = es.enter_context(nc.Block())

        C.dma("sync", [], ["epscol"], lambda e: e.dma_start(out=epscol[:], in_=epsd))
        C.dma("sync", [], ["eye"], lambda e: e.dma_start(out=eye[:], in_=eyed))
        C.dma("sync", [], ["cos"], lambda e: e.dma_start(out=cos[:], in_=cosd))
        C.dma("sync", [], ["sin"], lambda e: e.dma_start(out=sin[:], in_=sind))

        for i in range(16):
            C.dma("sync", [], ["xtile"],
                  lambda e, i=i: e.dma_start(out=xtile[:], in_=x[i * 128:(i + 1) * 128, :]))
            _ln_tile(C, xtile[:], "xtile", xn, mean, var, scr, epscol)
            for kc in range(8):
                C.op("tensor", ["xn", "eye"], ["ptr"],
                     lambda e, kc=kc: e.transpose(ptr[:], xn[:, kc * 128:(kc + 1) * 128], eye[:]))
                C.op("vector", ["ptr"], ["xnt_bf"],
                     lambda e, kc=kc, i=i: e.tensor_copy(
                         xnt_bf[:, kc * TPC + i * 128: kc * TPC + (i + 1) * 128], ptr[:]))
        C.dma("gpsimd", ["xnt_bf"], ["xnt"],
              lambda e: e.dma_start(out=xnt.rearrange("(kc p) t -> p kc t", p=128),
                                    in_=xnt_bf[:].rearrange("p (kc t) -> p kc t", t=TPC)))

        def load_weight_bf(wsrc):
            for hf in range(2):
                C.dma("sync", [], ["wstage"],
                      lambda e, wsrc=wsrc, hf=hf: e.dma_start(
                          out=wstage[:].rearrange("p (kc m) -> p kc m", m=DM),
                          in_=wsrc.rearrange("(kc p) m -> p kc m", p=128)[:, hf * 4:(hf + 1) * 4, :]))
                C.op("gpsimd", ["wstage"], ["wbf"],
                     lambda e, hf=hf: e.tensor_copy(
                         wbf[:, hf * 4 * DM:(hf + 1) * 4 * DM], wstage[:]))

        for wsrc, outap, okey in ((wq, qt, "qt"), (wk, kt, "kt")):
            load_weight_bf(wsrc)
            for m in range(8):
                for tt in range(4):
                    pb = pq0 if tt % 2 == 0 else pq1
                    pk = f"pq{tt % 2}"
                    for kc in range(8):
                        C.op("tensor", ["wbf", "xnt_bf"], [pk],
                             lambda e, m=m, tt=tt, kc=kc, pb=pb: e.matmul(
                                 pb[:], wbf[:, kc * DM + m * 128: kc * DM + (m + 1) * 128],
                                 xnt_bf[:, kc * TPC + tt * 512: kc * TPC + (tt + 1) * 512],
                                 start=(kc == 0), stop=(kc == 7), skip_group_check=True),
                             accum=(kc != 0))
                    C.op("vector", [pk], ["qraw"],
                         lambda e, m=m, tt=tt, pb=pb: e.tensor_copy(
                             qraw[:, m * TPC + tt * 512: m * TPC + (tt + 1) * 512], pb[:]))
            for m in range(4):
                a = qraw[:, m * TPC:(m + 1) * TPC]
                b = qraw[:, (m + 4) * TPC:(m + 5) * TPC]
                C.op("vector", ["qraw", "cos"], ["t1"],
                     lambda e, a=a: e.tensor_tensor(t1[:], a, cos[:], op=ALU.mult))
                C.op("vector", ["qraw", "sin"], ["t2"],
                     lambda e, a=a: e.tensor_tensor(t2[:], a, sin[:], op=ALU.mult))
                C.op("vector", ["qraw", "sin"], ["qraw"],
                     lambda e, a=a, b=b: e.tensor_tensor(a, b, sin[:], op=ALU.mult))
                C.op("vector", ["qraw", "cos"], ["qraw"],
                     lambda e, b=b: e.tensor_tensor(b, b, cos[:], op=ALU.mult))
                C.op("vector", ["qraw", "t2"], ["qraw"],
                     lambda e, b=b: e.tensor_tensor(b, t2[:], b, op=ALU.add))
                C.op("vector", ["qraw", "t1"], ["qraw"],
                     lambda e, a=a: e.tensor_tensor(a, t1[:], a, op=ALU.subtract))
            C.dma("gpsimd", ["qraw"], [okey],
                  lambda e, outap=outap: e.dma_start(
                      out=outap.rearrange("(m p) t -> p m t", p=128),
                      in_=qraw[:].rearrange("p (m t) -> p m t", t=TPC)))

        load_weight_bf(wv)
        for i in range(16):
            for nn in range(2):
                pb = pq0 if nn == 0 else pq1
                pk = f"pq{nn}"
                for kc in range(8):
                    C.op("tensor", ["wbf", "xnt_bf"], [pk],
                         lambda e, i=i, nn=nn, kc=kc, pb=pb: e.matmul(
                             pb[:], xnt_bf[:, kc * TPC + i * 128: kc * TPC + (i + 1) * 128],
                             wbf[:, kc * DM + nn * 512: kc * DM + (nn + 1) * 512],
                             start=(kc == 0), stop=(kc == 7), skip_group_check=True),
                         accum=(kc != 0))
                C.op("vector", [pk], ["vsb"],
                     lambda e, nn=nn, pb=pb: e.tensor_copy(vsb[:, nn * 512:(nn + 1) * 512], pb[:]))
            C.dma("gpsimd", ["vsb"], ["vout"],
                  lambda e, i=i: e.dma_start(out=vout[i * 128:(i + 1) * 128, :], in_=vsb[:]))
        C.sync_on("gpsimd", ["xnt", "qt", "kt", "vout"])
        C.emit(block)
    return nc


def build_p2():
    nc = bass.Bass("TRN2", target_bir_lowering=False)
    xnt = nc.dram_tensor("xnt", [DM, TPC], BF16, kind="ExternalInput").ap()
    w1f = nc.dram_tensor("w1f", [DM, 4096], F32, kind="ExternalInput").ap()
    w2f = nc.dram_tensor("w2f", [4096, DM], F32, kind="ExternalInput").ap()
    wcf = nc.dram_tensor("wcf", [DM, 128], F32, kind="ExternalInput").ap()
    bcb = nc.dram_tensor("bcb", [128, 1], F32, kind="ExternalInput").ap()
    tied = nc.dram_tensor("tied", [128, 512], F32, kind="ExternalInput").ap()
    b1fd = nc.dram_tensor("b1fd", [128, 32], F32, kind="ExternalInput").ap()
    b2fd = nc.dram_tensor("b2fd", [128, DM], F32, kind="ExternalInput").ap()
    f0 = nc.dram_tensor("f0", [TPC, DM], F32, kind="ExternalOutput").ap()
    es = contextlib.ExitStack()
    with es:
        xnb = _sb(nc, es, "xnb", (128, 8 * TPC), BF16)
        w1b = _sb(nc, es, "w1b", (128, 8 * 4096), BF16)
        w2b = _sb(nc, es, "w2b", (128, 32 * DM), BF16)
        wcb = _sb(nc, es, "wcb", (128, 8 * 128), BF16)
        stage = _sb(nc, es, "stage", (128, 1024), F32)
        bcc = _sb(nc, es, "bcc", (128, 1), F32)
        tie = _sb(nc, es, "tie", (128, 512), F32)
        b1f = _sb(nc, es, "b1f", (128, 32), F32)
        b2b = _sb(nc, es, "b2b", (128, DM), BF16)
        logits = _sb(nc, es, "logits", (128, TPC), F32)
        gmax = _sb(nc, es, "gmax", (128, 64), F32)
        perm = _sb(nc, es, "perm", (128, TPC), F32)
        permb = _sb(nc, es, "permb", (128, TPC), BF16)
        mif = _sb(nc, es, "mif", (128, 128), F32)
        mib0 = _sb(nc, es, "mib0", (128, 128), BF16)
        mib1 = _sb(nc, es, "mib1", (128, 128), BF16)
        f0sb = _sb(nc, es, "f0sb", (128, DM), F32)
        pin0 = _ps(nc, es, "pin0", (128, 128))
        pin1 = _ps(nc, es, "pin1", (128, 128))
        po0 = _ps(nc, es, "po0", (128, 512))
        po1 = _ps(nc, es, "po1", (128, 512))
        C = Ctx(nc, es)
        block = es.enter_context(nc.Block())

        C.dma("sync", [], ["xnb"],
              lambda e: e.dma_start(out=xnb[:].rearrange("p (kc t) -> p kc t", t=TPC),
                                    in_=xnt.rearrange("(kc p) t -> p kc t", p=128)))
        C.dma("sync", [], ["bcc"], lambda e: e.dma_start(out=bcc[:], in_=bcb))
        C.dma("sync", [], ["tie"], lambda e: e.dma_start(out=tie[:], in_=tied))
        C.dma("sync", [], ["b1f"], lambda e: e.dma_start(out=b1f[:], in_=b1fd))
        C.dma("sync", [], ["stage"],
              lambda e: e.dma_start(out=stage[:, 0:DM], in_=b2fd))
        C.op("gpsimd", ["stage"], ["b2b"], lambda e: e.tensor_copy(b2b[:], stage[:, 0:DM]))
        C.dma("sync", [], ["stage"],
              lambda e: e.dma_start(out=stage[:].rearrange("p (kc m) -> p kc m", m=128),
                                    in_=wcf.rearrange("(kc p) m -> p kc m", p=128)))
        C.op("gpsimd", ["stage"], ["wcb"], lambda e: e.tensor_copy(wcb[:], stage[:]))
        for ch in range(32):
            C.dma("sync", [], ["stage"],
                  lambda e, ch=ch: e.dma_start(
                      out=stage[:].rearrange("p (kc m) -> p kc m", m=128),
                      in_=w1f[:, ch * 128:(ch + 1) * 128].rearrange("(kc p) m -> p kc m", p=128)))
            C.op("gpsimd", ["stage"], ["w1b"],
                 lambda e, ch=ch: e.tensor_copy(
                     w1b[:].rearrange("p (kc f) -> p kc f", f=4096)[:, :, ch * 128:(ch + 1) * 128],
                     stage[:].rearrange("p (kc m) -> p kc m", m=128)))
        for ch in range(32):
            C.dma("sync", [], ["stage"],
                  lambda e, ch=ch: e.dma_start(
                      out=stage[:],
                      in_=w2f[ch * 128:(ch + 1) * 128, :].rearrange("(o p) m -> p (o m)", p=128)))
            C.op("gpsimd", ["stage"], ["w2b"],
                 lambda e, ch=ch: e.tensor_copy(
                     w2b[:, ch * DM:(ch + 1) * DM], stage[:]))
        for tt in range(4):
            pk = "po0"
            for kc in range(8):
                C.op("tensor", ["wcb", "xnb"], [pk],
                     lambda e, tt=tt, kc=kc: e.matmul(
                         po0[:], wcb[:, kc * 128:(kc + 1) * 128],
                         xnb[:, kc * TPC + tt * 512: kc * TPC + (tt + 1) * 512],
                         start=(kc == 0), stop=(kc == 7), skip_group_check=True),
                     accum=(kc != 0))
            C.op("vector", [pk, "bcc"], ["logits"],
                 lambda e, tt=tt: e.tensor_scalar(
                     logits[:, tt * 512:(tt + 1) * 512], po0[:], bcc[:], None, op0=ALU.add))
            C.op("vector", ["logits", "tie"], ["logits"],
                 lambda e, tt=tt: e.tensor_tensor(
                     logits[:, tt * 512:(tt + 1) * 512],
                     logits[:, tt * 512:(tt + 1) * 512], tie[:], op=ALU.add))
        C.op("vector", ["logits"], ["gmax"],
             lambda e: e.tensor_reduce(gmax[:], logits[:].rearrange("p (g w) -> p g w", w=32),
                                       axis=AX.X, op=ALU.max))
        C.op("vector", ["logits", "gmax"], ["perm"],
             lambda e: e.tensor_tensor(perm[:].rearrange("p (g w) -> p g w", w=32),
                                       logits[:].rearrange("p (g w) -> p g w", w=32),
                                       gmax[:].unsqueeze(2).broadcast_to([128, 64, 32]),
                                       op=ALU.is_ge))
        C.op("gpsimd", ["perm"], ["permb"], lambda e: e.tensor_copy(permb[:], perm[:]))
        for tc in range(16):
            C.op("tensor", ["permb", "b2b"], ["po0"],
                 lambda e, tc=tc: e.matmul(po0[:], permb[:, tc * 128:(tc + 1) * 128],
                                           b2b[:, 0:512], start=True, stop=False, skip_group_check=True))
            C.op("tensor", ["permb", "b2b"], ["po1"],
                 lambda e, tc=tc: e.matmul(po1[:], permb[:, tc * 128:(tc + 1) * 128],
                                           b2b[:, 512:1024], start=True, stop=False, skip_group_check=True))
            for cp in range(32):
                pin = pin0 if cp % 2 == 0 else pin1
                pik = f"pin{cp % 2}"
                mik = f"mib{cp % 2}"
                for kc in range(8):
                    C.op("tensor", ["w1b", "xnb"], [pik],
                         lambda e, tc=tc, cp=cp, kc=kc, pin=pin: e.matmul(
                             pin[:], w1b[:, kc * 4096 + cp * 128: kc * 4096 + (cp + 1) * 128],
                             xnb[:, kc * TPC + tc * 128: kc * TPC + (tc + 1) * 128],
                             start=(kc == 0), stop=(kc == 7), skip_group_check=True),
                         accum=(kc != 0))
                C.op("scalar", [pik, "b1f"], ["mif"],
                     lambda e, cp=cp, pin=pin: e.activation(
                         mif[:], pin[:], AF.Relu, bias=b1f[:, cp:cp + 1]))
                mib = mib0 if cp % 2 == 0 else mib1
                C.op("vector", ["mif", "perm"], [mik],
                     lambda e, tc=tc, mib=mib: e.tensor_tensor(
                         mib[:], mif[:], perm[:, tc * 128:(tc + 1) * 128], op=ALU.mult))
                C.op("tensor", [mik, "w2b"], ["po0"],
                     lambda e, cp=cp, mib=mib: e.matmul(
                         po0[:], mib[:], w2b[:, cp * DM: cp * DM + 512],
                         start=False, stop=(cp == 31), skip_group_check=True),
                     accum=True)
                C.op("tensor", [mik, "w2b"], ["po1"],
                     lambda e, cp=cp, mib=mib: e.matmul(
                         po1[:], mib[:], w2b[:, cp * DM + 512: (cp + 1) * DM],
                         start=False, stop=(cp == 31), skip_group_check=True),
                     accum=True)
            C.op("vector", ["po0"], ["f0sb"], lambda e: e.tensor_copy(f0sb[:, 0:512], po0[:]))
            C.op("vector", ["po1"], ["f0sb"], lambda e: e.tensor_copy(f0sb[:, 512:1024], po1[:]))
            C.dma("gpsimd", ["f0sb"], ["f0"],
                  lambda e, tc=tc: e.dma_start(out=f0[tc * 128:(tc + 1) * 128, :], in_=f0sb[:]))
        C.sync_on("gpsimd", ["f0"])
        C.emit(block)
    return nc


def build_p3():
    nc = bass.Bass("TRN2", target_bir_lowering=False)
    qin = nc.dram_tensor("qin", [8 * 64, T], F32, kind="ExternalInput").ap()
    kin = nc.dram_tensor("kin", [8 * 64, T], F32, kind="ExternalInput").ap()
    vin = nc.dram_tensor("vin", [T, 8 * 64], BF16, kind="ExternalInput").ap()
    eyed = nc.dram_tensor("eyed", [128, 128], F32, kind="ExternalInput").ap()
    aout = nc.dram_tensor("aout", [8 * 64, T], F32, kind="ExternalOutput").ap()
    LOC = T - G
    es = contextlib.ExitStack()
    with es:
        qsb = _sb(nc, es, "qsb", (64, T), F32)
        ksb = _sb(nc, es, "ksb", (64, T), F32)
        qbf = _sb(nc, es, "qbf", (64, T), BF16)
        kbf = _sb(nc, es, "kbf", (64, T), BF16)
        vbf = _sb(nc, es, "vbf", (128, 32 * 64), BF16)
        eye = _sb(nc, es, "eye", (128, 128), F32)
        eyeb = _sb(nc, es, "eyeb", (128, 128), BF16)
        m1 = _sb(nc, es, "m1", (128, 1), F32)
        nb = _sb(nc, es, "nb", (128, 1), F32)
        ssum = _sb(nc, es, "ssum", (128, 1), F32)
        attn_f = _sb(nc, es, "attn_f", (128, G), F32)
        abf = _sb(nc, es, "abf", (128, 4 * G), BF16)
        atT = _sb(nc, es, "atT", (128, G), BF16)
        kle = _sb(nc, es, "kle", (64, LOC), F32)
        klp = _sb(nc, es, "klp", (64, T), BF16)
        klT = _sb(nc, es, "klT", (128, 32 * 64), BF16)
        kvb = _sb(nc, es, "kvb", (64, 64), BF16)
        aosb = _sb(nc, es, "aosb", (64, T), F32)
        ps = _ps(nc, es, "ps", (128, G))
        pg = _ps(nc, es, "pg", (64, G))
        ptb = _ps(nc, es, "ptb", (128, 128), BF16)
        pkv = _ps(nc, es, "pkv", (64, 64))
        plo = _ps(nc, es, "plo", (64, 512))
        C = Ctx(nc, es)
        block = es.enter_context(nc.Block())

        C.dma("sync", [], ["eye"], lambda e: e.dma_start(out=eye[:], in_=eyed))
        C.op("gpsimd", ["eye"], ["eyeb"], lambda e: e.tensor_copy(eyeb[:], eye[:]))
        isz = [(0, 128), (128, 128), (256, 128), (384, 25)]
        for j in range(8):
            C.dma("sync", [], ["qsb"],
                  lambda e, j=j: e.dma_start(out=qsb[:], in_=qin[j * 64:(j + 1) * 64, :]))
            C.dma("sync", [], ["ksb"],
                  lambda e, j=j: e.dma_start(out=ksb[:], in_=kin[j * 64:(j + 1) * 64, :]))
            C.dma("sync", [], ["vbf"],
                  lambda e, j=j: e.dma_start(
                      out=vbf[:].rearrange("p (tt e) -> p tt e", e=64),
                      in_=vin[:, j * 64:(j + 1) * 64].rearrange("(tt p) e -> p tt e", p=128)))
            C.op("gpsimd", ["qsb"], ["qbf"], lambda e: e.tensor_copy(qbf[:], qsb[:]))
            C.op("gpsimd", ["ksb"], ["kbf"], lambda e: e.tensor_copy(kbf[:], ksb[:]))
            for ic, (i0, sz) in enumerate(isz):
                C.op("tensor", ["qbf", "kbf"], ["ps"],
                     lambda e, i0=i0, sz=sz: e.matmul(
                         ps[0:sz, :], qbf[:, i0:i0 + sz], kbf[:, 0:G], start=True, stop=True, skip_group_check=True))
                C.op("vector", ["ps"], ["m1"],
                     lambda e, sz=sz: e.reduce_max(m1[0:sz, :], ps[0:sz, :], axis=AX.X))
                C.op("vector", ["m1"], ["nb"],
                     lambda e, sz=sz: e.tensor_scalar_mul(nb[0:sz, :], m1[0:sz, :], -0.125))
                C.op("scalar", ["ps", "nb"], ["attn_f"],
                     lambda e, sz=sz: e.activation(attn_f[0:sz, :], ps[0:sz, :], AF.Exp,
                                                   bias=nb[0:sz, :], scale=0.125))
                C.op("vector", ["attn_f"], ["ssum"],
                     lambda e, sz=sz: e.reduce_sum(ssum[0:sz, :], attn_f[0:sz, :], axis=AX.X))
                C.op("vector", ["ssum"], ["ssum"],
                     lambda e, sz=sz: e.reciprocal(ssum[0:sz, :], ssum[0:sz, :]))
                C.op("vector", ["attn_f", "ssum"], ["abf"],
                     lambda e, ic=ic, sz=sz: e.tensor_scalar(
                         abf[0:sz, ic * G:(ic + 1) * G], attn_f[0:sz, :], ssum[0:sz, :],
                         None, op0=ALU.mult))
            jsz = [(0, 128), (128, 128), (256, 128), (384, 25)]
            for jc, (j0, szj) in enumerate(jsz):
                for ic, (i0, szi) in enumerate(isz):
                    C.op("tensor", ["abf", "eyeb"], ["ptb"],
                         lambda e, ic=ic, j0=j0, szj=szj, szi=szi: e.transpose(
                             ptb[0:szj, 0:szi], abf[0:szi, ic * G + j0: ic * G + j0 + szj],
                             eyeb[0:szi, 0:szi]))
                    C.op("vector", ["ptb"], ["atT"],
                         lambda e, i0=i0, szj=szj, szi=szi: e.tensor_copy(
                             atT[0:szj, i0:i0 + szi], ptb[0:szj, 0:szi]))
                C.op("tensor", ["atT", "vbf"], ["pg"],
                     lambda e, jc=jc, szj=szj: e.matmul(
                         pg[:, :], vbf[0:szj, jc * 64:(jc + 1) * 64], atT[0:szj, 0:G],
                         start=(jc == 0), stop=(jc == 3), skip_group_check=True),
                     accum=(jc != 0))
            C.op("vector", ["pg"], ["aosb"], lambda e: e.tensor_copy(aosb[:, 0:G], pg[:, :]))
            C.op("vector", ["ksb"], ["m1"],
                 lambda e: e.reduce_max(m1[0:64, :], ksb[:, G:T], axis=AX.X))
            C.op("vector", ["m1"], ["nb"],
                 lambda e: e.tensor_scalar_mul(nb[0:64, :], m1[0:64, :], -1.0))
            C.op("scalar", ["ksb", "nb"], ["kle"],
                 lambda e: e.activation(kle[:], ksb[:, G:T], AF.Exp, bias=nb[0:64, :]))
            C.op("vector", ["kle"], ["ssum"],
                 lambda e: e.reduce_sum(ssum[0:64, :], kle[:], axis=AX.X))
            C.op("vector", ["ssum"], ["ssum"],
                 lambda e: e.reciprocal(ssum[0:64, :], ssum[0:64, :]))
            C.op("vector", [], ["klp"], lambda e: e.memset(klp[:, 0:G], 0.0))
            C.op("vector", ["kle", "ssum"], ["klp"],
                 lambda e: e.tensor_scalar(klp[:, G:T], kle[:], ssum[0:64, :], None, op0=ALU.mult))
            for jc in range(32):
                C.op("tensor", ["klp", "eyeb"], ["ptb"],
                     lambda e, jc=jc: e.transpose(
                         ptb[:, 0:64], klp[:, jc * 128:(jc + 1) * 128], eyeb[0:64, 0:64]))
                C.op("vector", ["ptb"], ["klT"],
                     lambda e, jc=jc: e.tensor_copy(klT[:, jc * 64:(jc + 1) * 64], ptb[:, 0:64]))
            for jc in range(32):
                C.op("tensor", ["klT", "vbf"], ["pkv"],
                     lambda e, jc=jc: e.matmul(
                         pkv[:], vbf[:, jc * 64:(jc + 1) * 64], klT[:, jc * 64:(jc + 1) * 64],
                         start=(jc == 0), stop=(jc == 31), skip_group_check=True),
                     accum=(jc != 0))
            C.op("vector", ["pkv"], ["kvb"], lambda e: e.tensor_copy(kvb[:], pkv[:]))
            off = G
            while off < T:
                sz = min(512, T - off)
                C.op("tensor", ["kvb", "qbf"], ["plo"],
                     lambda e, off=off, sz=sz: e.matmul(
                         plo[:, 0:sz], kvb[:], qbf[:, off:off + sz], start=True, stop=True, skip_group_check=True))
                C.op("vector", ["plo"], ["aosb"],
                     lambda e, off=off, sz=sz: e.tensor_copy(aosb[:, off:off + sz], plo[:, 0:sz]))
                off += sz
            C.dma("gpsimd", ["aosb"], ["aout"],
                  lambda e, j=j: e.dma_start(out=aout[j * 64:(j + 1) * 64, :], in_=aosb[:]))
        C.sync_on("gpsimd", ["aout"])
        C.emit(block)
    return nc


def build_p4():
    nc = bass.Bass("TRN2", target_bir_lowering=False)
    x = nc.dram_tensor("x", [TPC, DM], F32, kind="ExternalInput").ap()
    f0 = nc.dram_tensor("f0", [TPC, DM], F32, kind="ExternalInput").ap()
    ao = nc.dram_tensor("ao", [DM, TPC], F32, kind="ExternalInput").ap()
    wo = nc.dram_tensor("wo", [DM, DM], F32, kind="ExternalInput").ap()
    x1 = nc.dram_tensor("x1", [TPC, DM], F32, kind="ExternalOutput").ap()
    es = contextlib.ExitStack()
    with es:
        stage = _sb(nc, es, "stage", (128, 8 * TPC), F32)
        aob = _sb(nc, es, "aob", (128, 8 * TPC), BF16)
        wob = _sb(nc, es, "wob", (128, 8 * DM), BF16)
        xtile = _sb(nc, es, "xtile", (128, DM), F32)
        ftile = _sb(nc, es, "ftile", (128, DM), F32)
        osb = _sb(nc, es, "osb", (128, DM), F32)
        p0 = _ps(nc, es, "p0", (128, 512))
        p1 = _ps(nc, es, "p1", (128, 512))
        C = Ctx(nc, es)
        block = es.enter_context(nc.Block())
        C.dma("sync", [], ["stage"],
              lambda e: e.dma_start(out=stage[:].rearrange("p (m t) -> p m t", t=TPC),
                                    in_=ao.rearrange("(m p) t -> p m t", p=128)))
        C.op("gpsimd", ["stage"], ["aob"], lambda e: e.tensor_copy(aob[:], stage[:]))
        C.dma("sync", [], ["stage"],
              lambda e: e.dma_start(out=stage[:, 0:8 * DM].rearrange("p (kc m) -> p kc m", m=DM),
                                    in_=wo.rearrange("(kc p) m -> p kc m", p=128)))
        C.op("gpsimd", ["stage"], ["wob"], lambda e: e.tensor_copy(wob[:], stage[:, 0:8 * DM]))
        for tc in range(16):
            for nn in range(2):
                pb = p0 if nn == 0 else p1
                pk = f"p{nn}"
                for kc in range(8):
                    C.op("tensor", ["aob", "wob"], [pk],
                         lambda e, tc=tc, nn=nn, kc=kc, pb=pb: e.matmul(
                             pb[:], aob[:, kc * TPC + tc * 128: kc * TPC + (tc + 1) * 128],
                             wob[:, kc * DM + nn * 512: kc * DM + (nn + 1) * 512],
                             start=(kc == 0), stop=(kc == 7), skip_group_check=True),
                         accum=(kc != 0))
            C.dma("sync", [], ["xtile"],
                  lambda e, tc=tc: e.dma_start(out=xtile[:], in_=x[tc * 128:(tc + 1) * 128, :]))
            C.dma("sync", [], ["ftile"],
                  lambda e, tc=tc: e.dma_start(out=ftile[:], in_=f0[tc * 128:(tc + 1) * 128, :]))
            for nn in range(2):
                pb = p0 if nn == 0 else p1
                pk = f"p{nn}"
                C.op("vector", [pk, "xtile"], ["osb"],
                     lambda e, nn=nn, pb=pb: e.tensor_tensor(
                         osb[:, nn * 512:(nn + 1) * 512], pb[:],
                         xtile[:, nn * 512:(nn + 1) * 512], op=ALU.add))
                C.op("vector", ["osb", "ftile"], ["osb"],
                     lambda e, nn=nn: e.tensor_tensor(
                         osb[:, nn * 512:(nn + 1) * 512], osb[:, nn * 512:(nn + 1) * 512],
                         ftile[:, nn * 512:(nn + 1) * 512], op=ALU.add))
            C.dma("gpsimd", ["osb"], ["x1"],
                  lambda e, tc=tc: e.dma_start(out=x1[tc * 128:(tc + 1) * 128, :], in_=osb[:]))
        C.sync_on("gpsimd", ["x1"])
        C.emit(block)
    return nc


def build_p5():
    W = 1152
    nc = bass.Bass("TRN2", target_bir_lowering=False)
    x1p = nc.dram_tensor("x1p", [2176, DM], F32, kind="ExternalInput").ap()
    inw = nc.dram_tensor("inw", [DM, 2 * DI], F32, kind="ExternalInput").ap()
    inb = nc.dram_tensor("inb", [128, 32], F32, kind="ExternalInput").ap()
    cwd = nc.dram_tensor("cwd", [128, 64], F32, kind="ExternalInput").ap()
    cbd = nc.dram_tensor("cbd", [128, 16], F32, kind="ExternalInput").ap()
    bwd = nc.dram_tensor("bwd", [DI, DS], F32, kind="ExternalInput").ap()
    bbd = nc.dram_tensor("bbd", [16, 1], F32, kind="ExternalInput").ap()
    cwd2 = nc.dram_tensor("cwd2", [DI, DS], F32, kind="ExternalInput").ap()
    cbd2 = nc.dram_tensor("cbd2", [16, 1], F32, kind="ExternalInput").ap()
    dtw = nc.dram_tensor("dtw", [DI, DI], F32, kind="ExternalInput").ap()
    dtb = nc.dram_tensor("dtb", [128, 16], F32, kind="ExternalInput").ap()
    eyed = nc.dram_tensor("eyed", [128, 128], F32, kind="ExternalInput").ap()
    epsd = nc.dram_tensor("epsd", [128, 1], F32, kind="ExternalInput").ap()
    oned = nc.dram_tensor("oned", [128, 1], F32, kind="ExternalInput").ap()
    xco = nc.dram_tensor("xco", [DI, TPC], F32, kind="ExternalOutput").ap()
    dto = nc.dram_tensor("dto", [DI, TPC], F32, kind="ExternalOutput").ap()
    sgo = nc.dram_tensor("sgo", [DI, TPC], F32, kind="ExternalOutput").ap()
    bmo = nc.dram_tensor("bmo", [DS, TPC], F32, kind="ExternalOutput").ap()
    cmo = nc.dram_tensor("cmo", [DS, TPC], F32, kind="ExternalOutput").ap()
    xn2o = nc.dram_tensor("xn2o", [DM, TPC], F32, kind="ExternalOutput").ap()
    es = contextlib.ExitStack()
    with es:
        xtile = _sb(nc, es, "xtile", (128, DM), F32)
        xn = _sb(nc, es, "xn", (128, DM), F32)
        scr = _sb(nc, es, "scr", (128, DM), F32)
        mean = _sb(nc, es, "mean", (128, 1), F32)
        var = _sb(nc, es, "var", (128, 1), F32)
        epscol = _sb(nc, es, "epscol", (128, 1), F32)
        onecol = _sb(nc, es, "onecol", (128, 1), F32)
        eye = _sb(nc, es, "eye", (128, 128), F32)
        x2T = _sb(nc, es, "x2T", (128, 8 * W), F32)
        x2Tb = _sb(nc, es, "x2Tb", (128, 8 * W), BF16)
        wstage = _sb(nc, es, "wstage", (128, 16 * 128), F32)
        iwb = _sb(nc, es, "iwb", (128, 8 * 128), BF16)
        dwb = _sb(nc, es, "dwb", (128, 16 * 128), BF16)
        bwb = _sb(nc, es, "bwb", (128, 16 * DS), BF16)
        cwb = _sb(nc, es, "cwb", (128, 16 * DS), BF16)
        inbc = _sb(nc, es, "inbc", (128, 32), F32)
        cw = _sb(nc, es, "cw", (128, 64), F32)
        cb = _sb(nc, es, "cb", (128, 16), F32)
        dtbc = _sb(nc, es, "dtbc", (128, 16), F32)
        bbc = _sb(nc, es, "bbc", (16, 1), F32)
        cbc2 = _sb(nc, es, "cbc2", (16, 1), F32)
        xzm = _sb(nc, es, "xzm", (128, W), F32)
        acc = _sb(nc, es, "acc", (128, 1024), F32)
        xcm = _sb(nc, es, "xcm", (128, 1024), F32)
        xcb = _sb(nc, es, "xcb", (128, 16 * 1024), BF16)
        sgm = _sb(nc, es, "sgm", (128, W), F32)
        zb = _sb(nc, es, "zb", (128, 512), F32)
        ab = _sb(nc, es, "ab", (128, 512), F32)
        lb = _sb(nc, es, "lb", (128, 512), F32)
        mb = _sb(nc, es, "mb", (128, 512), F32)
        bm = _sb(nc, es, "bm", (16, 512), F32)
        ptr = _ps(nc, es, "ptr", (128, 128))
        pp0 = _ps(nc, es, "pp0", (128, 512))
        pp1 = _ps(nc, es, "pp1", (128, 512))
        pbc = _ps(nc, es, "pbc", (16, 512))
        C = Ctx(nc, es)
        block = es.enter_context(nc.Block())

        C.dma("sync", [], ["epscol"], lambda e: e.dma_start(out=epscol[:], in_=epsd))
        C.dma("sync", [], ["onecol"], lambda e: e.dma_start(out=onecol[:], in_=oned))
        C.dma("sync", [], ["eye"], lambda e: e.dma_start(out=eye[:], in_=eyed))
        C.dma("sync", [], ["inbc"], lambda e: e.dma_start(out=inbc[:], in_=inb))
        C.dma("sync", [], ["cw"], lambda e: e.dma_start(out=cw[:], in_=cwd))
        C.dma("sync", [], ["cb"], lambda e: e.dma_start(out=cb[:], in_=cbd))
        C.dma("sync", [], ["dtbc"], lambda e: e.dma_start(out=dtbc[:], in_=dtb))
        C.dma("sync", [], ["bbc"], lambda e: e.dma_start(out=bbc[:], in_=bbd))
        C.dma("sync", [], ["cbc2"], lambda e: e.dma_start(out=cbc2[:], in_=cbd2))
        C.dma("sync", [], ["wstage"],
              lambda e: e.dma_start(out=wstage[:, 0:16 * DS].rearrange("p (kc n) -> p kc n", n=DS),
                                    in_=bwd.rearrange("(kc p) n -> p kc n", p=128)))
        C.op("gpsimd", ["wstage"], ["bwb"], lambda e: e.tensor_copy(bwb[:], wstage[:, 0:16 * DS]))
        C.dma("sync", [], ["wstage"],
              lambda e: e.dma_start(out=wstage[:, 0:16 * DS].rearrange("p (kc n) -> p kc n", n=DS),
                                    in_=cwd2.rearrange("(kc p) n -> p kc n", p=128)))
        C.op("gpsimd", ["wstage"], ["cwb"], lambda e: e.tensor_copy(cwb[:], wstage[:, 0:16 * DS]))

        for th in range(2):
            r0 = th * 1024
            for i in range(9):
                C.dma("sync", [], ["xtile"],
                      lambda e, r0=r0, i=i: e.dma_start(
                          out=xtile[:], in_=x1p[r0 + i * 128: r0 + (i + 1) * 128, :]))
                _ln_tile(C, xtile[:], "xtile", xn, mean, var, scr, epscol)
                for kc in range(8):
                    C.op("tensor", ["xn", "eye"], ["ptr"],
                         lambda e, kc=kc: e.transpose(ptr[:], xn[:, kc * 128:(kc + 1) * 128],
                                                      eye[:]))
                    C.op("vector", ["ptr"], ["x2T"],
                         lambda e, kc=kc, i=i: e.tensor_copy(
                             x2T[:, kc * W + i * 128: kc * W + (i + 1) * 128], ptr[:]))
            C.op("gpsimd", ["x2T"], ["x2Tb"], lambda e: e.tensor_copy(x2Tb[:], x2T[:]))
            C.dma("gpsimd", ["x2T"], ["xn2o"],
                  lambda e, th=th: e.dma_start(
                      out=xn2o[:, th * 1024:(th + 1) * 1024].rearrange("(kc p) t -> p kc t", p=128),
                      in_=x2T[:].rearrange("p (kc t) -> p kc t", t=W)[:, :, 1:1025]))
            for m in range(32):
                C.dma("sync", [], ["wstage"],
                      lambda e, m=m: e.dma_start(
                          out=wstage[:, 0:8 * 128].rearrange("p (kc c) -> p kc c", c=128),
                          in_=inw[:, m * 128:(m + 1) * 128].rearrange("(kc p) c -> p kc c", p=128)))
                C.op("gpsimd", ["wstage"], ["iwb"],
                     lambda e: e.tensor_copy(iwb[:], wstage[:, 0:8 * 128]))
                for nt in range(3):
                    pb = pp0 if nt % 2 == 0 else pp1
                    pk = f"pp{nt % 2}"
                    for kc in range(8):
                        C.op("tensor", ["iwb", "x2Tb"], [pk],
                             lambda e, nt=nt, kc=kc, pb=pb: e.matmul(
                                 pb[:, 0:384], iwb[:, kc * 128:(kc + 1) * 128],
                                 x2Tb[:, kc * W + nt * 384: kc * W + (nt + 1) * 384],
                                 start=(kc == 0), stop=(kc == 7), skip_group_check=True),
                             accum=(kc != 0))
                    if m < 16:
                        C.op("vector", [pk, "inbc"], ["xzm"],
                             lambda e, m=m, nt=nt, pb=pb: e.tensor_scalar(
                                 xzm[:, nt * 384:(nt + 1) * 384], pb[:, 0:384],
                                 inbc[:, m:m + 1], None, op0=ALU.add))
                    else:
                        C.op("scalar", [pk, "inbc"], ["sgm"],
                             lambda e, m=m, nt=nt, pb=pb: e.activation(
                                 sgm[:, nt * 384:(nt + 1) * 384], pb[:, 0:384], AF.Silu,
                                 bias=inbc[:, m:m + 1]))
                if m < 16:
                    C.op("vector", ["xzm", "cw"], ["acc"],
                         lambda e, m=m: e.tensor_scalar(
                             acc[:], xzm[:, 0:1024], cw[:, m * 4: m * 4 + 1], None, op0=ALU.mult))
                    for jj in range(1, 4):
                        C.op("vector", ["xzm", "cw", "acc"], ["acc"],
                             lambda e, m=m, jj=jj: e.scalar_tensor_tensor(
                                 acc[:], xzm[:, jj:jj + 1024], cw[:, m * 4 + jj: m * 4 + jj + 1],
                                 acc[:], op0=ALU.mult, op1=ALU.add))
                    C.op("scalar", ["acc", "cb"], ["xcm"],
                         lambda e, m=m: e.activation(xcm[:], acc[:], AF.Silu,
                                                     bias=cb[:, m:m + 1]))
                    C.dma("gpsimd", ["xcm"], ["xco"],
                          lambda e, m=m, th=th: e.dma_start(
                              out=xco[m * 128:(m + 1) * 128, th * 1024:(th + 1) * 1024],
                              in_=xcm[:]))
                    C.op("gpsimd", ["xcm"], ["xcb"],
                         lambda e, m=m: e.tensor_copy(xcb[:, m * 1024:(m + 1) * 1024], xcm[:]))
                else:
                    C.dma("gpsimd", ["sgm"], ["sgo"],
                          lambda e, m=m, th=th: e.dma_start(
                              out=sgo[(m - 16) * 128:(m - 15) * 128, th * 1024:(th + 1) * 1024],
                              in_=sgm[:, 1:1025]))
            for m2 in range(16):
                C.dma("sync", [], ["wstage"],
                      lambda e, m2=m2: e.dma_start(
                          out=wstage[:].rearrange("p (kc c) -> p kc c", c=128),
                          in_=dtw[:, m2 * 128:(m2 + 1) * 128].rearrange("(kc p) c -> p kc c",
                                                                       p=128)))
                C.op("gpsimd", ["wstage"], ["dwb"], lambda e: e.tensor_copy(dwb[:], wstage[:]))
                for nt in range(2):
                    pb = pp0 if nt == 0 else pp1
                    pk = f"pp{nt}"
                    for kc in range(16):
                        C.op("tensor", ["dwb", "xcb"], [pk],
                             lambda e, nt=nt, kc=kc, pb=pb: e.matmul(
                                 pb[:], dwb[:, kc * 128:(kc + 1) * 128],
                                 xcb[:, kc * 1024 + nt * 512: kc * 1024 + (nt + 1) * 512],
                                 start=(kc == 0), stop=(kc == 15), skip_group_check=True),
                             accum=(kc != 0))
                    C.op("vector", [pk, "dtbc"], ["zb"],
                         lambda e, m2=m2, pb=pb: e.tensor_scalar(
                             zb[:], pb[:], dtbc[:, m2:m2 + 1], None, op0=ALU.add))
                    C.op("scalar", ["zb"], ["ab"], lambda e: e.activation(ab[:], zb[:], AF.Abs))
                    C.op("scalar", ["ab"], ["ab"],
                         lambda e: e.activation(ab[:], ab[:], AF.Exp, scale=-1.0))
                    C.op("scalar", ["ab", "onecol"], ["lb"],
                         lambda e: e.activation(lb[:], ab[:], AF.Ln, bias=onecol[:]))
                    C.op("vector", ["zb"], ["mb"],
                         lambda e: e.tensor_scalar_max(mb[:], zb[:], 0.0))
                    C.op("vector", ["lb", "mb"], ["zb"],
                         lambda e: e.tensor_tensor(zb[:], lb[:], mb[:], op=ALU.add))
                    C.dma("gpsimd", ["zb"], ["dto"],
                          lambda e, m2=m2, nt=nt, th=th: e.dma_start(
                              out=dto[m2 * 128:(m2 + 1) * 128,
                                      th * 1024 + nt * 512: th * 1024 + (nt + 1) * 512],
                              in_=zb[:]))
            for wsb, bias, outap, okey in ((bwb, bbc, bmo, "bmo"), (cwb, cbc2, cmo, "cmo")):
                for nt in range(2):
                    for kc in range(16):
                        C.op("tensor", ["bwb", "cwb", "xcb"], ["pbc"],
                             lambda e, wsb=wsb, nt=nt, kc=kc: e.matmul(
                                 pbc[:], wsb[:, kc * DS:(kc + 1) * DS],
                                 xcb[:, kc * 1024 + nt * 512: kc * 1024 + (nt + 1) * 512],
                                 start=(kc == 0), stop=(kc == 15), skip_group_check=True),
                             accum=(kc != 0))
                    C.op("vector", ["pbc"], ["bm"],
                         lambda e, bias=bias: e.tensor_scalar(
                             bm[:], pbc[:], bias[:], None, op0=ALU.add))
                    C.dma("gpsimd", ["bm"], [okey],
                          lambda e, outap=outap, nt=nt, th=th: e.dma_start(
                              out=outap[:, th * 1024 + nt * 512: th * 1024 + (nt + 1) * 512],
                              in_=bm[:]))
        C.sync_on("gpsimd", ["xco", "dto", "sgo", "bmo", "cmo", "xn2o"])
        C.emit(block)
    return nc


def build_p6():
    nc = bass.Bass("TRN2", target_bir_lowering=False)
    dtd = nc.dram_tensor("dtd", [1024, T], F32, kind="ExternalInput").ap()
    xcd = nc.dram_tensor("xcd", [1024, T], F32, kind="ExternalInput").ap()
    sgd = nc.dram_tensor("sgd", [1024, T], F32, kind="ExternalInput").ap()
    bmd = nc.dram_tensor("bmd", [DS, T], F32, kind="ExternalInput").ap()
    cmd = nc.dram_tensor("cmd", [DS, T], F32, kind="ExternalInput").ap()
    acold = nc.dram_tensor("acold", [128, 128], F32, kind="ExternalInput").ap()
    dpd = nc.dram_tensor("dpd", [128, 8], F32, kind="ExternalInput").ap()
    owd = nc.dram_tensor("owd", [1024, DM], F32, kind="ExternalInput").ap()
    onesd = nc.dram_tensor("onesd", [1, 128], F32, kind="ExternalInput").ap()
    opo = nc.dram_tensor("opo", [DM, T], F32, kind="ExternalOutput").ap()
    CH = 512
    es = contextlib.ExitStack()
    with es:
        dt = _sb(nc, es, "dt", (128, 8 * CH), F32)
        xc = _sb(nc, es, "xc", (128, 8 * CH), F32)
        sg = _sb(nc, es, "sg", (128, 8 * CH), F32)
        bmr = _sb(nc, es, "bmr", (1, CH), F32)
        cmr = _sb(nc, es, "cmr", (1, CH), F32)
        Bbc = _sb(nc, es, "Bbc", (128, DS * CH), F32)
        Cbc = _sb(nc, es, "Cbc", (128, DS * CH), F32)
        acol = _sb(nc, es, "acol", (128, 128), F32)
        dpc = _sb(nc, es, "dpc", (128, 8), F32)
        ones1 = _sb(nc, es, "ones1", (1, 128), F32)
        hstate = _sb(nc, es, "hstate", (128, 128), F32)
        dA = [_sb(nc, es, f"dA{i}", (128, CH), F32) for i in range(2)]
        dBx = [_sb(nc, es, f"dBx{i}", (128, CH), F32) for i in range(2)]
        hh = [_sb(nc, es, f"hh{i}", (128, CH), F32) for i in range(2)]
        yt = [_sb(nc, es, f"yt{i}", (128, CH), F32) for i in range(2)]
        ycc = _sb(nc, es, "ycc", (128, CH), F32)
        ybf = _sb(nc, es, "ybf", (128, 8 * CH), BF16)
        owst = _sb(nc, es, "owst", (128, 4 * DM), F32)
        owb = _sb(nc, es, "owb", (128, 8 * DM), BF16)
        osb = _sb(nc, es, "osb", (128, CH), F32)
        pb0 = _ps(nc, es, "pb0", (128, 512))
        pb1 = _ps(nc, es, "pb1", (128, 512))
        C = Ctx(nc, es)
        block = es.enter_context(nc.Block())

        C.dma("sync", [], ["acol"], lambda e: e.dma_start(out=acol[:], in_=acold))
        C.dma("sync", [], ["dpc"], lambda e: e.dma_start(out=dpc[:], in_=dpd))
        C.dma("sync", [], ["ones1"], lambda e: e.dma_start(out=ones1[:], in_=onesd))
        for hf in range(2):
            C.dma("sync", [], ["owst"],
                  lambda e, hf=hf: e.dma_start(
                      out=owst[:].rearrange("p (cc m) -> p cc m", m=DM),
                      in_=owd.rearrange("(cc p) m -> p cc m", p=128)[:, hf * 4:(hf + 1) * 4, :]))
            C.op("gpsimd", ["owst"], ["owb"],
                 lambda e, hf=hf: e.tensor_copy(owb[:, hf * 4 * DM:(hf + 1) * 4 * DM], owst[:]))
        C.op("vector", [], ["hstate"], lambda e: e.memset(hstate[:], 0.0))
        for ch in range(8):
            t0 = ch * CH
            C.dma("sync", [], ["dt"],
                  lambda e, t0=t0: e.dma_start(
                      out=dt[:].rearrange("p (cc t) -> p cc t", t=CH),
                      in_=dtd[:, t0:t0 + CH].rearrange("(cc p) t -> p cc t", p=128)))
            C.dma("sync", [], ["xc"],
                  lambda e, t0=t0: e.dma_start(
                      out=xc[:].rearrange("p (cc t) -> p cc t", t=CH),
                      in_=xcd[:, t0:t0 + CH].rearrange("(cc p) t -> p cc t", p=128)))
            C.dma("sync", [], ["sg"],
                  lambda e, t0=t0: e.dma_start(
                      out=sg[:].rearrange("p (cc t) -> p cc t", t=CH),
                      in_=sgd[:, t0:t0 + CH].rearrange("(cc p) t -> p cc t", p=128)))
            for n in range(DS):
                pb = pb0 if n % 2 == 0 else pb1
                pk = f"pb{n % 2}"
                C.dma("sync", [], ["bmr"],
                      lambda e, t0=t0, n=n: e.dma_start(
                          out=bmr[:], in_=bmd[n:n + 1, t0:t0 + CH]))
                C.op("tensor", ["bmr", "ones1"], [pk],
                     lambda e, pb=pb: e.matmul(
                         pb[:], ones1[:], bmr[:], start=True, stop=True, skip_group_check=True))
                C.op("vector", [pk], ["Bbc"],
                     lambda e, n=n, pb=pb: e.tensor_copy(Bbc[:, n * CH:(n + 1) * CH], pb[:]))
                C.dma("sync", [], ["cmr"],
                      lambda e, t0=t0, n=n: e.dma_start(
                          out=cmr[:], in_=cmd[n:n + 1, t0:t0 + CH]))
                C.op("tensor", ["cmr", "ones1"], [pk],
                     lambda e, pb=pb: e.matmul(
                         pb[:], ones1[:], cmr[:], start=True, stop=True, skip_group_check=True))
                C.op("vector", [pk], ["Cbc"],
                     lambda e, n=n, pb=pb: e.tensor_copy(Cbc[:, n * CH:(n + 1) * CH], pb[:]))
            for cc in range(8):
                C.op("vector", ["xc", "dpc"], ["ycc"],
                     lambda e, cc=cc: e.tensor_scalar(
                         ycc[:], xc[:, cc * CH:(cc + 1) * CH], dpc[:, cc:cc + 1],
                         None, op0=ALU.mult))
                for n in range(DS):
                    i = n % 2
                    kda, kdb, kh, kyt = f"dA{i}", f"dBx{i}", f"hh{i}", f"yt{i}"
                    col = cc * 16 + n
                    C.op("scalar", ["dt", "acol"], [kda],
                         lambda e, cc=cc, col=col, i=i: e.activation(
                             dA[i][:], dt[:, cc * CH:(cc + 1) * CH], AF.Exp,
                             scale=acol[:, col:col + 1]))
                    C.op("gpsimd", ["dt", "Bbc"], [kdb],
                         lambda e, cc=cc, n=n, i=i: e.tensor_tensor(
                             dBx[i][:], dt[:, cc * CH:(cc + 1) * CH],
                             Bbc[:, n * CH:(n + 1) * CH], op=ALU.mult))
                    C.op("gpsimd", ["xc", kdb], [kdb],
                         lambda e, cc=cc, i=i: e.tensor_tensor(
                             dBx[i][:], dBx[i][:], xc[:, cc * CH:(cc + 1) * CH], op=ALU.mult))
                    C.op("vector", [kda, kdb, "hstate"], [kh],
                         lambda e, col=col, i=i: e.tensor_tensor_scan(
                             hh[i][:], dA[i][:], dBx[i][:], hstate[:, col:col + 1],
                             op0=ALU.mult, op1=ALU.add))
                    C.op("vector", [kh], ["hstate"],
                         lambda e, col=col, i=i: e.tensor_copy(
                             hstate[:, col:col + 1], hh[i][:, CH - 1:CH]))
                    C.op("gpsimd", [kh, "Cbc"], [kyt],
                         lambda e, n=n, i=i: e.tensor_tensor(
                             yt[i][:], hh[i][:], Cbc[:, n * CH:(n + 1) * CH], op=ALU.mult))
                    C.op("vector", [kyt, "ycc"], ["ycc"],
                         lambda e, i=i: e.tensor_tensor(ycc[:], ycc[:], yt[i][:], op=ALU.add))
                C.op("vector", ["ycc", "sg"], ["ycc"],
                     lambda e, cc=cc: e.tensor_tensor(
                         ycc[:], ycc[:], sg[:, cc * CH:(cc + 1) * CH], op=ALU.mult))
                C.op("gpsimd", ["ycc"], ["ybf"],
                     lambda e, cc=cc: e.tensor_copy(ybf[:, cc * CH:(cc + 1) * CH], ycc[:]))
            for m in range(8):
                pb = pb0 if m % 2 == 0 else pb1
                pk = f"pb{m % 2}"
                for cc in range(8):
                    C.op("tensor", ["owb", "ybf"], [pk],
                         lambda e, m=m, cc=cc, pb=pb: e.matmul(
                             pb[:], owb[:, cc * DM + m * 128: cc * DM + (m + 1) * 128],
                             ybf[:, cc * CH:(cc + 1) * CH],
                             start=(cc == 0), stop=(cc == 7), skip_group_check=True),
                         accum=(cc != 0))
                C.op("vector", [pk], ["osb"], lambda e, pb=pb: e.tensor_copy(osb[:], pb[:]))
                C.dma("gpsimd", ["osb"], ["opo"],
                      lambda e, m=m, t0=t0: e.dma_start(
                          out=opo[m * 128:(m + 1) * 128, t0:t0 + CH], in_=osb[:]))
        C.sync_on("gpsimd", ["opo"])
        C.emit(block)
    return nc


def build_p7():
    nc = bass.Bass("TRN2", target_bir_lowering=False)
    x1 = nc.dram_tensor("x1", [TPC, DM], F32, kind="ExternalInput").ap()
    xn2 = nc.dram_tensor("xn2", [DM, TPC], F32, kind="ExternalInput").ap()
    opa = nc.dram_tensor("opa", [DM, TPC], F32, kind="ExternalInput").ap()
    opb = nc.dram_tensor("opb", [DM, TPC], F32, kind="ExternalInput").ap()
    outbd = nc.dram_tensor("outbd", [128, 8], F32, kind="ExternalInput").ap()
    gwd = nc.dram_tensor("gwd", [DM, NE], F32, kind="ExternalInput").ap()
    gbd = nc.dram_tensor("gbd", [1, NE], F32, kind="ExternalInput").ap()
    onesd = nc.dram_tensor("onesd", [1, 128], F32, kind="ExternalInput").ap()
    eyed = nc.dram_tensor("eyed", [128, 128], F32, kind="ExternalInput").ap()
    base = nc.dram_tensor("base", [DM, TPC], F32, kind="ExternalOutput").ap()
    cto = nc.dram_tensor("cto", [NE, TPC], F32, kind="ExternalOutput").ap()
    xnbo = nc.dram_tensor("xnbo", [DM, TPC], BF16, kind="ExternalOutput").ap()
    es = contextlib.ExitStack()
    with es:
        x1T = _sb(nc, es, "x1T", (128, 8 * TPC), F32)
        xnb = _sb(nc, es, "xnb", (128, 8 * TPC), BF16)
        xtile = _sb(nc, es, "xtile", (128, DM), F32)
        xstage = _sb(nc, es, "xstage", (128, TPC), F32)
        ostage = _sb(nc, es, "ostage", (128, TPC), F32)
        outbc = _sb(nc, es, "outbc", (128, 8), F32)
        gw = _sb(nc, es, "gw", (128, 8 * NE), F32)
        gwb = _sb(nc, es, "gwb", (128, 8 * NE), BF16)
        gb = _sb(nc, es, "gb", (1, NE), F32)
        gbb = _sb(nc, es, "gbb", (1, NE), BF16)
        ones1 = _sb(nc, es, "ones1", (1, 128), F32)
        onesb = _sb(nc, es, "onesb", (1, 128), BF16)
        eye = _sb(nc, es, "eye", (128, 128), F32)
        s = _sb(nc, es, "s", (128, NE), F32)
        eq1 = _sb(nc, es, "eq1", (128, NE), F32)
        s2 = _sb(nc, es, "s2", (128, NE), F32)
        eq2 = _sb(nc, es, "eq2", (128, NE), F32)
        m1 = _sb(nc, es, "m1", (128, 1), F32)
        m2 = _sb(nc, es, "m2", (128, 1), F32)
        dd = _sb(nc, es, "dd", (128, 1), F32)
        ed = _sb(nc, es, "ed", (128, 1), F32)
        p1c = _sb(nc, es, "p1c", (128, 1), F32)
        p2c = _sb(nc, es, "p2c", (128, 1), F32)
        comb = _sb(nc, es, "comb", (128, NE), F32)
        ctsb = _sb(nc, es, "ctsb", (NE, TPC), F32)
        ptr = _ps(nc, es, "ptr", (128, 128))
        pgs = _ps(nc, es, "pgs", (128, NE))
        C = Ctx(nc, es)
        block = es.enter_context(nc.Block())

        C.dma("sync", [], ["outbc"], lambda e: e.dma_start(out=outbc[:], in_=outbd))
        C.dma("sync", [], ["ones1"], lambda e: e.dma_start(out=ones1[:], in_=onesd))
        C.dma("sync", [], ["eye"], lambda e: e.dma_start(out=eye[:], in_=eyed))
        C.dma("sync", [], ["gb"], lambda e: e.dma_start(out=gb[:], in_=gbd))
        C.dma("sync", [], ["gw"],
              lambda e: e.dma_start(out=gw[:].rearrange("p (kc m) -> p kc m", m=NE),
                                    in_=gwd.rearrange("(kc p) m -> p kc m", p=128)))
        C.op("gpsimd", ["gw"], ["gwb"], lambda e: e.tensor_copy(gwb[:], gw[:]))
        C.op("gpsimd", ["gb"], ["gbb"], lambda e: e.tensor_copy(gbb[:], gb[:]))
        C.op("gpsimd", ["ones1"], ["onesb"], lambda e: e.tensor_copy(onesb[:], ones1[:]))
        for tc in range(16):
            C.dma("sync", [], ["xtile"],
                  lambda e, tc=tc: e.dma_start(out=xtile[:], in_=x1[tc * 128:(tc + 1) * 128, :]))
            for kc in range(8):
                C.op("tensor", ["xtile", "eye"], ["ptr"],
                     lambda e, kc=kc: e.transpose(ptr[:], xtile[:, kc * 128:(kc + 1) * 128],
                                                  eye[:]))
                C.op("vector", ["ptr"], ["x1T"],
                     lambda e, kc=kc, tc=tc: e.tensor_copy(
                         x1T[:, kc * TPC + tc * 128: kc * TPC + (tc + 1) * 128], ptr[:]))
        for m in range(8):
            C.dma("sync", [], ["xstage"],
                  lambda e, m=m: e.dma_start(out=xstage[:], in_=xn2[m * 128:(m + 1) * 128, :]))
            C.op("gpsimd", ["xstage"], ["xnb"],
                 lambda e, m=m: e.tensor_copy(xnb[:, m * TPC:(m + 1) * TPC], xstage[:]))
            C.op("vector", ["xstage", "x1T"], ["xstage"],
                 lambda e, m=m: e.tensor_tensor(
                     xstage[:], xstage[:], x1T[:, m * TPC:(m + 1) * TPC], op=ALU.add))
            C.dma("sync", [], ["ostage"],
                  lambda e, m=m: e.dma_start(out=ostage[:], in_=opa[m * 128:(m + 1) * 128, :]))
            C.op("vector", ["xstage", "ostage"], ["xstage"],
                 lambda e: e.tensor_tensor(xstage[:], xstage[:], ostage[:], op=ALU.add))
            C.dma("sync", [], ["ostage"],
                  lambda e, m=m: e.dma_start(out=ostage[:], in_=opb[m * 128:(m + 1) * 128, :]))
            C.op("vector", ["xstage", "ostage"], ["xstage"],
                 lambda e: e.tensor_tensor(xstage[:], xstage[:], ostage[:], op=ALU.add))
            C.op("vector", ["xstage", "outbc"], ["xstage"],
                 lambda e, m=m: e.tensor_scalar(xstage[:], xstage[:], outbc[:, m:m + 1],
                                                None, op0=ALU.add))
            C.dma("gpsimd", ["xstage"], ["base"],
                  lambda e, m=m: e.dma_start(out=base[m * 128:(m + 1) * 128, :], in_=xstage[:]))
        C.dma("gpsimd", ["xnb"], ["xnbo"],
              lambda e: e.dma_start(out=xnbo.rearrange("(m p) t -> p m t", p=128),
                                    in_=xnb[:].rearrange("p (m t) -> p m t", t=TPC)))
        for tc in range(16):
            for kc in range(8):
                C.op("tensor", ["gwb", "xnb"], ["pgs"],
                     lambda e, tc=tc, kc=kc: e.matmul(
                         pgs[:], xnb[:, kc * TPC + tc * 128: kc * TPC + (tc + 1) * 128],
                         gwb[:, kc * NE:(kc + 1) * NE],
                         start=(kc == 0), stop=False, skip_group_check=True),
                     accum=(kc != 0))
            C.op("tensor", ["gbb", "onesb"], ["pgs"],
                 lambda e: e.matmul(pgs[:], onesb[:], gbb[:], start=False, stop=True, skip_group_check=True),
                 accum=True)
            C.op("vector", ["pgs"], ["s"], lambda e: e.tensor_copy(s[:], pgs[:]))
            C.op("vector", ["s"], ["m1"], lambda e: e.reduce_max(m1[:], s[:], axis=AX.X))
            C.op("vector", ["s", "m1"], ["eq1"],
                 lambda e: e.tensor_scalar(eq1[:], s[:], m1[:], None, op0=ALU.is_ge))
            C.op("vector", ["eq1", "s"], ["s2"],
                 lambda e: e.scalar_tensor_tensor(s2[:], eq1[:], -1e30, s[:],
                                                  op0=ALU.mult, op1=ALU.add))
            C.op("vector", ["s2"], ["m2"], lambda e: e.reduce_max(m2[:], s2[:], axis=AX.X))
            C.op("vector", ["s2", "m2"], ["eq2"],
                 lambda e: e.tensor_scalar(eq2[:], s2[:], m2[:], None, op0=ALU.is_ge))
            C.op("vector", ["m1", "m2"], ["dd"],
                 lambda e: e.tensor_tensor(dd[:], m2[:], m1[:], op=ALU.subtract))
            C.op("scalar", ["dd"], ["ed"], lambda e: e.activation(ed[:], dd[:], AF.Exp))
            C.op("vector", ["ed"], ["p1c"], lambda e: e.tensor_scalar_add(p1c[:], ed[:], 1.0))
            C.op("vector", ["p1c"], ["p1c"], lambda e: e.reciprocal(p1c[:], p1c[:]))
            C.op("vector", ["ed", "p1c"], ["p2c"],
                 lambda e: e.tensor_tensor(p2c[:], ed[:], p1c[:], op=ALU.mult))
            C.op("vector", ["eq1", "p1c"], ["eq1"],
                 lambda e: e.tensor_scalar(eq1[:], eq1[:], p1c[:], None, op0=ALU.mult))
            C.op("vector", ["eq2", "p2c", "eq1"], ["comb"],
                 lambda e: e.scalar_tensor_tensor(comb[:], eq2[:], p2c[:], eq1[:],
                                                  op0=ALU.mult, op1=ALU.add))
            C.op("tensor", ["comb", "eye"], ["ptr"],
                 lambda e: e.transpose(ptr[0:NE, :], comb[:], eye[:]))
            C.op("vector", ["ptr"], ["ctsb"],
                 lambda e, tc=tc: e.tensor_copy(ctsb[:, tc * 128:(tc + 1) * 128], ptr[0:NE, :]))
        C.dma("gpsimd", ["ctsb"], ["cto"], lambda e: e.dma_start(out=cto, in_=ctsb[:]))
        C.sync_on("gpsimd", ["base", "cto", "xnbo"])
        C.emit(block)
    return nc


def build_p8():
    nc = bass.Bass("TRN2", target_bir_lowering=False)
    xnb_d = nc.dram_tensor("xnb_d", [DM, TPC], BF16, kind="ExternalInput").ap()
    ctd = nc.dram_tensor("ctd", [1, TPC], F32, kind="ExternalInput").ap()
    accin = nc.dram_tensor("accin", [DM, TPC], F32, kind="ExternalInput").ap()
    w1d = nc.dram_tensor("w1d", [DM, DF], F32, kind="ExternalInput").ap()
    b1d = nc.dram_tensor("b1d", [128, 32], F32, kind="ExternalInput").ap()
    w2d = nc.dram_tensor("w2d", [DF, DM], F32, kind="ExternalInput").ap()
    b2d = nc.dram_tensor("b2d", [128, 8], F32, kind="ExternalInput").ap()
    onesd = nc.dram_tensor("onesd", [1, 128], F32, kind="ExternalInput").ap()
    accout = nc.dram_tensor("accout", [DM, TPC], F32, kind="ExternalOutput").ap()
    es = contextlib.ExitStack()
    with es:
        w1b = _sb(nc, es, "w1b", (128, 8 * DF), BF16)
        w2b = _sb(nc, es, "w2b", (128, 32 * DM), BF16)
        stage = _sb(nc, es, "stage", (128, 2048), F32)
        xnb = _sb(nc, es, "xnb", (128, 8 * 512), BF16)
        hT = _sb(nc, es, "hT", (128, 32 * 512), BF16)
        b1c = _sb(nc, es, "b1c", (128, 32), F32)
        b2c = _sb(nc, es, "b2c", (128, 8), F32)
        ctrow = _sb(nc, es, "ctrow", (1, TPC), F32)
        ones1 = _sb(nc, es, "ones1", (1, 128), F32)
        cbc = _sb(nc, es, "cbc", (128, 512), F32)
        acct = _sb(nc, es, "acct", (128, 512), F32)
        scry = _sb(nc, es, "scry", (128, 512), F32)
        ph = _ps(nc, es, "ph", (128, 512))
        py0 = _ps(nc, es, "py0", (128, 512))
        py1 = _ps(nc, es, "py1", (128, 512))
        pc = _ps(nc, es, "pc", (128, 512))
        C = Ctx(nc, es)
        block = es.enter_context(nc.Block())

        C.dma("sync", [], ["b1c"], lambda e: e.dma_start(out=b1c[:], in_=b1d))
        C.dma("sync", [], ["b2c"], lambda e: e.dma_start(out=b2c[:], in_=b2d))
        C.dma("sync", [], ["ctrow"], lambda e: e.dma_start(out=ctrow[:], in_=ctd))
        C.dma("sync", [], ["ones1"], lambda e: e.dma_start(out=ones1[:], in_=onesd))
        for ch in range(16):
            C.dma("sync", [], ["stage"],
                  lambda e, ch=ch: e.dma_start(
                      out=stage[:].rearrange("p (kc m) -> p kc m", m=256),
                      in_=w1d[:, ch * 256:(ch + 1) * 256].rearrange("(kc p) m -> p kc m", p=128)))
            C.op("gpsimd", ["stage"], ["w1b"],
                 lambda e, ch=ch: e.tensor_copy(
                     w1b[:].rearrange("p (kc f) -> p kc f", f=DF)[:, :, ch * 256:(ch + 1) * 256],
                     stage[:].rearrange("p (kc m) -> p kc m", m=256)))
        for ch in range(16):
            C.dma("sync", [], ["stage"],
                  lambda e, ch=ch: e.dma_start(
                      out=stage[:].rearrange("p (fc m) -> p fc m", m=DM),
                      in_=w2d[ch * 256:(ch + 1) * 256, :].rearrange("(fc p) m -> p fc m", p=128)))
            C.op("gpsimd", ["stage"], ["w2b"],
                 lambda e, ch=ch: e.tensor_copy(
                     w2b[:, ch * 2 * DM:(ch + 1) * 2 * DM], stage[:]))
        for tt in range(4):
            C.dma("sync", [], ["xnb"],
                  lambda e, tt=tt: e.dma_start(
                      out=xnb[:].rearrange("p (kc t) -> p kc t", t=512),
                      in_=xnb_d[:, tt * 512:(tt + 1) * 512].rearrange("(kc p) t -> p kc t",
                                                                     p=128)))
            C.op("tensor", ["ctrow", "ones1"], ["pc"],
                 lambda e, tt=tt: e.matmul(pc[:], ones1[:],
                                           ctrow[:, tt * 512:(tt + 1) * 512],
                                           start=True, stop=True, skip_group_check=True))
            C.op("vector", ["pc"], ["cbc"], lambda e: e.tensor_copy(cbc[:], pc[:]))
            for fc in range(32):
                for kc in range(8):
                    C.op("tensor", ["w1b", "xnb"], ["ph"],
                         lambda e, fc=fc, kc=kc: e.matmul(
                             ph[:], w1b[:, kc * DF + fc * 128: kc * DF + (fc + 1) * 128],
                             xnb[:, kc * 512:(kc + 1) * 512],
                             start=(kc == 0), stop=(kc == 7), skip_group_check=True),
                         accum=(kc != 0))
                C.op("scalar", ["ph", "b1c"], ["hT"],
                     lambda e, fc=fc: e.activation(hT[:, fc * 512:(fc + 1) * 512], ph[:],
                                                   AF.Gelu, bias=b1c[:, fc:fc + 1]))
            for m in range(8):
                pb = py0 if m % 2 == 0 else py1
                pk = f"py{m % 2}"
                for fc in range(32):
                    C.op("tensor", ["w2b", "hT"], [pk],
                         lambda e, m=m, fc=fc, pb=pb: e.matmul(
                             pb[:], w2b[:, fc * DM + m * 128: fc * DM + (m + 1) * 128],
                             hT[:, fc * 512:(fc + 1) * 512],
                             start=(fc == 0), stop=(fc == 31), skip_group_check=True),
                         accum=(fc != 0))
                C.dma("sync", [], ["acct"],
                      lambda e, m=m, tt=tt: e.dma_start(
                          out=acct[:], in_=accin[m * 128:(m + 1) * 128, tt * 512:(tt + 1) * 512]))
                C.op("vector", [pk, "b2c"], ["scry"],
                     lambda e, m=m, pb=pb: e.tensor_scalar(
                         scry[:], pb[:], b2c[:, m:m + 1], None, op0=ALU.add))
                C.op("vector", ["scry", "cbc"], ["scry"],
                     lambda e: e.tensor_tensor(scry[:], scry[:], cbc[:], op=ALU.mult))
                C.op("vector", ["scry", "acct"], ["scry"],
                     lambda e: e.tensor_tensor(scry[:], scry[:], acct[:], op=ALU.add))
                C.dma("gpsimd", ["scry"], ["accout"],
                      lambda e, m=m, tt=tt: e.dma_start(
                          out=accout[m * 128:(m + 1) * 128, tt * 512:(tt + 1) * 512],
                          in_=scry[:]))
        C.sync_on("gpsimd", ["accout"])
        C.emit(block)
    return nc



GEXP = 6


def build_moe():
    nc = bass.Bass("TRN2", target_bir_lowering=False)
    xnb_d = nc.dram_tensor("xnb_d", [DM, TPC], BF16, kind="ExternalInput").ap()
    ctg = nc.dram_tensor("ctg", [GEXP, TPC], F32, kind="ExternalInput").ap()
    accin = nc.dram_tensor("accin", [DM, TPC], F32, kind="ExternalInput").ap()
    w1g = nc.dram_tensor("w1g", [GEXP, DM, DF], F32, kind="ExternalInput").ap()
    b1g = nc.dram_tensor("b1g", [128, GEXP * 32], F32, kind="ExternalInput").ap()
    w2g = nc.dram_tensor("w2g", [GEXP, DF, DM], F32, kind="ExternalInput").ap()
    b2g = nc.dram_tensor("b2g", [128, GEXP * 8], F32, kind="ExternalInput").ap()
    onesd = nc.dram_tensor("onesd", [1, 128], F32, kind="ExternalInput").ap()
    accout = nc.dram_tensor("accout", [DM, TPC], F32, kind="ExternalOutput").ap()
    es = contextlib.ExitStack()
    with es:
        w1b = _sb(nc, es, "w1b", (128, 8 * DF), BF16)
        w2b = _sb(nc, es, "w2b", (128, 32 * DM), BF16)
        stage = [_sb(nc, es, f"stage{i}", (128, 1024), F32) for i in range(2)]
        xnb = _sb(nc, es, "xnb", (128, 8 * 512), BF16)
        hT = _sb(nc, es, "hT", (128, 32 * 512), BF16)
        b1c = _sb(nc, es, "b1c", (128, GEXP * 32), F32)
        b2c = _sb(nc, es, "b2c", (128, GEXP * 8), F32)
        ctrow = _sb(nc, es, "ctrow", (1, 512), F32)
        ones1 = _sb(nc, es, "ones1", (1, 128), F32)
        cbc = _sb(nc, es, "cbc", (128, 512), F32)
        acct = _sb(nc, es, "acct", (128, 512), F32)
        scry = _sb(nc, es, "scry", (128, 512), F32)
        ph0 = _ps(nc, es, "ph0", (128, 512))
        ph1 = _ps(nc, es, "ph1", (128, 512))
        py0 = _ps(nc, es, "py0", (128, 512))
        py1 = _ps(nc, es, "py1", (128, 512))
        pc = _ps(nc, es, "pc", (128, 512))
        C = Ctx(nc, es)
        block = es.enter_context(nc.Block())

        C.dma("sync", [], ["b1c"], lambda e: e.dma_start(out=b1c[:], in_=b1g))
        C.dma("sync", [], ["b2c"], lambda e: e.dma_start(out=b2c[:], in_=b2g))
        C.dma("sync", [], ["ones1"], lambda e: e.dma_start(out=ones1[:], in_=onesd))
        for ex in range(GEXP):
            for ch in range(32):
                sg = stage[ch % 2]
                sk = f"stage{ch % 2}"
                C.dma("sync", [], [sk],
                      lambda e, ex=ex, ch=ch, sg=sg: e.dma_start(
                          out=sg[:].rearrange("p (kc m) -> p kc m", m=128),
                          in_=w1g[ex, :, ch * 128:(ch + 1) * 128]
                          .rearrange("(kc p) m -> p kc m", p=128)))
                C.op("vector", [sk], ["w1b"],
                     lambda e, ch=ch, sg=sg: e.tensor_copy(
                         w1b[:].rearrange("p (kc f) -> p kc f", f=DF)
                         [:, :, ch * 128:(ch + 1) * 128],
                         sg[:].rearrange("p (kc m) -> p kc m", m=128)))
            for ch in range(32):
                sg = stage[ch % 2]
                sk = f"stage{ch % 2}"
                C.dma("sync", [], [sk],
                      lambda e, ex=ex, ch=ch, sg=sg: e.dma_start(
                          out=sg[:],
                          in_=w2g[ex, ch * 128:(ch + 1) * 128, :]
                          .rearrange("(o p) m -> p (o m)", p=128)))
                C.op("gpsimd", [sk], ["w2b"],
                     lambda e, ch=ch, sg=sg: e.tensor_copy(
                         w2b[:, ch * DM:(ch + 1) * DM], sg[:]))
            for tt in range(4):
                C.dma("scalar", [], ["xnb"],
                      lambda e, tt=tt: e.dma_start(
                          out=xnb[:].rearrange("p (kc t) -> p kc t", t=512),
                          in_=xnb_d[:, tt * 512:(tt + 1) * 512]
                          .rearrange("(kc p) t -> p kc t", p=128)))
                C.dma("scalar", [], ["ctrow"],
                      lambda e, ex=ex, tt=tt: e.dma_start(
                          out=ctrow[:], in_=ctg[ex:ex + 1, tt * 512:(tt + 1) * 512]))
                C.op("tensor", ["ctrow", "ones1"], ["pc"],
                     lambda e: e.matmul(pc[:], ones1[:], ctrow[:],
                                        start=True, stop=True, skip_group_check=True))
                C.op("vector", ["pc"], ["cbc"], lambda e: e.tensor_copy(cbc[:], pc[:]))
                for fc in range(32):
                    ph = ph0 if fc % 2 == 0 else ph1
                    phk = f"ph{fc % 2}"
                    for kc in range(8):
                        C.op("tensor", ["w1b", "xnb"], [phk],
                             lambda e, fc=fc, kc=kc, ph=ph: e.matmul(
                                 ph[:], w1b[:, kc * DF + fc * 128: kc * DF + (fc + 1) * 128],
                                 xnb[:, kc * 512:(kc + 1) * 512],
                                 start=(kc == 0), stop=(kc == 7), skip_group_check=True),
                             accum=(kc != 0))
                    C.op("scalar", [phk, "b1c"], ["hT"],
                         lambda e, ex=ex, fc=fc, ph=ph: e.activation(
                             hT[:, fc * 512:(fc + 1) * 512], ph[:], AF.Gelu,
                             bias=b1c[:, ex * 32 + fc: ex * 32 + fc + 1]))
                for m in range(8):
                    pb = py0 if m % 2 == 0 else py1
                    pk = f"py{m % 2}"
                    for fc in range(32):
                        C.op("tensor", ["w2b", "hT"], [pk],
                             lambda e, m=m, fc=fc, pb=pb: e.matmul(
                                 pb[:], w2b[:, fc * DM + m * 128: fc * DM + (m + 1) * 128],
                                 hT[:, fc * 512:(fc + 1) * 512],
                                 start=(fc == 0), stop=(fc == 31), skip_group_check=True),
                             accum=(fc != 0))
                    ak = f"acc_{m}_{tt}"
                    asrc = accin if ex == 0 else accout
                    C.dma("scalar", [ak], ["acct"],
                          lambda e, m=m, tt=tt, asrc=asrc: e.dma_start(
                              out=acct[:],
                              in_=asrc[m * 128:(m + 1) * 128, tt * 512:(tt + 1) * 512]))
                    C.op("vector", [pk, "b2c"], ["scry"],
                         lambda e, ex=ex, m=m, pb=pb: e.tensor_scalar(
                             scry[:], pb[:], b2c[:, ex * 8 + m: ex * 8 + m + 1],
                             None, op0=ALU.add))
                    C.op("vector", ["scry", "cbc"], ["scry"],
                         lambda e: e.tensor_tensor(scry[:], scry[:], cbc[:], op=ALU.mult))
                    C.op("vector", ["scry", "acct"], ["scry"],
                         lambda e: e.tensor_tensor(scry[:], scry[:], acct[:], op=ALU.add))
                    C.dma("gpsimd", ["scry"], [ak],
                          lambda e, m=m, tt=tt: e.dma_start(
                              out=accout[m * 128:(m + 1) * 128, tt * 512:(tt + 1) * 512],
                              in_=scry[:]))
        C.sync_on("gpsimd", [f"acc_{m}_{tt}" for m in range(8) for tt in range(4)])
        C.emit(block)
    return nc


def build_p9():
    nc = bass.Bass("TRN2", target_bir_lowering=False)
    accd = nc.dram_tensor("accd", [DM, TPC], F32, kind="ExternalInput").ap()
    eyed = nc.dram_tensor("eyed", [128, 128], F32, kind="ExternalInput").ap()
    outd = nc.dram_tensor("outd", [TPC, DM], F32, kind="ExternalOutput").ap()
    es = contextlib.ExitStack()
    with es:
        accs = _sb(nc, es, "accs", (128, 8 * TPC), F32)
        eye = _sb(nc, es, "eye", (128, 128), F32)
        otile = _sb(nc, es, "otile", (128, DM), F32)
        ptr = _ps(nc, es, "ptr", (128, 128))
        C = Ctx(nc, es)
        block = es.enter_context(nc.Block())
        C.dma("sync", [], ["eye"], lambda e: e.dma_start(out=eye[:], in_=eyed))
        C.dma("sync", [], ["accs"],
              lambda e: e.dma_start(out=accs[:].rearrange("p (m t) -> p m t", t=TPC),
                                    in_=accd.rearrange("(m p) t -> p m t", p=128)))
        for tc in range(16):
            for m in range(8):
                C.op("tensor", ["accs", "eye"], ["ptr"],
                     lambda e, tc=tc, m=m: e.transpose(
                         ptr[:], accs[:, m * TPC + tc * 128: m * TPC + (tc + 1) * 128], eye[:]))
                C.op("vector", ["ptr"], ["otile"],
                     lambda e, m=m: e.tensor_copy(otile[:, m * 128:(m + 1) * 128], ptr[:]))
            C.dma("gpsimd", ["otile"], ["outd"],
                  lambda e, tc=tc: e.dma_start(out=outd[tc * 128:(tc + 1) * 128, :], in_=otile[:]))
        C.sync_on("gpsimd", ["outd"])
        C.emit(block)
    return nc


def _run(nc, in_maps):
    return run_bass_kernel_spmd(nc, in_maps, core_ids=list(range(NCORES))).results


def kernel(**inputs):
    f32 = np.float32
    cont = np.ascontiguousarray
    eye = np.eye(128, dtype=f32)
    eps = np.full((128, 1), 1e-5, f32)
    one = np.full((128, 1), 1.0, f32)
    ones1 = np.ones((1, 128), f32)

    x_full = np.asarray(inputs["L0_attn_x"], f32).reshape(NB * T, DM)

    inv = 10000.0 ** (-(np.arange(32, dtype=np.float64) / 32.0))
    ang = np.arange(T, dtype=np.float64)[:, None] * inv[None, :]
    cosT = np.cos(ang).astype(f32)
    sinT = np.sin(ang).astype(f32)

    r = np.arange(DM)
    qcol = ((r % 512) // 32) * 64 + (r // 512) * 32 + (r % 32)
    wq_r = cont(np.asarray(inputs["L0_attn_Wq"], f32)[:, qcol])
    wk_r = cont(np.asarray(inputs["L0_attn_Wk"], f32)[:, qcol])
    wv = cont(np.asarray(inputs["L0_attn_Wv"], f32))
    wo = cont(np.asarray(inputs["L0_attn_Wout"], f32))

    nc1 = build_p1()
    maps = []
    for c in range(NCORES):
        b, half = c // 2, c % 2
        t0 = half * TPC
        cosd = cont(np.broadcast_to(cosT[t0:t0 + TPC, :].T, (32, TPC))[np.arange(128) % 32])
        sind = cont(np.broadcast_to(sinT[t0:t0 + TPC, :].T, (32, TPC))[np.arange(128) % 32])
        maps.append({"x": cont(x_full[b * T + t0: b * T + t0 + TPC]),
                     "wq": wq_r, "wk": wk_r, "wv": wv,
                     "cosd": cosd, "sind": sind, "eyed": eye, "epsd": eps})
    r1 = _run(nc1, maps)

    W1 = np.asarray(inputs["L0_ffn_W1"], f32)
    W2 = np.asarray(inputs["L0_ffn_W2"], f32)
    w1f = cont(W1.transpose(2, 3, 0, 1).reshape(DM, 32, 128).reshape(DM, 4096))
    w2f = cont(W2.transpose(3, 0, 1, 2).reshape(32, 128, DM).reshape(4096, DM))
    wcf = cont(np.asarray(inputs["L0_ffn_Wc"], f32).reshape(DM, 128))
    bcb = cont(np.asarray(inputs["L0_ffn_bc"], f32).reshape(128, 1))
    b1f = cont(np.asarray(inputs["L0_ffn_b1"], f32).reshape(128, 32))
    b2f = cont(np.asarray(inputs["L0_ffn_b2"], f32).reshape(128, DM))
    tie_row = np.linspace(0.0, 1e-6, 32, dtype=f32)[np.arange(512) % 32]
    tied = cont(np.broadcast_to(tie_row[None, :], (128, 512)))
    nc2 = build_p2()
    maps = [{"xnt": r1[c]["xnt"], "w1f": w1f, "w2f": w2f, "wcf": wcf, "bcb": bcb,
             "tied": tied, "b1fd": b1f, "b2fd": b2f} for c in range(NCORES)]
    r2 = _run(nc2, maps)

    QT = [np.concatenate([r1[2 * b]["qt"], r1[2 * b + 1]["qt"]], axis=1) for b in range(NB)]
    KT = [np.concatenate([r1[2 * b]["kt"], r1[2 * b + 1]["kt"]], axis=1) for b in range(NB)]
    VF = [np.concatenate([r1[2 * b]["vout"], r1[2 * b + 1]["vout"]], axis=0) for b in range(NB)]
    nc3 = build_p3()
    maps = []
    for c in range(NCORES):
        b = c // 2
        heads = range(0, 8) if c % 2 == 0 else range(8, 16)
        rows = np.concatenate([np.r_[h * 32:(h + 1) * 32, 512 + h * 32: 512 + (h + 1) * 32]
                               for h in heads])
        cols = np.concatenate([np.arange(h * 64, (h + 1) * 64) for h in heads])
        maps.append({"qin": cont(QT[b][rows]), "kin": cont(KT[b][rows]),
                     "vin": cont(VF[b][:, cols]), "eyed": eye})
    r3 = _run(nc3, maps)

    AO = [np.concatenate([r3[2 * b]["aout"], r3[2 * b + 1]["aout"]], axis=0) for b in range(NB)]
    nc4 = build_p4()
    maps = []
    for c in range(NCORES):
        b, half = c // 2, c % 2
        t0 = half * TPC
        maps.append({"x": cont(x_full[b * T + t0: b * T + t0 + TPC]),
                     "f0": r2[c]["f0"], "ao": cont(AO[b][:, t0:t0 + TPC]), "wo": wo})
    r4 = _run(nc4, maps)

    X1 = [np.concatenate([r4[2 * b]["x1"], r4[2 * b + 1]["x1"]], axis=0) for b in range(NB)]
    inw = cont(np.asarray(inputs["L1_attn_in_w"], f32))
    inb = cont(np.asarray(inputs["L1_attn_in_b"], f32).reshape(32, 128).T)
    cwd = cont(np.asarray(inputs["L1_attn_conv_w"], f32)[:, 0, :]
               .reshape(4, 16, 128).transpose(2, 1, 0).reshape(128, 64))
    cbd = cont(np.asarray(inputs["L1_attn_conv_b"], f32).reshape(16, 128).T)
    bwd = cont(np.asarray(inputs["L1_attn_B_w"], f32))
    bbd = cont(np.asarray(inputs["L1_attn_B_b"], f32).reshape(DS, 1))
    cwd2 = cont(np.asarray(inputs["L1_attn_C_w"], f32))
    cbd2 = cont(np.asarray(inputs["L1_attn_C_b"], f32).reshape(DS, 1))
    dtw = cont(np.asarray(inputs["L1_attn_dt_w"], f32))
    dtb = cont(np.asarray(inputs["L1_attn_dt_b"], f32).reshape(16, 128).T)
    nc5 = build_p5()
    maps = []
    for c in range(NCORES):
        b, half = c // 2, c % 2
        t0 = half * TPC
        pad = np.zeros((2176, DM), f32)
        lo = t0 - 1
        hi = min(T, t0 + TPC + 2)
        src0 = max(0, lo)
        pad[src0 - lo: hi - lo] = X1[b][src0:hi]
        maps.append({"x1p": pad, "inw": inw, "inb": inb, "cwd": cwd, "cbd": cbd,
                     "bwd": bwd, "bbd": bbd, "cwd2": cwd2, "cbd2": cbd2,
                     "dtw": dtw, "dtb": dtb, "eyed": eye, "epsd": eps, "oned": one})
    r5 = _run(nc5, maps)

    XC = [np.concatenate([r5[2 * b]["xco"], r5[2 * b + 1]["xco"]], axis=1) for b in range(NB)]
    DT_ = [np.concatenate([r5[2 * b]["dto"], r5[2 * b + 1]["dto"]], axis=1) for b in range(NB)]
    SG = [np.concatenate([r5[2 * b]["sgo"], r5[2 * b + 1]["sgo"]], axis=1) for b in range(NB)]
    BM = [np.concatenate([r5[2 * b]["bmo"], r5[2 * b + 1]["bmo"]], axis=1) for b in range(NB)]
    CM = [np.concatenate([r5[2 * b]["cmo"], r5[2 * b + 1]["cmo"]], axis=1) for b in range(NB)]
    A = np.asarray(inputs["L1_attn_A"], f32)
    Dp = np.asarray(inputs["L1_attn_Dp"], f32)
    ow = cont(np.asarray(inputs["L1_attn_out_w"], f32))
    nc6 = build_p6()
    maps = []
    for c in range(NCORES):
        b, chh = c // 2, c % 2
        c0 = chh * 1024
        acol = cont(A[c0:c0 + 1024].reshape(8, 128, DS).transpose(1, 0, 2).reshape(128, 128))
        dpc = cont(Dp[c0:c0 + 1024].reshape(8, 128).T)
        maps.append({"dtd": cont(DT_[b][c0:c0 + 1024]), "xcd": cont(XC[b][c0:c0 + 1024]),
                     "sgd": cont(SG[b][c0:c0 + 1024]), "bmd": cont(BM[b]), "cmd": cont(CM[b]),
                     "acold": acol, "dpd": dpc, "owd": cont(ow[c0:c0 + 1024]),
                     "onesd": ones1})
    r6 = _run(nc6, maps)

    gw = cont(np.asarray(inputs["L1_ffn_gate_w"], f32))
    gb = cont(np.asarray(inputs["L1_ffn_gate_b"], f32).reshape(1, NE))
    outb = cont(np.asarray(inputs["L1_attn_out_b"], f32).reshape(8, 128).T)
    nc7 = build_p7()
    maps = []
    for c in range(NCORES):
        b, half = c // 2, c % 2
        t0 = half * TPC
        maps.append({"x1": r4[c]["x1"], "xn2": r5[c]["xn2o"],
                     "opa": cont(r6[2 * b]["opo"][:, t0:t0 + TPC]),
                     "opb": cont(r6[2 * b + 1]["opo"][:, t0:t0 + TPC]),
                     "outbd": outb, "gwd": gw, "gbd": gb, "onesd": ones1, "eyed": eye})
    r7 = _run(nc7, maps)

    w1e = np.asarray(inputs["L1_ffn_w1"], f32)
    b1e = np.asarray(inputs["L1_ffn_b1"], f32)
    w2e = np.asarray(inputs["L1_ffn_w2"], f32)
    b2e = np.asarray(inputs["L1_ffn_b2"], f32)
    nc8 = build_moe()
    acc = [r7[c]["base"] for c in range(NCORES)]
    for g in range(NE // GEXP):
        e0 = g * GEXP
        w1d = cont(w1e[e0:e0 + GEXP])
        b1d = cont(np.concatenate([b1e[e0 + i].reshape(32, 128).T for i in range(GEXP)], axis=1))
        w2d = cont(w2e[e0:e0 + GEXP])
        b2d = cont(np.concatenate([b2e[e0 + i].reshape(8, 128).T for i in range(GEXP)], axis=1))
        maps = [{"xnb_d": r7[c]["xnbo"], "ctg": cont(r7[c]["cto"][e0:e0 + GEXP]),
                 "accin": acc[c], "w1g": w1d, "b1g": b1d, "w2g": w2d, "b2g": b2d,
                 "onesd": ones1} for c in range(NCORES)]
        r8 = _run(nc8, maps)
        acc = [r8[c]["accout"] for c in range(NCORES)]

    nc9 = build_p9()
    maps = [{"accd": acc[c], "eyed": eye} for c in range(NCORES)]
    r9 = _run(nc9, maps)

    out = np.empty((NB, T, DM), f32)
    for c in range(NCORES):
        b, half = c // 2, c % 2
        t0 = half * TPC
        out[b, t0:t0 + TPC] = r9[c]["outd"]
    return out


# version 04 (final)
# speedup vs baseline: 1.3228x; 1.1584x over previous
import contextlib
import numpy as np
import ml_dtypes
import concourse.bass as bass
import concourse.mybir as mybir
from concourse.bass_utils import run_bass_kernel_spmd

F32 = mybir.dt.float32
BF16 = mybir.dt.bfloat16
AF = mybir.ActivationFunctionType
ALU = mybir.AluOpType
AX = mybir.AxisListType
NPBF = ml_dtypes.bfloat16

NB, T, DM = 4, 4096, 1024
H, DH, G = 16, 64, 409
TPC = 2048
DI, DS = 2048, 16
NE, DF = 24, 4096
NCORES = 8


class Ctx:
    QUEUES = ("sync", "vector", "scalar", "tensor", "gpsimd")

    def __init__(self, nc, es):
        self.nc = nc
        self.sem = {q: es.enter_context(nc.semaphore(f"prog_{q}")) for q in self.QUEUES}
        self.NDMA = 10
        self.dsem, self.duses = {}, {}
        for q in ("sync", "scalar", "gpsimd", "vector"):
            self.dsem[q] = [es.enter_context(nc.semaphore(f"dma_{q}{i}")) for i in range(self.NDMA)]
            self.duses[q] = [0] * self.NDMA
        self.drr = {q: 0 for q in self.dsem}
        self.cnt = {q: 0 for q in self.QUEUES}
        self.last_write = {}
        self.readers = {}
        self.watermark = {q: {} for q in self.QUEUES}
        self.queue = {q: [] for q in self.QUEUES}

    def _wait(self, eng, kind, q, c):
        wm = self.watermark[eng]
        k = (kind, q)
        if wm.get(k, 0) >= c:
            return
        sem = self.sem[q] if kind == "prog" else self.dsem[q[0]][q[1]]
        self.queue[eng].append(("wait", sem, c))
        wm[k] = c

    def _raw(self, eng, keys):
        for key in keys:
            lw = self.last_write.get(key)
            if lw is not None:
                self._wait(eng, lw[0], lw[1], lw[2])

    def _war_waw(self, eng, keys):
        for key in keys:
            lw = self.last_write.get(key)
            if lw is not None:
                self._wait(eng, lw[0], lw[1], lw[2])
            for (kind, q), c in self.readers.get(key, {}).items():
                self._wait(eng, kind, q, c)

    def _record(self, kind, q, c, reads, writes):
        for key in writes:
            self.last_write[key] = (kind, q, c)
            self.readers[key] = {}
        for key in reads:
            d = self.readers.setdefault(key, {})
            k = (kind, q)
            d[k] = max(d.get(k, 0), c)

    def op(self, eng, reads, writes, fn, accum=False):
        self._raw(eng, reads)
        if accum:
            for key in writes:
                lw = self.last_write.get(key)
                if lw is not None and not (lw[0] == "prog" and lw[1] == eng):
                    self._wait(eng, lw[0], lw[1], lw[2])
                for (kind, q), c in self.readers.get(key, {}).items():
                    if not (kind == "prog" and q == eng):
                        self._wait(eng, kind, q, c)
        else:
            self._war_waw(eng, writes)
        self.cnt[eng] += 1
        self.queue[eng].append(("op", fn, self.sem[eng], 1))
        self._record("prog", eng, self.cnt[eng], reads, writes)

    def dma(self, q, reads, writes, fn):
        self._raw(q, reads)
        self._war_waw(q, writes)
        slot = self.drr[q]
        self.drr[q] = (slot + 1) % self.NDMA
        prev = self.duses[q][slot]
        if prev > 0:
            self._wait(q, "dma", (q, slot), 16 * prev)
        self.duses[q][slot] = prev + 1
        self.queue[q].append(("op", fn, self.dsem[q][slot], 16))
        self._record("dma", (q, slot), 16 * (prev + 1), reads, writes)

    def sync_on(self, eng, keys):
        self._raw(eng, keys)

    def emit(self, block):
        def make_body(qname):
            def body(engine):
                for item in self.queue[qname]:
                    if item[0] == "wait":
                        engine.wait_ge(item[1], item[2])
                    else:
                        _, fn, sem, inc = item
                        fn(engine).then_inc(sem, inc)
            return body
        for qname in self.QUEUES:
            if self.queue[qname]:
                getattr(block, qname)(make_body(qname))


def _sb(nc, es, name, shape, dt):
    return es.enter_context(nc.sbuf_tensor(name, list(shape), dt))


def _ps(nc, es, name, shape, dt=F32):
    return es.enter_context(nc.psum_tensor(name, list(shape), dt))


def _ln_tile(C, xin_ap, xin_key, xn, mean, var, scr, epscol):
    C.op("vector", [xin_key], ["mean"], lambda e: e.reduce_sum(mean[:], xin_ap, axis=AX.X))
    C.op("vector", ["mean"], ["mean"], lambda e: e.tensor_scalar_mul(mean[:], mean[:], 1.0 / DM))
    C.op("vector", [xin_key, "mean"], ["scr"],
         lambda e: e.tensor_scalar(scr[:], xin_ap, mean[:], None, op0=ALU.subtract))
    C.op("scalar", ["scr"], ["xn"], lambda e: e.activation(xn[:], scr[:], AF.Square))
    C.op("vector", ["xn"], ["var"], lambda e: e.reduce_sum(var[:], xn[:], axis=AX.X))
    C.op("vector", ["var"], ["var"], lambda e: e.tensor_scalar_mul(var[:], var[:], 1.0 / DM))
    C.op("scalar", ["var", "epscol"], ["var"],
         lambda e: e.activation(var[:], var[:], AF.Sqrt, bias=epscol[:]))
    C.op("vector", ["var"], ["var"], lambda e: e.reciprocal(var[:], var[:]))
    C.op("vector", ["scr", "var"], ["xn"],
         lambda e: e.tensor_scalar(xn[:], scr[:], var[:], None, op0=ALU.mult))


def build_p1():
    nc = bass.Bass("TRN2", target_bir_lowering=False)
    x = nc.dram_tensor("x", [TPC, DM], F32, kind="ExternalInput").ap()
    wq = nc.dram_tensor("wq", [DM, DM], F32, kind="ExternalInput").ap()
    wk = nc.dram_tensor("wk", [DM, DM], F32, kind="ExternalInput").ap()
    wv = nc.dram_tensor("wv", [DM, DM], F32, kind="ExternalInput").ap()
    cosd = nc.dram_tensor("cosd", [128, TPC], F32, kind="ExternalInput").ap()
    sind = nc.dram_tensor("sind", [128, TPC], F32, kind="ExternalInput").ap()
    eyed = nc.dram_tensor("eyed", [128, 128], F32, kind="ExternalInput").ap()
    epsd = nc.dram_tensor("epsd", [128, 1], F32, kind="ExternalInput").ap()
    qt = nc.dram_tensor("qt", [DM, TPC], F32, kind="ExternalOutput").ap()
    kt = nc.dram_tensor("kt", [DM, TPC], F32, kind="ExternalOutput").ap()
    vout = nc.dram_tensor("vout", [TPC, DM], BF16, kind="ExternalOutput").ap()
    xnt = nc.dram_tensor("xnt", [DM, TPC], BF16, kind="ExternalOutput").ap()
    es = contextlib.ExitStack()
    with es:
        xtile = _sb(nc, es, "xtile", (128, DM), F32)
        xn = _sb(nc, es, "xn", (128, DM), F32)
        scr = _sb(nc, es, "scr", (128, DM), F32)
        mean = _sb(nc, es, "mean", (128, 1), F32)
        var = _sb(nc, es, "var", (128, 1), F32)
        epscol = _sb(nc, es, "epscol", (128, 1), F32)
        eye = _sb(nc, es, "eye", (128, 128), F32)
        cos = _sb(nc, es, "cos", (128, TPC), F32)
        sin = _sb(nc, es, "sin", (128, TPC), F32)
        xnt_bf = _sb(nc, es, "xnt_bf", (128, 8 * TPC), BF16)
        wstage = _sb(nc, es, "wstage", (128, 4 * DM), F32)
        wbf = _sb(nc, es, "wbf", (128, 8 * DM), BF16)
        qraw = _sb(nc, es, "qraw", (128, 8 * TPC), F32)
        t1 = _sb(nc, es, "t1", (128, TPC), F32)
        t2 = _sb(nc, es, "t2", (128, TPC), F32)
        vsb = _sb(nc, es, "vsb", (128, DM), BF16)
        ptr = _ps(nc, es, "ptr", (128, 128))
        pq0 = _ps(nc, es, "pq0", (128, 512))
        pq1 = _ps(nc, es, "pq1", (128, 512))
        C = Ctx(nc, es)
        block = es.enter_context(nc.Block())

        C.dma("sync", [], ["epscol"], lambda e: e.dma_start(out=epscol[:], in_=epsd))
        C.dma("sync", [], ["eye"], lambda e: e.dma_start(out=eye[:], in_=eyed))
        C.dma("sync", [], ["cos"], lambda e: e.dma_start(out=cos[:], in_=cosd))
        C.dma("sync", [], ["sin"], lambda e: e.dma_start(out=sin[:], in_=sind))

        for i in range(16):
            C.dma("sync", [], ["xtile"],
                  lambda e, i=i: e.dma_start(out=xtile[:], in_=x[i * 128:(i + 1) * 128, :]))
            _ln_tile(C, xtile[:], "xtile", xn, mean, var, scr, epscol)
            for kc in range(8):
                C.op("tensor", ["xn", "eye"], ["ptr"],
                     lambda e, kc=kc: e.transpose(ptr[:], xn[:, kc * 128:(kc + 1) * 128], eye[:]))
                C.op("vector", ["ptr"], ["xnt_bf"],
                     lambda e, kc=kc, i=i: e.tensor_copy(
                         xnt_bf[:, kc * TPC + i * 128: kc * TPC + (i + 1) * 128], ptr[:]))
        C.dma("gpsimd", ["xnt_bf"], ["xnt"],
              lambda e: e.dma_start(out=xnt.rearrange("(kc p) t -> p kc t", p=128),
                                    in_=xnt_bf[:].rearrange("p (kc t) -> p kc t", t=TPC)))

        def load_weight_bf(wsrc):
            for hf in range(2):
                C.dma("sync", [], ["wstage"],
                      lambda e, wsrc=wsrc, hf=hf: e.dma_start(
                          out=wstage[:].rearrange("p (kc m) -> p kc m", m=DM),
                          in_=wsrc.rearrange("(kc p) m -> p kc m", p=128)[:, hf * 4:(hf + 1) * 4, :]))
                C.op("gpsimd", ["wstage"], ["wbf"],
                     lambda e, hf=hf: e.tensor_copy(
                         wbf[:, hf * 4 * DM:(hf + 1) * 4 * DM], wstage[:]))

        for wsrc, outap, okey in ((wq, qt, "qt"), (wk, kt, "kt")):
            load_weight_bf(wsrc)
            for m in range(8):
                for tt in range(4):
                    pb = pq0 if tt % 2 == 0 else pq1
                    pk = f"pq{tt % 2}"
                    for kc in range(8):
                        C.op("tensor", ["wbf", "xnt_bf"], [pk],
                             lambda e, m=m, tt=tt, kc=kc, pb=pb: e.matmul(
                                 pb[:], wbf[:, kc * DM + m * 128: kc * DM + (m + 1) * 128],
                                 xnt_bf[:, kc * TPC + tt * 512: kc * TPC + (tt + 1) * 512],
                                 start=(kc == 0), stop=(kc == 7), skip_group_check=True),
                             accum=(kc != 0))
                    C.op("vector", [pk], ["qraw"],
                         lambda e, m=m, tt=tt, pb=pb: e.tensor_copy(
                             qraw[:, m * TPC + tt * 512: m * TPC + (tt + 1) * 512], pb[:]))
            for m in range(4):
                a = qraw[:, m * TPC:(m + 1) * TPC]
                b = qraw[:, (m + 4) * TPC:(m + 5) * TPC]
                C.op("vector", ["qraw", "cos"], ["t1"],
                     lambda e, a=a: e.tensor_tensor(t1[:], a, cos[:], op=ALU.mult))
                C.op("vector", ["qraw", "sin"], ["t2"],
                     lambda e, a=a: e.tensor_tensor(t2[:], a, sin[:], op=ALU.mult))
                C.op("vector", ["qraw", "sin"], ["qraw"],
                     lambda e, a=a, b=b: e.tensor_tensor(a, b, sin[:], op=ALU.mult))
                C.op("vector", ["qraw", "cos"], ["qraw"],
                     lambda e, b=b: e.tensor_tensor(b, b, cos[:], op=ALU.mult))
                C.op("vector", ["qraw", "t2"], ["qraw"],
                     lambda e, b=b: e.tensor_tensor(b, t2[:], b, op=ALU.add))
                C.op("vector", ["qraw", "t1"], ["qraw"],
                     lambda e, a=a: e.tensor_tensor(a, t1[:], a, op=ALU.subtract))
            C.dma("gpsimd", ["qraw"], [okey],
                  lambda e, outap=outap: e.dma_start(
                      out=outap.rearrange("(m p) t -> p m t", p=128),
                      in_=qraw[:].rearrange("p (m t) -> p m t", t=TPC)))

        load_weight_bf(wv)
        for i in range(16):
            for nn in range(2):
                pb = pq0 if nn == 0 else pq1
                pk = f"pq{nn}"
                for kc in range(8):
                    C.op("tensor", ["wbf", "xnt_bf"], [pk],
                         lambda e, i=i, nn=nn, kc=kc, pb=pb: e.matmul(
                             pb[:], xnt_bf[:, kc * TPC + i * 128: kc * TPC + (i + 1) * 128],
                             wbf[:, kc * DM + nn * 512: kc * DM + (nn + 1) * 512],
                             start=(kc == 0), stop=(kc == 7), skip_group_check=True),
                         accum=(kc != 0))
                C.op("vector", [pk], ["vsb"],
                     lambda e, nn=nn, pb=pb: e.tensor_copy(vsb[:, nn * 512:(nn + 1) * 512], pb[:]))
            C.dma("gpsimd", ["vsb"], ["vout"],
                  lambda e, i=i: e.dma_start(out=vout[i * 128:(i + 1) * 128, :], in_=vsb[:]))
        C.sync_on("gpsimd", ["xnt", "qt", "kt", "vout"])
        C.emit(block)
    return nc


def build_p2():
    nc = bass.Bass("TRN2", target_bir_lowering=False)
    xnt = nc.dram_tensor("xnt", [DM, TPC], BF16, kind="ExternalInput").ap()
    w1f = nc.dram_tensor("w1f", [DM, 4096], F32, kind="ExternalInput").ap()
    w2f = nc.dram_tensor("w2f", [4096, DM], F32, kind="ExternalInput").ap()
    wcf = nc.dram_tensor("wcf", [DM, 128], F32, kind="ExternalInput").ap()
    bcb = nc.dram_tensor("bcb", [128, 1], F32, kind="ExternalInput").ap()
    tied = nc.dram_tensor("tied", [128, 512], F32, kind="ExternalInput").ap()
    b1fd = nc.dram_tensor("b1fd", [128, 32], F32, kind="ExternalInput").ap()
    b2fd = nc.dram_tensor("b2fd", [128, DM], F32, kind="ExternalInput").ap()
    f0 = nc.dram_tensor("f0", [TPC, DM], F32, kind="ExternalOutput").ap()
    es = contextlib.ExitStack()
    with es:
        xnb = _sb(nc, es, "xnb", (128, 8 * TPC), BF16)
        w1b = _sb(nc, es, "w1b", (128, 8 * 4096), BF16)
        w2b = _sb(nc, es, "w2b", (128, 32 * DM), BF16)
        wcb = _sb(nc, es, "wcb", (128, 8 * 128), BF16)
        stage = _sb(nc, es, "stage", (128, 1024), F32)
        bcc = _sb(nc, es, "bcc", (128, 1), F32)
        tie = _sb(nc, es, "tie", (128, 512), F32)
        b1f = _sb(nc, es, "b1f", (128, 32), F32)
        b2b = _sb(nc, es, "b2b", (128, DM), BF16)
        logits = _sb(nc, es, "logits", (128, TPC), F32)
        gmax = _sb(nc, es, "gmax", (128, 64), F32)
        perm = _sb(nc, es, "perm", (128, TPC), F32)
        permb = _sb(nc, es, "permb", (128, TPC), BF16)
        mif = _sb(nc, es, "mif", (128, 128), F32)
        mib0 = _sb(nc, es, "mib0", (128, 128), BF16)
        mib1 = _sb(nc, es, "mib1", (128, 128), BF16)
        f0sb = _sb(nc, es, "f0sb", (128, DM), F32)
        pin0 = _ps(nc, es, "pin0", (128, 128))
        pin1 = _ps(nc, es, "pin1", (128, 128))
        po0 = _ps(nc, es, "po0", (128, 512))
        po1 = _ps(nc, es, "po1", (128, 512))
        C = Ctx(nc, es)
        block = es.enter_context(nc.Block())

        C.dma("sync", [], ["xnb"],
              lambda e: e.dma_start(out=xnb[:].rearrange("p (kc t) -> p kc t", t=TPC),
                                    in_=xnt.rearrange("(kc p) t -> p kc t", p=128)))
        C.dma("sync", [], ["bcc"], lambda e: e.dma_start(out=bcc[:], in_=bcb))
        C.dma("sync", [], ["tie"], lambda e: e.dma_start(out=tie[:], in_=tied))
        C.dma("sync", [], ["b1f"], lambda e: e.dma_start(out=b1f[:], in_=b1fd))
        C.dma("sync", [], ["stage"],
              lambda e: e.dma_start(out=stage[:, 0:DM], in_=b2fd))
        C.op("gpsimd", ["stage"], ["b2b"], lambda e: e.tensor_copy(b2b[:], stage[:, 0:DM]))
        C.dma("sync", [], ["stage"],
              lambda e: e.dma_start(out=stage[:].rearrange("p (kc m) -> p kc m", m=128),
                                    in_=wcf.rearrange("(kc p) m -> p kc m", p=128)))
        C.op("gpsimd", ["stage"], ["wcb"], lambda e: e.tensor_copy(wcb[:], stage[:]))
        for ch in range(32):
            C.dma("sync", [], ["stage"],
                  lambda e, ch=ch: e.dma_start(
                      out=stage[:].rearrange("p (kc m) -> p kc m", m=128),
                      in_=w1f[:, ch * 128:(ch + 1) * 128].rearrange("(kc p) m -> p kc m", p=128)))
            C.op("gpsimd", ["stage"], ["w1b"],
                 lambda e, ch=ch: e.tensor_copy(
                     w1b[:].rearrange("p (kc f) -> p kc f", f=4096)[:, :, ch * 128:(ch + 1) * 128],
                     stage[:].rearrange("p (kc m) -> p kc m", m=128)))
        for ch in range(32):
            C.dma("sync", [], ["stage"],
                  lambda e, ch=ch: e.dma_start(
                      out=stage[:],
                      in_=w2f[ch * 128:(ch + 1) * 128, :].rearrange("(o p) m -> p (o m)", p=128)))
            C.op("gpsimd", ["stage"], ["w2b"],
                 lambda e, ch=ch: e.tensor_copy(
                     w2b[:, ch * DM:(ch + 1) * DM], stage[:]))
        for tt in range(4):
            pk = "po0"
            for kc in range(8):
                C.op("tensor", ["wcb", "xnb"], [pk],
                     lambda e, tt=tt, kc=kc: e.matmul(
                         po0[:], wcb[:, kc * 128:(kc + 1) * 128],
                         xnb[:, kc * TPC + tt * 512: kc * TPC + (tt + 1) * 512],
                         start=(kc == 0), stop=(kc == 7), skip_group_check=True),
                     accum=(kc != 0))
            C.op("vector", [pk, "bcc"], ["logits"],
                 lambda e, tt=tt: e.tensor_scalar(
                     logits[:, tt * 512:(tt + 1) * 512], po0[:], bcc[:], None, op0=ALU.add))
            C.op("vector", ["logits", "tie"], ["logits"],
                 lambda e, tt=tt: e.tensor_tensor(
                     logits[:, tt * 512:(tt + 1) * 512],
                     logits[:, tt * 512:(tt + 1) * 512], tie[:], op=ALU.add))
        C.op("vector", ["logits"], ["gmax"],
             lambda e: e.tensor_reduce(gmax[:], logits[:].rearrange("p (g w) -> p g w", w=32),
                                       axis=AX.X, op=ALU.max))
        C.op("vector", ["logits", "gmax"], ["perm"],
             lambda e: e.tensor_tensor(perm[:].rearrange("p (g w) -> p g w", w=32),
                                       logits[:].rearrange("p (g w) -> p g w", w=32),
                                       gmax[:].unsqueeze(2).broadcast_to([128, 64, 32]),
                                       op=ALU.is_ge))
        C.op("gpsimd", ["perm"], ["permb"], lambda e: e.tensor_copy(permb[:], perm[:]))
        for tc in range(16):
            C.op("tensor", ["permb", "b2b"], ["po0"],
                 lambda e, tc=tc: e.matmul(po0[:], permb[:, tc * 128:(tc + 1) * 128],
                                           b2b[:, 0:512], start=True, stop=False, skip_group_check=True))
            C.op("tensor", ["permb", "b2b"], ["po1"],
                 lambda e, tc=tc: e.matmul(po1[:], permb[:, tc * 128:(tc + 1) * 128],
                                           b2b[:, 512:1024], start=True, stop=False, skip_group_check=True))
            for cp in range(32):
                pin = pin0 if cp % 2 == 0 else pin1
                pik = f"pin{cp % 2}"
                mik = f"mib{cp % 2}"
                for kc in range(8):
                    C.op("tensor", ["w1b", "xnb"], [pik],
                         lambda e, tc=tc, cp=cp, kc=kc, pin=pin: e.matmul(
                             pin[:], w1b[:, kc * 4096 + cp * 128: kc * 4096 + (cp + 1) * 128],
                             xnb[:, kc * TPC + tc * 128: kc * TPC + (tc + 1) * 128],
                             start=(kc == 0), stop=(kc == 7), skip_group_check=True),
                         accum=(kc != 0))
                C.op("scalar", [pik, "b1f"], ["mif"],
                     lambda e, cp=cp, pin=pin: e.activation(
                         mif[:], pin[:], AF.Relu, bias=b1f[:, cp:cp + 1]))
                mib = mib0 if cp % 2 == 0 else mib1
                C.op("vector", ["mif", "perm"], [mik],
                     lambda e, tc=tc, mib=mib: e.tensor_tensor(
                         mib[:], mif[:], perm[:, tc * 128:(tc + 1) * 128], op=ALU.mult))
                C.op("tensor", [mik, "w2b"], ["po0"],
                     lambda e, cp=cp, mib=mib: e.matmul(
                         po0[:], mib[:], w2b[:, cp * DM: cp * DM + 512],
                         start=False, stop=(cp == 31), skip_group_check=True),
                     accum=True)
                C.op("tensor", [mik, "w2b"], ["po1"],
                     lambda e, cp=cp, mib=mib: e.matmul(
                         po1[:], mib[:], w2b[:, cp * DM + 512: (cp + 1) * DM],
                         start=False, stop=(cp == 31), skip_group_check=True),
                     accum=True)
            C.op("vector", ["po0"], ["f0sb"], lambda e: e.tensor_copy(f0sb[:, 0:512], po0[:]))
            C.op("vector", ["po1"], ["f0sb"], lambda e: e.tensor_copy(f0sb[:, 512:1024], po1[:]))
            C.dma("gpsimd", ["f0sb"], ["f0"],
                  lambda e, tc=tc: e.dma_start(out=f0[tc * 128:(tc + 1) * 128, :], in_=f0sb[:]))
        C.sync_on("gpsimd", ["f0"])
        C.emit(block)
    return nc


def build_p3():
    nc = bass.Bass("TRN2", target_bir_lowering=False)
    qin = nc.dram_tensor("qin", [8 * 64, T], F32, kind="ExternalInput").ap()
    kin = nc.dram_tensor("kin", [8 * 64, T], F32, kind="ExternalInput").ap()
    vin = nc.dram_tensor("vin", [T, 8 * 64], BF16, kind="ExternalInput").ap()
    eyed = nc.dram_tensor("eyed", [128, 128], F32, kind="ExternalInput").ap()
    aout = nc.dram_tensor("aout", [8 * 64, T], F32, kind="ExternalOutput").ap()
    LOC = T - G
    es = contextlib.ExitStack()
    with es:
        qsb = _sb(nc, es, "qsb", (64, T), F32)
        ksb = _sb(nc, es, "ksb", (64, T), F32)
        qbf = _sb(nc, es, "qbf", (64, T), BF16)
        kbf = _sb(nc, es, "kbf", (64, T), BF16)
        vbf = _sb(nc, es, "vbf", (128, 32 * 64), BF16)
        eye = _sb(nc, es, "eye", (128, 128), F32)
        eyeb = _sb(nc, es, "eyeb", (128, 128), BF16)
        m1 = _sb(nc, es, "m1", (128, 1), F32)
        nb = _sb(nc, es, "nb", (128, 1), F32)
        ssum = _sb(nc, es, "ssum", (128, 1), F32)
        attn_f = _sb(nc, es, "attn_f", (128, G), F32)
        abf = _sb(nc, es, "abf", (128, 4 * G), BF16)
        atT = _sb(nc, es, "atT", (128, G), BF16)
        kle = _sb(nc, es, "kle", (64, LOC), F32)
        klp = _sb(nc, es, "klp", (64, T), BF16)
        klT = _sb(nc, es, "klT", (128, 32 * 64), BF16)
        kvb = _sb(nc, es, "kvb", (64, 64), BF16)
        aosb = _sb(nc, es, "aosb", (64, T), F32)
        ps = _ps(nc, es, "ps", (128, G))
        pg = _ps(nc, es, "pg", (64, G))
        ptb = _ps(nc, es, "ptb", (128, 128), BF16)
        pkv = _ps(nc, es, "pkv", (64, 64))
        plo = _ps(nc, es, "plo", (64, 512))
        C = Ctx(nc, es)
        block = es.enter_context(nc.Block())

        C.dma("sync", [], ["eye"], lambda e: e.dma_start(out=eye[:], in_=eyed))
        C.op("gpsimd", ["eye"], ["eyeb"], lambda e: e.tensor_copy(eyeb[:], eye[:]))
        isz = [(0, 128), (128, 128), (256, 128), (384, 25)]
        for j in range(8):
            C.dma("sync", [], ["qsb"],
                  lambda e, j=j: e.dma_start(out=qsb[:], in_=qin[j * 64:(j + 1) * 64, :]))
            C.dma("sync", [], ["ksb"],
                  lambda e, j=j: e.dma_start(out=ksb[:], in_=kin[j * 64:(j + 1) * 64, :]))
            C.dma("sync", [], ["vbf"],
                  lambda e, j=j: e.dma_start(
                      out=vbf[:].rearrange("p (tt e) -> p tt e", e=64),
                      in_=vin[:, j * 64:(j + 1) * 64].rearrange("(tt p) e -> p tt e", p=128)))
            C.op("gpsimd", ["qsb"], ["qbf"], lambda e: e.tensor_copy(qbf[:], qsb[:]))
            C.op("gpsimd", ["ksb"], ["kbf"], lambda e: e.tensor_copy(kbf[:], ksb[:]))
            for ic, (i0, sz) in enumerate(isz):
                C.op("tensor", ["qbf", "kbf"], ["ps"],
                     lambda e, i0=i0, sz=sz: e.matmul(
                         ps[0:sz, :], qbf[:, i0:i0 + sz], kbf[:, 0:G], start=True, stop=True, skip_group_check=True))
                C.op("vector", ["ps"], ["m1"],
                     lambda e, sz=sz: e.reduce_max(m1[0:sz, :], ps[0:sz, :], axis=AX.X))
                C.op("vector", ["m1"], ["nb"],
                     lambda e, sz=sz: e.tensor_scalar_mul(nb[0:sz, :], m1[0:sz, :], -0.125))
                C.op("scalar", ["ps", "nb"], ["attn_f"],
                     lambda e, sz=sz: e.activation(attn_f[0:sz, :], ps[0:sz, :], AF.Exp,
                                                   bias=nb[0:sz, :], scale=0.125))
                C.op("vector", ["attn_f"], ["ssum"],
                     lambda e, sz=sz: e.reduce_sum(ssum[0:sz, :], attn_f[0:sz, :], axis=AX.X))
                C.op("vector", ["ssum"], ["ssum"],
                     lambda e, sz=sz: e.reciprocal(ssum[0:sz, :], ssum[0:sz, :]))
                C.op("vector", ["attn_f", "ssum"], ["abf"],
                     lambda e, ic=ic, sz=sz: e.tensor_scalar(
                         abf[0:sz, ic * G:(ic + 1) * G], attn_f[0:sz, :], ssum[0:sz, :],
                         None, op0=ALU.mult))
            jsz = [(0, 128), (128, 128), (256, 128), (384, 25)]
            for jc, (j0, szj) in enumerate(jsz):
                for ic, (i0, szi) in enumerate(isz):
                    C.op("tensor", ["abf", "eyeb"], ["ptb"],
                         lambda e, ic=ic, j0=j0, szj=szj, szi=szi: e.transpose(
                             ptb[0:szj, 0:szi], abf[0:szi, ic * G + j0: ic * G + j0 + szj],
                             eyeb[0:szi, 0:szi]))
                    C.op("vector", ["ptb"], ["atT"],
                         lambda e, i0=i0, szj=szj, szi=szi: e.tensor_copy(
                             atT[0:szj, i0:i0 + szi], ptb[0:szj, 0:szi]))
                C.op("tensor", ["atT", "vbf"], ["pg"],
                     lambda e, jc=jc, szj=szj: e.matmul(
                         pg[:, :], vbf[0:szj, jc * 64:(jc + 1) * 64], atT[0:szj, 0:G],
                         start=(jc == 0), stop=(jc == 3), skip_group_check=True),
                     accum=(jc != 0))
            C.op("vector", ["pg"], ["aosb"], lambda e: e.tensor_copy(aosb[:, 0:G], pg[:, :]))
            C.op("vector", ["ksb"], ["m1"],
                 lambda e: e.reduce_max(m1[0:64, :], ksb[:, G:T], axis=AX.X))
            C.op("vector", ["m1"], ["nb"],
                 lambda e: e.tensor_scalar_mul(nb[0:64, :], m1[0:64, :], -1.0))
            C.op("scalar", ["ksb", "nb"], ["kle"],
                 lambda e: e.activation(kle[:], ksb[:, G:T], AF.Exp, bias=nb[0:64, :]))
            C.op("vector", ["kle"], ["ssum"],
                 lambda e: e.reduce_sum(ssum[0:64, :], kle[:], axis=AX.X))
            C.op("vector", ["ssum"], ["ssum"],
                 lambda e: e.reciprocal(ssum[0:64, :], ssum[0:64, :]))
            C.op("vector", [], ["klp"], lambda e: e.memset(klp[:, 0:G], 0.0))
            C.op("vector", ["kle", "ssum"], ["klp"],
                 lambda e: e.tensor_scalar(klp[:, G:T], kle[:], ssum[0:64, :], None, op0=ALU.mult))
            for jc in range(32):
                C.op("tensor", ["klp", "eyeb"], ["ptb"],
                     lambda e, jc=jc: e.transpose(
                         ptb[:, 0:64], klp[:, jc * 128:(jc + 1) * 128], eyeb[0:64, 0:64]))
                C.op("vector", ["ptb"], ["klT"],
                     lambda e, jc=jc: e.tensor_copy(klT[:, jc * 64:(jc + 1) * 64], ptb[:, 0:64]))
            for jc in range(32):
                C.op("tensor", ["klT", "vbf"], ["pkv"],
                     lambda e, jc=jc: e.matmul(
                         pkv[:], vbf[:, jc * 64:(jc + 1) * 64], klT[:, jc * 64:(jc + 1) * 64],
                         start=(jc == 0), stop=(jc == 31), skip_group_check=True),
                     accum=(jc != 0))
            C.op("vector", ["pkv"], ["kvb"], lambda e: e.tensor_copy(kvb[:], pkv[:]))
            off = G
            while off < T:
                sz = min(512, T - off)
                C.op("tensor", ["kvb", "qbf"], ["plo"],
                     lambda e, off=off, sz=sz: e.matmul(
                         plo[:, 0:sz], kvb[:], qbf[:, off:off + sz], start=True, stop=True, skip_group_check=True))
                C.op("vector", ["plo"], ["aosb"],
                     lambda e, off=off, sz=sz: e.tensor_copy(aosb[:, off:off + sz], plo[:, 0:sz]))
                off += sz
            C.dma("gpsimd", ["aosb"], ["aout"],
                  lambda e, j=j: e.dma_start(out=aout[j * 64:(j + 1) * 64, :], in_=aosb[:]))
        C.sync_on("gpsimd", ["aout"])
        C.emit(block)
    return nc


def build_p4():
    nc = bass.Bass("TRN2", target_bir_lowering=False)
    x = nc.dram_tensor("x", [TPC, DM], F32, kind="ExternalInput").ap()
    f0 = nc.dram_tensor("f0", [TPC, DM], F32, kind="ExternalInput").ap()
    ao = nc.dram_tensor("ao", [DM, TPC], F32, kind="ExternalInput").ap()
    wo = nc.dram_tensor("wo", [DM, DM], F32, kind="ExternalInput").ap()
    x1 = nc.dram_tensor("x1", [TPC, DM], F32, kind="ExternalOutput").ap()
    es = contextlib.ExitStack()
    with es:
        stage = _sb(nc, es, "stage", (128, 8 * TPC), F32)
        aob = _sb(nc, es, "aob", (128, 8 * TPC), BF16)
        wob = _sb(nc, es, "wob", (128, 8 * DM), BF16)
        xtile = _sb(nc, es, "xtile", (128, DM), F32)
        ftile = _sb(nc, es, "ftile", (128, DM), F32)
        osb = _sb(nc, es, "osb", (128, DM), F32)
        p0 = _ps(nc, es, "p0", (128, 512))
        p1 = _ps(nc, es, "p1", (128, 512))
        C = Ctx(nc, es)
        block = es.enter_context(nc.Block())
        C.dma("sync", [], ["stage"],
              lambda e: e.dma_start(out=stage[:].rearrange("p (m t) -> p m t", t=TPC),
                                    in_=ao.rearrange("(m p) t -> p m t", p=128)))
        C.op("gpsimd", ["stage"], ["aob"], lambda e: e.tensor_copy(aob[:], stage[:]))
        C.dma("sync", [], ["stage"],
              lambda e: e.dma_start(out=stage[:, 0:8 * DM].rearrange("p (kc m) -> p kc m", m=DM),
                                    in_=wo.rearrange("(kc p) m -> p kc m", p=128)))
        C.op("gpsimd", ["stage"], ["wob"], lambda e: e.tensor_copy(wob[:], stage[:, 0:8 * DM]))
        for tc in range(16):
            for nn in range(2):
                pb = p0 if nn == 0 else p1
                pk = f"p{nn}"
                for kc in range(8):
                    C.op("tensor", ["aob", "wob"], [pk],
                         lambda e, tc=tc, nn=nn, kc=kc, pb=pb: e.matmul(
                             pb[:], aob[:, kc * TPC + tc * 128: kc * TPC + (tc + 1) * 128],
                             wob[:, kc * DM + nn * 512: kc * DM + (nn + 1) * 512],
                             start=(kc == 0), stop=(kc == 7), skip_group_check=True),
                         accum=(kc != 0))
            C.dma("sync", [], ["xtile"],
                  lambda e, tc=tc: e.dma_start(out=xtile[:], in_=x[tc * 128:(tc + 1) * 128, :]))
            C.dma("sync", [], ["ftile"],
                  lambda e, tc=tc: e.dma_start(out=ftile[:], in_=f0[tc * 128:(tc + 1) * 128, :]))
            for nn in range(2):
                pb = p0 if nn == 0 else p1
                pk = f"p{nn}"
                C.op("vector", [pk, "xtile"], ["osb"],
                     lambda e, nn=nn, pb=pb: e.tensor_tensor(
                         osb[:, nn * 512:(nn + 1) * 512], pb[:],
                         xtile[:, nn * 512:(nn + 1) * 512], op=ALU.add))
                C.op("vector", ["osb", "ftile"], ["osb"],
                     lambda e, nn=nn: e.tensor_tensor(
                         osb[:, nn * 512:(nn + 1) * 512], osb[:, nn * 512:(nn + 1) * 512],
                         ftile[:, nn * 512:(nn + 1) * 512], op=ALU.add))
            C.dma("gpsimd", ["osb"], ["x1"],
                  lambda e, tc=tc: e.dma_start(out=x1[tc * 128:(tc + 1) * 128, :], in_=osb[:]))
        C.sync_on("gpsimd", ["x1"])
        C.emit(block)
    return nc


def build_p5():
    W = 1152
    nc = bass.Bass("TRN2", target_bir_lowering=False)
    x1p = nc.dram_tensor("x1p", [2176, DM], F32, kind="ExternalInput").ap()
    inw = nc.dram_tensor("inw", [DM, 2 * DI], F32, kind="ExternalInput").ap()
    inb = nc.dram_tensor("inb", [128, 32], F32, kind="ExternalInput").ap()
    cwd = nc.dram_tensor("cwd", [128, 64], F32, kind="ExternalInput").ap()
    cbd = nc.dram_tensor("cbd", [128, 16], F32, kind="ExternalInput").ap()
    bwd = nc.dram_tensor("bwd", [DI, DS], F32, kind="ExternalInput").ap()
    bbd = nc.dram_tensor("bbd", [16, 1], F32, kind="ExternalInput").ap()
    cwd2 = nc.dram_tensor("cwd2", [DI, DS], F32, kind="ExternalInput").ap()
    cbd2 = nc.dram_tensor("cbd2", [16, 1], F32, kind="ExternalInput").ap()
    dtw = nc.dram_tensor("dtw", [DI, DI], F32, kind="ExternalInput").ap()
    dtb = nc.dram_tensor("dtb", [128, 16], F32, kind="ExternalInput").ap()
    eyed = nc.dram_tensor("eyed", [128, 128], F32, kind="ExternalInput").ap()
    epsd = nc.dram_tensor("epsd", [128, 1], F32, kind="ExternalInput").ap()
    oned = nc.dram_tensor("oned", [128, 1], F32, kind="ExternalInput").ap()
    xco = nc.dram_tensor("xco", [DI, TPC], F32, kind="ExternalOutput").ap()
    dto = nc.dram_tensor("dto", [DI, TPC], F32, kind="ExternalOutput").ap()
    sgo = nc.dram_tensor("sgo", [DI, TPC], F32, kind="ExternalOutput").ap()
    bmo = nc.dram_tensor("bmo", [DS, TPC], F32, kind="ExternalOutput").ap()
    cmo = nc.dram_tensor("cmo", [DS, TPC], F32, kind="ExternalOutput").ap()
    xn2o = nc.dram_tensor("xn2o", [DM, TPC], F32, kind="ExternalOutput").ap()
    es = contextlib.ExitStack()
    with es:
        xtile = _sb(nc, es, "xtile", (128, DM), F32)
        xn = _sb(nc, es, "xn", (128, DM), F32)
        scr = _sb(nc, es, "scr", (128, DM), F32)
        mean = _sb(nc, es, "mean", (128, 1), F32)
        var = _sb(nc, es, "var", (128, 1), F32)
        epscol = _sb(nc, es, "epscol", (128, 1), F32)
        onecol = _sb(nc, es, "onecol", (128, 1), F32)
        eye = _sb(nc, es, "eye", (128, 128), F32)
        x2T = _sb(nc, es, "x2T", (128, 8 * W), F32)
        x2Tb = _sb(nc, es, "x2Tb", (128, 8 * W), BF16)
        wstage = _sb(nc, es, "wstage", (128, 16 * 128), F32)
        iwb = _sb(nc, es, "iwb", (128, 8 * 128), BF16)
        dwb = _sb(nc, es, "dwb", (128, 16 * 128), BF16)
        bwb = _sb(nc, es, "bwb", (128, 16 * DS), BF16)
        cwb = _sb(nc, es, "cwb", (128, 16 * DS), BF16)
        inbc = _sb(nc, es, "inbc", (128, 32), F32)
        cw = _sb(nc, es, "cw", (128, 64), F32)
        cb = _sb(nc, es, "cb", (128, 16), F32)
        dtbc = _sb(nc, es, "dtbc", (128, 16), F32)
        bbc = _sb(nc, es, "bbc", (16, 1), F32)
        cbc2 = _sb(nc, es, "cbc2", (16, 1), F32)
        xzm = _sb(nc, es, "xzm", (128, W), F32)
        acc = _sb(nc, es, "acc", (128, 1024), F32)
        xcm = _sb(nc, es, "xcm", (128, 1024), F32)
        xcb = _sb(nc, es, "xcb", (128, 16 * 1024), BF16)
        sgm = _sb(nc, es, "sgm", (128, W), F32)
        zb = _sb(nc, es, "zb", (128, 512), F32)
        ab = _sb(nc, es, "ab", (128, 512), F32)
        lb = _sb(nc, es, "lb", (128, 512), F32)
        mb = _sb(nc, es, "mb", (128, 512), F32)
        bm = _sb(nc, es, "bm", (16, 512), F32)
        ptr = _ps(nc, es, "ptr", (128, 128))
        pp0 = _ps(nc, es, "pp0", (128, 512))
        pp1 = _ps(nc, es, "pp1", (128, 512))
        pbc = _ps(nc, es, "pbc", (16, 512))
        C = Ctx(nc, es)
        block = es.enter_context(nc.Block())

        C.dma("sync", [], ["epscol"], lambda e: e.dma_start(out=epscol[:], in_=epsd))
        C.dma("sync", [], ["onecol"], lambda e: e.dma_start(out=onecol[:], in_=oned))
        C.dma("sync", [], ["eye"], lambda e: e.dma_start(out=eye[:], in_=eyed))
        C.dma("sync", [], ["inbc"], lambda e: e.dma_start(out=inbc[:], in_=inb))
        C.dma("sync", [], ["cw"], lambda e: e.dma_start(out=cw[:], in_=cwd))
        C.dma("sync", [], ["cb"], lambda e: e.dma_start(out=cb[:], in_=cbd))
        C.dma("sync", [], ["dtbc"], lambda e: e.dma_start(out=dtbc[:], in_=dtb))
        C.dma("sync", [], ["bbc"], lambda e: e.dma_start(out=bbc[:], in_=bbd))
        C.dma("sync", [], ["cbc2"], lambda e: e.dma_start(out=cbc2[:], in_=cbd2))
        C.dma("sync", [], ["wstage"],
              lambda e: e.dma_start(out=wstage[:, 0:16 * DS].rearrange("p (kc n) -> p kc n", n=DS),
                                    in_=bwd.rearrange("(kc p) n -> p kc n", p=128)))
        C.op("gpsimd", ["wstage"], ["bwb"], lambda e: e.tensor_copy(bwb[:], wstage[:, 0:16 * DS]))
        C.dma("sync", [], ["wstage"],
              lambda e: e.dma_start(out=wstage[:, 0:16 * DS].rearrange("p (kc n) -> p kc n", n=DS),
                                    in_=cwd2.rearrange("(kc p) n -> p kc n", p=128)))
        C.op("gpsimd", ["wstage"], ["cwb"], lambda e: e.tensor_copy(cwb[:], wstage[:, 0:16 * DS]))

        for th in range(2):
            r0 = th * 1024
            for i in range(9):
                C.dma("sync", [], ["xtile"],
                      lambda e, r0=r0, i=i: e.dma_start(
                          out=xtile[:], in_=x1p[r0 + i * 128: r0 + (i + 1) * 128, :]))
                _ln_tile(C, xtile[:], "xtile", xn, mean, var, scr, epscol)
                for kc in range(8):
                    C.op("tensor", ["xn", "eye"], ["ptr"],
                         lambda e, kc=kc: e.transpose(ptr[:], xn[:, kc * 128:(kc + 1) * 128],
                                                      eye[:]))
                    C.op("vector", ["ptr"], ["x2T"],
                         lambda e, kc=kc, i=i: e.tensor_copy(
                             x2T[:, kc * W + i * 128: kc * W + (i + 1) * 128], ptr[:]))
            C.op("gpsimd", ["x2T"], ["x2Tb"], lambda e: e.tensor_copy(x2Tb[:], x2T[:]))
            C.dma("gpsimd", ["x2T"], ["xn2o"],
                  lambda e, th=th: e.dma_start(
                      out=xn2o[:, th * 1024:(th + 1) * 1024].rearrange("(kc p) t -> p kc t", p=128),
                      in_=x2T[:].rearrange("p (kc t) -> p kc t", t=W)[:, :, 1:1025]))
            for m in range(32):
                C.dma("sync", [], ["wstage"],
                      lambda e, m=m: e.dma_start(
                          out=wstage[:, 0:8 * 128].rearrange("p (kc c) -> p kc c", c=128),
                          in_=inw[:, m * 128:(m + 1) * 128].rearrange("(kc p) c -> p kc c", p=128)))
                C.op("gpsimd", ["wstage"], ["iwb"],
                     lambda e: e.tensor_copy(iwb[:], wstage[:, 0:8 * 128]))
                for nt in range(3):
                    pb = pp0 if nt % 2 == 0 else pp1
                    pk = f"pp{nt % 2}"
                    for kc in range(8):
                        C.op("tensor", ["iwb", "x2Tb"], [pk],
                             lambda e, nt=nt, kc=kc, pb=pb: e.matmul(
                                 pb[:, 0:384], iwb[:, kc * 128:(kc + 1) * 128],
                                 x2Tb[:, kc * W + nt * 384: kc * W + (nt + 1) * 384],
                                 start=(kc == 0), stop=(kc == 7), skip_group_check=True),
                             accum=(kc != 0))
                    if m < 16:
                        C.op("vector", [pk, "inbc"], ["xzm"],
                             lambda e, m=m, nt=nt, pb=pb: e.tensor_scalar(
                                 xzm[:, nt * 384:(nt + 1) * 384], pb[:, 0:384],
                                 inbc[:, m:m + 1], None, op0=ALU.add))
                    else:
                        C.op("scalar", [pk, "inbc"], ["sgm"],
                             lambda e, m=m, nt=nt, pb=pb: e.activation(
                                 sgm[:, nt * 384:(nt + 1) * 384], pb[:, 0:384], AF.Silu,
                                 bias=inbc[:, m:m + 1]))
                if m < 16:
                    C.op("vector", ["xzm", "cw"], ["acc"],
                         lambda e, m=m: e.tensor_scalar(
                             acc[:], xzm[:, 0:1024], cw[:, m * 4: m * 4 + 1], None, op0=ALU.mult))
                    for jj in range(1, 4):
                        C.op("vector", ["xzm", "cw", "acc"], ["acc"],
                             lambda e, m=m, jj=jj: e.scalar_tensor_tensor(
                                 acc[:], xzm[:, jj:jj + 1024], cw[:, m * 4 + jj: m * 4 + jj + 1],
                                 acc[:], op0=ALU.mult, op1=ALU.add))
                    C.op("scalar", ["acc", "cb"], ["xcm"],
                         lambda e, m=m: e.activation(xcm[:], acc[:], AF.Silu,
                                                     bias=cb[:, m:m + 1]))
                    C.dma("gpsimd", ["xcm"], ["xco"],
                          lambda e, m=m, th=th: e.dma_start(
                              out=xco[m * 128:(m + 1) * 128, th * 1024:(th + 1) * 1024],
                              in_=xcm[:]))
                    C.op("gpsimd", ["xcm"], ["xcb"],
                         lambda e, m=m: e.tensor_copy(xcb[:, m * 1024:(m + 1) * 1024], xcm[:]))
                else:
                    C.dma("gpsimd", ["sgm"], ["sgo"],
                          lambda e, m=m, th=th: e.dma_start(
                              out=sgo[(m - 16) * 128:(m - 15) * 128, th * 1024:(th + 1) * 1024],
                              in_=sgm[:, 1:1025]))
            for m2 in range(16):
                C.dma("sync", [], ["wstage"],
                      lambda e, m2=m2: e.dma_start(
                          out=wstage[:].rearrange("p (kc c) -> p kc c", c=128),
                          in_=dtw[:, m2 * 128:(m2 + 1) * 128].rearrange("(kc p) c -> p kc c",
                                                                       p=128)))
                C.op("gpsimd", ["wstage"], ["dwb"], lambda e: e.tensor_copy(dwb[:], wstage[:]))
                for nt in range(2):
                    pb = pp0 if nt == 0 else pp1
                    pk = f"pp{nt}"
                    for kc in range(16):
                        C.op("tensor", ["dwb", "xcb"], [pk],
                             lambda e, nt=nt, kc=kc, pb=pb: e.matmul(
                                 pb[:], dwb[:, kc * 128:(kc + 1) * 128],
                                 xcb[:, kc * 1024 + nt * 512: kc * 1024 + (nt + 1) * 512],
                                 start=(kc == 0), stop=(kc == 15), skip_group_check=True),
                             accum=(kc != 0))
                    C.op("vector", [pk, "dtbc"], ["zb"],
                         lambda e, m2=m2, pb=pb: e.tensor_scalar(
                             zb[:], pb[:], dtbc[:, m2:m2 + 1], None, op0=ALU.add))
                    C.op("scalar", ["zb"], ["ab"], lambda e: e.activation(ab[:], zb[:], AF.Abs))
                    C.op("scalar", ["ab"], ["ab"],
                         lambda e: e.activation(ab[:], ab[:], AF.Exp, scale=-1.0))
                    C.op("scalar", ["ab", "onecol"], ["lb"],
                         lambda e: e.activation(lb[:], ab[:], AF.Ln, bias=onecol[:]))
                    C.op("vector", ["zb"], ["mb"],
                         lambda e: e.tensor_scalar_max(mb[:], zb[:], 0.0))
                    C.op("vector", ["lb", "mb"], ["zb"],
                         lambda e: e.tensor_tensor(zb[:], lb[:], mb[:], op=ALU.add))
                    C.dma("gpsimd", ["zb"], ["dto"],
                          lambda e, m2=m2, nt=nt, th=th: e.dma_start(
                              out=dto[m2 * 128:(m2 + 1) * 128,
                                      th * 1024 + nt * 512: th * 1024 + (nt + 1) * 512],
                              in_=zb[:]))
            for wsb, bias, outap, okey in ((bwb, bbc, bmo, "bmo"), (cwb, cbc2, cmo, "cmo")):
                for nt in range(2):
                    for kc in range(16):
                        C.op("tensor", ["bwb", "cwb", "xcb"], ["pbc"],
                             lambda e, wsb=wsb, nt=nt, kc=kc: e.matmul(
                                 pbc[:], wsb[:, kc * DS:(kc + 1) * DS],
                                 xcb[:, kc * 1024 + nt * 512: kc * 1024 + (nt + 1) * 512],
                                 start=(kc == 0), stop=(kc == 15), skip_group_check=True),
                             accum=(kc != 0))
                    C.op("vector", ["pbc"], ["bm"],
                         lambda e, bias=bias: e.tensor_scalar(
                             bm[:], pbc[:], bias[:], None, op0=ALU.add))
                    C.dma("gpsimd", ["bm"], [okey],
                          lambda e, outap=outap, nt=nt, th=th: e.dma_start(
                              out=outap[:, th * 1024 + nt * 512: th * 1024 + (nt + 1) * 512],
                              in_=bm[:]))
        C.sync_on("gpsimd", ["xco", "dto", "sgo", "bmo", "cmo", "xn2o"])
        C.emit(block)
    return nc


def build_p6():
    nc = bass.Bass("TRN2", target_bir_lowering=False)
    dtd = nc.dram_tensor("dtd", [1024, T], F32, kind="ExternalInput").ap()
    xcd = nc.dram_tensor("xcd", [1024, T], F32, kind="ExternalInput").ap()
    sgd = nc.dram_tensor("sgd", [1024, T], F32, kind="ExternalInput").ap()
    bmd = nc.dram_tensor("bmd", [DS, T], F32, kind="ExternalInput").ap()
    cmd = nc.dram_tensor("cmd", [DS, T], F32, kind="ExternalInput").ap()
    acold = nc.dram_tensor("acold", [128, 128], F32, kind="ExternalInput").ap()
    dpd = nc.dram_tensor("dpd", [128, 8], F32, kind="ExternalInput").ap()
    owd = nc.dram_tensor("owd", [1024, DM], F32, kind="ExternalInput").ap()
    onesd = nc.dram_tensor("onesd", [1, 128], F32, kind="ExternalInput").ap()
    opo = nc.dram_tensor("opo", [DM, T], F32, kind="ExternalOutput").ap()
    CH = 512
    es = contextlib.ExitStack()
    with es:
        dt = _sb(nc, es, "dt", (128, 8 * CH), F32)
        xc = _sb(nc, es, "xc", (128, 8 * CH), F32)
        sg = _sb(nc, es, "sg", (128, 8 * CH), F32)
        bmr = _sb(nc, es, "bmr", (1, CH), F32)
        cmr = _sb(nc, es, "cmr", (1, CH), F32)
        Bbc = _sb(nc, es, "Bbc", (128, DS * CH), F32)
        Cbc = _sb(nc, es, "Cbc", (128, DS * CH), F32)
        acol = _sb(nc, es, "acol", (128, 128), F32)
        dpc = _sb(nc, es, "dpc", (128, 8), F32)
        ones1 = _sb(nc, es, "ones1", (1, 128), F32)
        hstate = _sb(nc, es, "hstate", (128, 128), F32)
        dA = [_sb(nc, es, f"dA{i}", (128, CH), F32) for i in range(2)]
        dBx = [_sb(nc, es, f"dBx{i}", (128, CH), F32) for i in range(2)]
        hh = [_sb(nc, es, f"hh{i}", (128, CH), F32) for i in range(2)]
        yt = [_sb(nc, es, f"yt{i}", (128, CH), F32) for i in range(2)]
        ycc = _sb(nc, es, "ycc", (128, CH), F32)
        ybf = _sb(nc, es, "ybf", (128, 8 * CH), BF16)
        owst = _sb(nc, es, "owst", (128, 4 * DM), F32)
        owb = _sb(nc, es, "owb", (128, 8 * DM), BF16)
        osb = _sb(nc, es, "osb", (128, CH), F32)
        pb0 = _ps(nc, es, "pb0", (128, 512))
        pb1 = _ps(nc, es, "pb1", (128, 512))
        C = Ctx(nc, es)
        block = es.enter_context(nc.Block())

        C.dma("sync", [], ["acol"], lambda e: e.dma_start(out=acol[:], in_=acold))
        C.dma("sync", [], ["dpc"], lambda e: e.dma_start(out=dpc[:], in_=dpd))
        C.dma("sync", [], ["ones1"], lambda e: e.dma_start(out=ones1[:], in_=onesd))
        for hf in range(2):
            C.dma("sync", [], ["owst"],
                  lambda e, hf=hf: e.dma_start(
                      out=owst[:].rearrange("p (cc m) -> p cc m", m=DM),
                      in_=owd.rearrange("(cc p) m -> p cc m", p=128)[:, hf * 4:(hf + 1) * 4, :]))
            C.op("gpsimd", ["owst"], ["owb"],
                 lambda e, hf=hf: e.tensor_copy(owb[:, hf * 4 * DM:(hf + 1) * 4 * DM], owst[:]))
        C.op("vector", [], ["hstate"], lambda e: e.memset(hstate[:], 0.0))
        for ch in range(8):
            t0 = ch * CH
            C.dma("sync", [], ["dt"],
                  lambda e, t0=t0: e.dma_start(
                      out=dt[:].rearrange("p (cc t) -> p cc t", t=CH),
                      in_=dtd[:, t0:t0 + CH].rearrange("(cc p) t -> p cc t", p=128)))
            C.dma("sync", [], ["xc"],
                  lambda e, t0=t0: e.dma_start(
                      out=xc[:].rearrange("p (cc t) -> p cc t", t=CH),
                      in_=xcd[:, t0:t0 + CH].rearrange("(cc p) t -> p cc t", p=128)))
            C.dma("sync", [], ["sg"],
                  lambda e, t0=t0: e.dma_start(
                      out=sg[:].rearrange("p (cc t) -> p cc t", t=CH),
                      in_=sgd[:, t0:t0 + CH].rearrange("(cc p) t -> p cc t", p=128)))
            for n in range(DS):
                pb = pb0 if n % 2 == 0 else pb1
                pk = f"pb{n % 2}"
                C.dma("sync", [], ["bmr"],
                      lambda e, t0=t0, n=n: e.dma_start(
                          out=bmr[:], in_=bmd[n:n + 1, t0:t0 + CH]))
                C.op("tensor", ["bmr", "ones1"], [pk],
                     lambda e, pb=pb: e.matmul(
                         pb[:], ones1[:], bmr[:], start=True, stop=True, skip_group_check=True))
                C.op("vector", [pk], ["Bbc"],
                     lambda e, n=n, pb=pb: e.tensor_copy(Bbc[:, n * CH:(n + 1) * CH], pb[:]))
                C.dma("sync", [], ["cmr"],
                      lambda e, t0=t0, n=n: e.dma_start(
                          out=cmr[:], in_=cmd[n:n + 1, t0:t0 + CH]))
                C.op("tensor", ["cmr", "ones1"], [pk],
                     lambda e, pb=pb: e.matmul(
                         pb[:], ones1[:], cmr[:], start=True, stop=True, skip_group_check=True))
                C.op("vector", [pk], ["Cbc"],
                     lambda e, n=n, pb=pb: e.tensor_copy(Cbc[:, n * CH:(n + 1) * CH], pb[:]))
            for cc in range(8):
                C.op("vector", ["xc", "dpc"], ["ycc"],
                     lambda e, cc=cc: e.tensor_scalar(
                         ycc[:], xc[:, cc * CH:(cc + 1) * CH], dpc[:, cc:cc + 1],
                         None, op0=ALU.mult))
                for n in range(DS):
                    i = n % 2
                    kda, kdb, kh, kyt = f"dA{i}", f"dBx{i}", f"hh{i}", f"yt{i}"
                    col = cc * 16 + n
                    C.op("scalar", ["dt", "acol"], [kda],
                         lambda e, cc=cc, col=col, i=i: e.activation(
                             dA[i][:], dt[:, cc * CH:(cc + 1) * CH], AF.Exp,
                             scale=acol[:, col:col + 1]))
                    C.op("gpsimd", ["dt", "Bbc"], [kdb],
                         lambda e, cc=cc, n=n, i=i: e.tensor_tensor(
                             dBx[i][:], dt[:, cc * CH:(cc + 1) * CH],
                             Bbc[:, n * CH:(n + 1) * CH], op=ALU.mult))
                    C.op("gpsimd", ["xc", kdb], [kdb],
                         lambda e, cc=cc, i=i: e.tensor_tensor(
                             dBx[i][:], dBx[i][:], xc[:, cc * CH:(cc + 1) * CH], op=ALU.mult))
                    C.op("vector", [kda, kdb, "hstate"], [kh],
                         lambda e, col=col, i=i: e.tensor_tensor_scan(
                             hh[i][:], dA[i][:], dBx[i][:], hstate[:, col:col + 1],
                             op0=ALU.mult, op1=ALU.add))
                    C.op("vector", [kh], ["hstate"],
                         lambda e, col=col, i=i: e.tensor_copy(
                             hstate[:, col:col + 1], hh[i][:, CH - 1:CH]))
                    C.op("gpsimd", [kh, "Cbc"], [kyt],
                         lambda e, n=n, i=i: e.tensor_tensor(
                             yt[i][:], hh[i][:], Cbc[:, n * CH:(n + 1) * CH], op=ALU.mult))
                    C.op("vector", [kyt, "ycc"], ["ycc"],
                         lambda e, i=i: e.tensor_tensor(ycc[:], ycc[:], yt[i][:], op=ALU.add))
                C.op("vector", ["ycc", "sg"], ["ycc"],
                     lambda e, cc=cc: e.tensor_tensor(
                         ycc[:], ycc[:], sg[:, cc * CH:(cc + 1) * CH], op=ALU.mult))
                C.op("gpsimd", ["ycc"], ["ybf"],
                     lambda e, cc=cc: e.tensor_copy(ybf[:, cc * CH:(cc + 1) * CH], ycc[:]))
            for m in range(8):
                pb = pb0 if m % 2 == 0 else pb1
                pk = f"pb{m % 2}"
                for cc in range(8):
                    C.op("tensor", ["owb", "ybf"], [pk],
                         lambda e, m=m, cc=cc, pb=pb: e.matmul(
                             pb[:], owb[:, cc * DM + m * 128: cc * DM + (m + 1) * 128],
                             ybf[:, cc * CH:(cc + 1) * CH],
                             start=(cc == 0), stop=(cc == 7), skip_group_check=True),
                         accum=(cc != 0))
                C.op("vector", [pk], ["osb"], lambda e, pb=pb: e.tensor_copy(osb[:], pb[:]))
                C.dma("gpsimd", ["osb"], ["opo"],
                      lambda e, m=m, t0=t0: e.dma_start(
                          out=opo[m * 128:(m + 1) * 128, t0:t0 + CH], in_=osb[:]))
        C.sync_on("gpsimd", ["opo"])
        C.emit(block)
    return nc


def build_p7():
    nc = bass.Bass("TRN2", target_bir_lowering=False)
    x1 = nc.dram_tensor("x1", [TPC, DM], F32, kind="ExternalInput").ap()
    xn2 = nc.dram_tensor("xn2", [DM, TPC], F32, kind="ExternalInput").ap()
    opa = nc.dram_tensor("opa", [DM, TPC], F32, kind="ExternalInput").ap()
    opb = nc.dram_tensor("opb", [DM, TPC], F32, kind="ExternalInput").ap()
    outbd = nc.dram_tensor("outbd", [128, 8], F32, kind="ExternalInput").ap()
    gwd = nc.dram_tensor("gwd", [DM, NE], F32, kind="ExternalInput").ap()
    gbd = nc.dram_tensor("gbd", [1, NE], F32, kind="ExternalInput").ap()
    onesd = nc.dram_tensor("onesd", [1, 128], F32, kind="ExternalInput").ap()
    eyed = nc.dram_tensor("eyed", [128, 128], F32, kind="ExternalInput").ap()
    base = nc.dram_tensor("base", [DM, TPC], F32, kind="ExternalOutput").ap()
    cto = nc.dram_tensor("cto", [NE, TPC], F32, kind="ExternalOutput").ap()
    xnbo = nc.dram_tensor("xnbo", [DM, TPC], BF16, kind="ExternalOutput").ap()
    es = contextlib.ExitStack()
    with es:
        x1T = _sb(nc, es, "x1T", (128, 8 * TPC), F32)
        xnb = _sb(nc, es, "xnb", (128, 8 * TPC), BF16)
        xtile = _sb(nc, es, "xtile", (128, DM), F32)
        xstage = _sb(nc, es, "xstage", (128, TPC), F32)
        ostage = _sb(nc, es, "ostage", (128, TPC), F32)
        outbc = _sb(nc, es, "outbc", (128, 8), F32)
        gw = _sb(nc, es, "gw", (128, 8 * NE), F32)
        gwb = _sb(nc, es, "gwb", (128, 8 * NE), BF16)
        gb = _sb(nc, es, "gb", (1, NE), F32)
        gbb = _sb(nc, es, "gbb", (1, NE), BF16)
        ones1 = _sb(nc, es, "ones1", (1, 128), F32)
        onesb = _sb(nc, es, "onesb", (1, 128), BF16)
        eye = _sb(nc, es, "eye", (128, 128), F32)
        s = _sb(nc, es, "s", (128, NE), F32)
        eq1 = _sb(nc, es, "eq1", (128, NE), F32)
        s2 = _sb(nc, es, "s2", (128, NE), F32)
        eq2 = _sb(nc, es, "eq2", (128, NE), F32)
        m1 = _sb(nc, es, "m1", (128, 1), F32)
        m2 = _sb(nc, es, "m2", (128, 1), F32)
        dd = _sb(nc, es, "dd", (128, 1), F32)
        ed = _sb(nc, es, "ed", (128, 1), F32)
        p1c = _sb(nc, es, "p1c", (128, 1), F32)
        p2c = _sb(nc, es, "p2c", (128, 1), F32)
        comb = _sb(nc, es, "comb", (128, NE), F32)
        ctsb = _sb(nc, es, "ctsb", (NE, TPC), F32)
        ptr = _ps(nc, es, "ptr", (128, 128))
        pgs = _ps(nc, es, "pgs", (128, NE))
        C = Ctx(nc, es)
        block = es.enter_context(nc.Block())

        C.dma("sync", [], ["outbc"], lambda e: e.dma_start(out=outbc[:], in_=outbd))
        C.dma("sync", [], ["ones1"], lambda e: e.dma_start(out=ones1[:], in_=onesd))
        C.dma("sync", [], ["eye"], lambda e: e.dma_start(out=eye[:], in_=eyed))
        C.dma("sync", [], ["gb"], lambda e: e.dma_start(out=gb[:], in_=gbd))
        C.dma("sync", [], ["gw"],
              lambda e: e.dma_start(out=gw[:].rearrange("p (kc m) -> p kc m", m=NE),
                                    in_=gwd.rearrange("(kc p) m -> p kc m", p=128)))
        C.op("gpsimd", ["gw"], ["gwb"], lambda e: e.tensor_copy(gwb[:], gw[:]))
        C.op("gpsimd", ["gb"], ["gbb"], lambda e: e.tensor_copy(gbb[:], gb[:]))
        C.op("gpsimd", ["ones1"], ["onesb"], lambda e: e.tensor_copy(onesb[:], ones1[:]))
        for tc in range(16):
            C.dma("sync", [], ["xtile"],
                  lambda e, tc=tc: e.dma_start(out=xtile[:], in_=x1[tc * 128:(tc + 1) * 128, :]))
            for kc in range(8):
                C.op("tensor", ["xtile", "eye"], ["ptr"],
                     lambda e, kc=kc: e.transpose(ptr[:], xtile[:, kc * 128:(kc + 1) * 128],
                                                  eye[:]))
                C.op("vector", ["ptr"], ["x1T"],
                     lambda e, kc=kc, tc=tc: e.tensor_copy(
                         x1T[:, kc * TPC + tc * 128: kc * TPC + (tc + 1) * 128], ptr[:]))
        for m in range(8):
            C.dma("sync", [], ["xstage"],
                  lambda e, m=m: e.dma_start(out=xstage[:], in_=xn2[m * 128:(m + 1) * 128, :]))
            C.op("gpsimd", ["xstage"], ["xnb"],
                 lambda e, m=m: e.tensor_copy(xnb[:, m * TPC:(m + 1) * TPC], xstage[:]))
            C.op("vector", ["xstage", "x1T"], ["xstage"],
                 lambda e, m=m: e.tensor_tensor(
                     xstage[:], xstage[:], x1T[:, m * TPC:(m + 1) * TPC], op=ALU.add))
            C.dma("sync", [], ["ostage"],
                  lambda e, m=m: e.dma_start(out=ostage[:], in_=opa[m * 128:(m + 1) * 128, :]))
            C.op("vector", ["xstage", "ostage"], ["xstage"],
                 lambda e: e.tensor_tensor(xstage[:], xstage[:], ostage[:], op=ALU.add))
            C.dma("sync", [], ["ostage"],
                  lambda e, m=m: e.dma_start(out=ostage[:], in_=opb[m * 128:(m + 1) * 128, :]))
            C.op("vector", ["xstage", "ostage"], ["xstage"],
                 lambda e: e.tensor_tensor(xstage[:], xstage[:], ostage[:], op=ALU.add))
            C.op("vector", ["xstage", "outbc"], ["xstage"],
                 lambda e, m=m: e.tensor_scalar(xstage[:], xstage[:], outbc[:, m:m + 1],
                                                None, op0=ALU.add))
            C.dma("gpsimd", ["xstage"], ["base"],
                  lambda e, m=m: e.dma_start(out=base[m * 128:(m + 1) * 128, :], in_=xstage[:]))
        C.dma("gpsimd", ["xnb"], ["xnbo"],
              lambda e: e.dma_start(out=xnbo.rearrange("(m p) t -> p m t", p=128),
                                    in_=xnb[:].rearrange("p (m t) -> p m t", t=TPC)))
        for tc in range(16):
            for kc in range(8):
                C.op("tensor", ["gwb", "xnb"], ["pgs"],
                     lambda e, tc=tc, kc=kc: e.matmul(
                         pgs[:], xnb[:, kc * TPC + tc * 128: kc * TPC + (tc + 1) * 128],
                         gwb[:, kc * NE:(kc + 1) * NE],
                         start=(kc == 0), stop=False, skip_group_check=True),
                     accum=(kc != 0))
            C.op("tensor", ["gbb", "onesb"], ["pgs"],
                 lambda e: e.matmul(pgs[:], onesb[:], gbb[:], start=False, stop=True, skip_group_check=True),
                 accum=True)
            C.op("vector", ["pgs"], ["s"], lambda e: e.tensor_copy(s[:], pgs[:]))
            C.op("vector", ["s"], ["m1"], lambda e: e.reduce_max(m1[:], s[:], axis=AX.X))
            C.op("vector", ["s", "m1"], ["eq1"],
                 lambda e: e.tensor_scalar(eq1[:], s[:], m1[:], None, op0=ALU.is_ge))
            C.op("vector", ["eq1", "s"], ["s2"],
                 lambda e: e.scalar_tensor_tensor(s2[:], eq1[:], -1e30, s[:],
                                                  op0=ALU.mult, op1=ALU.add))
            C.op("vector", ["s2"], ["m2"], lambda e: e.reduce_max(m2[:], s2[:], axis=AX.X))
            C.op("vector", ["s2", "m2"], ["eq2"],
                 lambda e: e.tensor_scalar(eq2[:], s2[:], m2[:], None, op0=ALU.is_ge))
            C.op("vector", ["m1", "m2"], ["dd"],
                 lambda e: e.tensor_tensor(dd[:], m2[:], m1[:], op=ALU.subtract))
            C.op("scalar", ["dd"], ["ed"], lambda e: e.activation(ed[:], dd[:], AF.Exp))
            C.op("vector", ["ed"], ["p1c"], lambda e: e.tensor_scalar_add(p1c[:], ed[:], 1.0))
            C.op("vector", ["p1c"], ["p1c"], lambda e: e.reciprocal(p1c[:], p1c[:]))
            C.op("vector", ["ed", "p1c"], ["p2c"],
                 lambda e: e.tensor_tensor(p2c[:], ed[:], p1c[:], op=ALU.mult))
            C.op("vector", ["eq1", "p1c"], ["eq1"],
                 lambda e: e.tensor_scalar(eq1[:], eq1[:], p1c[:], None, op0=ALU.mult))
            C.op("vector", ["eq2", "p2c", "eq1"], ["comb"],
                 lambda e: e.scalar_tensor_tensor(comb[:], eq2[:], p2c[:], eq1[:],
                                                  op0=ALU.mult, op1=ALU.add))
            C.op("tensor", ["comb", "eye"], ["ptr"],
                 lambda e: e.transpose(ptr[0:NE, :], comb[:], eye[:]))
            C.op("vector", ["ptr"], ["ctsb"],
                 lambda e, tc=tc: e.tensor_copy(ctsb[:, tc * 128:(tc + 1) * 128], ptr[0:NE, :]))
        C.dma("gpsimd", ["ctsb"], ["cto"], lambda e: e.dma_start(out=cto, in_=ctsb[:]))
        C.sync_on("gpsimd", ["base", "cto", "xnbo"])
        C.emit(block)
    return nc


def build_p8():
    nc = bass.Bass("TRN2", target_bir_lowering=False)
    xnb_d = nc.dram_tensor("xnb_d", [DM, TPC], BF16, kind="ExternalInput").ap()
    ctd = nc.dram_tensor("ctd", [1, TPC], F32, kind="ExternalInput").ap()
    accin = nc.dram_tensor("accin", [DM, TPC], F32, kind="ExternalInput").ap()
    w1d = nc.dram_tensor("w1d", [DM, DF], F32, kind="ExternalInput").ap()
    b1d = nc.dram_tensor("b1d", [128, 32], F32, kind="ExternalInput").ap()
    w2d = nc.dram_tensor("w2d", [DF, DM], F32, kind="ExternalInput").ap()
    b2d = nc.dram_tensor("b2d", [128, 8], F32, kind="ExternalInput").ap()
    onesd = nc.dram_tensor("onesd", [1, 128], F32, kind="ExternalInput").ap()
    accout = nc.dram_tensor("accout", [DM, TPC], F32, kind="ExternalOutput").ap()
    es = contextlib.ExitStack()
    with es:
        w1b = _sb(nc, es, "w1b", (128, 8 * DF), BF16)
        w2b = _sb(nc, es, "w2b", (128, 32 * DM), BF16)
        stage = _sb(nc, es, "stage", (128, 2048), F32)
        xnb = _sb(nc, es, "xnb", (128, 8 * 512), BF16)
        hT = _sb(nc, es, "hT", (128, 32 * 512), BF16)
        b1c = _sb(nc, es, "b1c", (128, 32), F32)
        b2c = _sb(nc, es, "b2c", (128, 8), F32)
        ctrow = _sb(nc, es, "ctrow", (1, TPC), F32)
        ones1 = _sb(nc, es, "ones1", (1, 128), F32)
        cbc = _sb(nc, es, "cbc", (128, 512), F32)
        acct = _sb(nc, es, "acct", (128, 512), F32)
        scry = _sb(nc, es, "scry", (128, 512), F32)
        ph = _ps(nc, es, "ph", (128, 512))
        py0 = _ps(nc, es, "py0", (128, 512))
        py1 = _ps(nc, es, "py1", (128, 512))
        pc = _ps(nc, es, "pc", (128, 512))
        C = Ctx(nc, es)
        block = es.enter_context(nc.Block())

        C.dma("sync", [], ["b1c"], lambda e: e.dma_start(out=b1c[:], in_=b1d))
        C.dma("sync", [], ["b2c"], lambda e: e.dma_start(out=b2c[:], in_=b2d))
        C.dma("sync", [], ["ctrow"], lambda e: e.dma_start(out=ctrow[:], in_=ctd))
        C.dma("sync", [], ["ones1"], lambda e: e.dma_start(out=ones1[:], in_=onesd))
        for ch in range(16):
            C.dma("sync", [], ["stage"],
                  lambda e, ch=ch: e.dma_start(
                      out=stage[:].rearrange("p (kc m) -> p kc m", m=256),
                      in_=w1d[:, ch * 256:(ch + 1) * 256].rearrange("(kc p) m -> p kc m", p=128)))
            C.op("gpsimd", ["stage"], ["w1b"],
                 lambda e, ch=ch: e.tensor_copy(
                     w1b[:].rearrange("p (kc f) -> p kc f", f=DF)[:, :, ch * 256:(ch + 1) * 256],
                     stage[:].rearrange("p (kc m) -> p kc m", m=256)))
        for ch in range(16):
            C.dma("sync", [], ["stage"],
                  lambda e, ch=ch: e.dma_start(
                      out=stage[:].rearrange("p (fc m) -> p fc m", m=DM),
                      in_=w2d[ch * 256:(ch + 1) * 256, :].rearrange("(fc p) m -> p fc m", p=128)))
            C.op("gpsimd", ["stage"], ["w2b"],
                 lambda e, ch=ch: e.tensor_copy(
                     w2b[:, ch * 2 * DM:(ch + 1) * 2 * DM], stage[:]))
        for tt in range(4):
            C.dma("sync", [], ["xnb"],
                  lambda e, tt=tt: e.dma_start(
                      out=xnb[:].rearrange("p (kc t) -> p kc t", t=512),
                      in_=xnb_d[:, tt * 512:(tt + 1) * 512].rearrange("(kc p) t -> p kc t",
                                                                     p=128)))
            C.op("tensor", ["ctrow", "ones1"], ["pc"],
                 lambda e, tt=tt: e.matmul(pc[:], ones1[:],
                                           ctrow[:, tt * 512:(tt + 1) * 512],
                                           start=True, stop=True, skip_group_check=True))
            C.op("vector", ["pc"], ["cbc"], lambda e: e.tensor_copy(cbc[:], pc[:]))
            for fc in range(32):
                for kc in range(8):
                    C.op("tensor", ["w1b", "xnb"], ["ph"],
                         lambda e, fc=fc, kc=kc: e.matmul(
                             ph[:], w1b[:, kc * DF + fc * 128: kc * DF + (fc + 1) * 128],
                             xnb[:, kc * 512:(kc + 1) * 512],
                             start=(kc == 0), stop=(kc == 7), skip_group_check=True),
                         accum=(kc != 0))
                C.op("scalar", ["ph", "b1c"], ["hT"],
                     lambda e, fc=fc: e.activation(hT[:, fc * 512:(fc + 1) * 512], ph[:],
                                                   AF.Gelu, bias=b1c[:, fc:fc + 1]))
            for m in range(8):
                pb = py0 if m % 2 == 0 else py1
                pk = f"py{m % 2}"
                for fc in range(32):
                    C.op("tensor", ["w2b", "hT"], [pk],
                         lambda e, m=m, fc=fc, pb=pb: e.matmul(
                             pb[:], w2b[:, fc * DM + m * 128: fc * DM + (m + 1) * 128],
                             hT[:, fc * 512:(fc + 1) * 512],
                             start=(fc == 0), stop=(fc == 31), skip_group_check=True),
                         accum=(fc != 0))
                C.dma("sync", [], ["acct"],
                      lambda e, m=m, tt=tt: e.dma_start(
                          out=acct[:], in_=accin[m * 128:(m + 1) * 128, tt * 512:(tt + 1) * 512]))
                C.op("vector", [pk, "b2c"], ["scry"],
                     lambda e, m=m, pb=pb: e.tensor_scalar(
                         scry[:], pb[:], b2c[:, m:m + 1], None, op0=ALU.add))
                C.op("vector", ["scry", "cbc"], ["scry"],
                     lambda e: e.tensor_tensor(scry[:], scry[:], cbc[:], op=ALU.mult))
                C.op("vector", ["scry", "acct"], ["scry"],
                     lambda e: e.tensor_tensor(scry[:], scry[:], acct[:], op=ALU.add))
                C.dma("gpsimd", ["scry"], ["accout"],
                      lambda e, m=m, tt=tt: e.dma_start(
                          out=accout[m * 128:(m + 1) * 128, tt * 512:(tt + 1) * 512],
                          in_=scry[:]))
        C.sync_on("gpsimd", ["accout"])
        C.emit(block)
    return nc



GEXP = 6


def build_moe():
    nc = bass.Bass("TRN2", target_bir_lowering=False)
    xnb_d = nc.dram_tensor("xnb_d", [DM, TPC], BF16, kind="ExternalInput").ap()
    ctg = nc.dram_tensor("ctg", [GEXP, TPC], F32, kind="ExternalInput").ap()
    accin = nc.dram_tensor("accin", [DM, TPC], F32, kind="ExternalInput").ap()
    w1g = nc.dram_tensor("w1g", [GEXP, DM, DF], F32, kind="ExternalInput").ap()
    b1g = nc.dram_tensor("b1g", [128, GEXP * 32], F32, kind="ExternalInput").ap()
    w2g = nc.dram_tensor("w2g", [GEXP, DF, DM], F32, kind="ExternalInput").ap()
    b2g = nc.dram_tensor("b2g", [128, GEXP * 8], F32, kind="ExternalInput").ap()
    onesd = nc.dram_tensor("onesd", [1, 128], F32, kind="ExternalInput").ap()
    accout = nc.dram_tensor("accout", [DM, TPC], F32, kind="ExternalOutput").ap()
    es = contextlib.ExitStack()
    with es:
        w1b = _sb(nc, es, "w1b", (128, 8 * DF), BF16)
        w2b = _sb(nc, es, "w2b", (128, 32 * DM), BF16)
        stage = [_sb(nc, es, f"stage{i}", (128, 1024), F32) for i in range(2)]
        xnb = _sb(nc, es, "xnb", (128, 8 * 512), BF16)
        hT = _sb(nc, es, "hT", (128, 32 * 512), BF16)
        b1c = _sb(nc, es, "b1c", (128, GEXP * 32), F32)
        b2c = _sb(nc, es, "b2c", (128, GEXP * 8), F32)
        ctrow = _sb(nc, es, "ctrow", (1, 512), F32)
        ones1 = _sb(nc, es, "ones1", (1, 128), F32)
        cbc = _sb(nc, es, "cbc", (128, 512), F32)
        acct = _sb(nc, es, "acct", (128, 512), F32)
        scry = _sb(nc, es, "scry", (128, 512), F32)
        ph0 = _ps(nc, es, "ph0", (128, 512))
        ph1 = _ps(nc, es, "ph1", (128, 512))
        py0 = _ps(nc, es, "py0", (128, 512))
        py1 = _ps(nc, es, "py1", (128, 512))
        pc = _ps(nc, es, "pc", (128, 512))
        C = Ctx(nc, es)
        block = es.enter_context(nc.Block())

        C.dma("sync", [], ["b1c"], lambda e: e.dma_start(out=b1c[:], in_=b1g))
        C.dma("sync", [], ["b2c"], lambda e: e.dma_start(out=b2c[:], in_=b2g))
        C.dma("sync", [], ["ones1"], lambda e: e.dma_start(out=ones1[:], in_=onesd))
        for ex in range(GEXP):
            for ch in range(32):
                sg = stage[ch % 2]
                sk = f"stage{ch % 2}"
                C.dma("sync", [], [sk],
                      lambda e, ex=ex, ch=ch, sg=sg: e.dma_start(
                          out=sg[:].rearrange("p (kc m) -> p kc m", m=128),
                          in_=w1g[ex, :, ch * 128:(ch + 1) * 128]
                          .rearrange("(kc p) m -> p kc m", p=128)))
                C.op("vector", [sk], ["w1bA" if ch < 16 else "w1bB"],
                     lambda e, ch=ch, sg=sg: e.tensor_copy(
                         w1b[:].rearrange("p (kc f) -> p kc f", f=DF)
                         [:, :, ch * 128:(ch + 1) * 128],
                         sg[:].rearrange("p (kc m) -> p kc m", m=128)))
            for ch in range(32):
                sg = stage[ch % 2]
                sk = f"stage{ch % 2}"
                C.dma("sync", [], [sk],
                      lambda e, ex=ex, ch=ch, sg=sg: e.dma_start(
                          out=sg[:],
                          in_=w2g[ex, ch * 128:(ch + 1) * 128, :]
                          .rearrange("(o p) m -> p (o m)", p=128)))
                C.op("gpsimd", [sk], ["w2bA" if ch < 16 else "w2bB"],
                     lambda e, ch=ch, sg=sg: e.tensor_copy(
                         w2b[:, ch * DM:(ch + 1) * DM], sg[:]))
            for tt in range(4):
                C.dma("scalar", [], ["xnb"],
                      lambda e, tt=tt: e.dma_start(
                          out=xnb[:].rearrange("p (kc t) -> p kc t", t=512),
                          in_=xnb_d[:, tt * 512:(tt + 1) * 512]
                          .rearrange("(kc p) t -> p kc t", p=128)))
                C.dma("scalar", [], ["ctrow"],
                      lambda e, ex=ex, tt=tt: e.dma_start(
                          out=ctrow[:], in_=ctg[ex:ex + 1, tt * 512:(tt + 1) * 512]))
                C.op("tensor", ["ctrow", "ones1"], ["pc"],
                     lambda e: e.matmul(pc[:], ones1[:], ctrow[:],
                                        start=True, stop=True, skip_group_check=True))
                C.op("vector", ["pc"], ["cbc"], lambda e: e.tensor_copy(cbc[:], pc[:]))
                for fc in range(32):
                    ph = ph0 if fc % 2 == 0 else ph1
                    phk = f"ph{fc % 2}"
                    for kc in range(8):
                        C.op("tensor", ["w1bA" if fc < 16 else "w1bB", "xnb"], [phk],
                             lambda e, fc=fc, kc=kc, ph=ph: e.matmul(
                                 ph[:], w1b[:, kc * DF + fc * 128: kc * DF + (fc + 1) * 128],
                                 xnb[:, kc * 512:(kc + 1) * 512],
                                 start=(kc == 0), stop=(kc == 7), skip_group_check=True),
                             accum=(kc != 0))
                    C.op("scalar", [phk, "b1c"], ["hT"],
                         lambda e, ex=ex, fc=fc, ph=ph: e.activation(
                             hT[:, fc * 512:(fc + 1) * 512], ph[:], AF.Gelu,
                             bias=b1c[:, ex * 32 + fc: ex * 32 + fc + 1]))
                for m in range(8):
                    pb = py0 if m % 2 == 0 else py1
                    pk = f"py{m % 2}"
                    for fc in range(32):
                        C.op("tensor", ["w2bA" if fc < 16 else "w2bB", "hT"], [pk],
                             lambda e, m=m, fc=fc, pb=pb: e.matmul(
                                 pb[:], w2b[:, fc * DM + m * 128: fc * DM + (m + 1) * 128],
                                 hT[:, fc * 512:(fc + 1) * 512],
                                 start=(fc == 0), stop=(fc == 31), skip_group_check=True),
                             accum=(fc != 0))
                    ak = f"acc_{m}_{tt}"
                    asrc = accin if ex == 0 else accout
                    C.dma("scalar", [ak], ["acct"],
                          lambda e, m=m, tt=tt, asrc=asrc: e.dma_start(
                              out=acct[:],
                              in_=asrc[m * 128:(m + 1) * 128, tt * 512:(tt + 1) * 512]))
                    C.op("vector", [pk, "b2c"], ["scry"],
                         lambda e, ex=ex, m=m, pb=pb: e.tensor_scalar(
                             scry[:], pb[:], b2c[:, ex * 8 + m: ex * 8 + m + 1],
                             None, op0=ALU.add))
                    C.op("vector", ["scry", "cbc"], ["scry"],
                         lambda e: e.tensor_tensor(scry[:], scry[:], cbc[:], op=ALU.mult))
                    C.op("vector", ["scry", "acct"], ["scry"],
                         lambda e: e.tensor_tensor(scry[:], scry[:], acct[:], op=ALU.add))
                    C.dma("gpsimd", ["scry"], [ak],
                          lambda e, m=m, tt=tt: e.dma_start(
                              out=accout[m * 128:(m + 1) * 128, tt * 512:(tt + 1) * 512],
                              in_=scry[:]))
        C.sync_on("gpsimd", [f"acc_{m}_{tt}" for m in range(8) for tt in range(4)])
        C.emit(block)
    return nc


def build_p9():
    nc = bass.Bass("TRN2", target_bir_lowering=False)
    accd = nc.dram_tensor("accd", [DM, TPC], F32, kind="ExternalInput").ap()
    eyed = nc.dram_tensor("eyed", [128, 128], F32, kind="ExternalInput").ap()
    outd = nc.dram_tensor("outd", [TPC, DM], F32, kind="ExternalOutput").ap()
    es = contextlib.ExitStack()
    with es:
        accs = _sb(nc, es, "accs", (128, 8 * TPC), F32)
        eye = _sb(nc, es, "eye", (128, 128), F32)
        otile = _sb(nc, es, "otile", (128, DM), F32)
        ptr = _ps(nc, es, "ptr", (128, 128))
        C = Ctx(nc, es)
        block = es.enter_context(nc.Block())
        C.dma("sync", [], ["eye"], lambda e: e.dma_start(out=eye[:], in_=eyed))
        C.dma("sync", [], ["accs"],
              lambda e: e.dma_start(out=accs[:].rearrange("p (m t) -> p m t", t=TPC),
                                    in_=accd.rearrange("(m p) t -> p m t", p=128)))
        for tc in range(16):
            for m in range(8):
                C.op("tensor", ["accs", "eye"], ["ptr"],
                     lambda e, tc=tc, m=m: e.transpose(
                         ptr[:], accs[:, m * TPC + tc * 128: m * TPC + (tc + 1) * 128], eye[:]))
                C.op("vector", ["ptr"], ["otile"],
                     lambda e, m=m: e.tensor_copy(otile[:, m * 128:(m + 1) * 128], ptr[:]))
            C.dma("gpsimd", ["otile"], ["outd"],
                  lambda e, tc=tc: e.dma_start(out=outd[tc * 128:(tc + 1) * 128, :], in_=otile[:]))
        C.sync_on("gpsimd", ["outd"])
        C.emit(block)
    return nc


def _run(nc, in_maps):
    return run_bass_kernel_spmd(nc, in_maps, core_ids=list(range(NCORES))).results


def kernel(**inputs):
    f32 = np.float32
    cont = np.ascontiguousarray
    eye = np.eye(128, dtype=f32)
    eps = np.full((128, 1), 1e-5, f32)
    one = np.full((128, 1), 1.0, f32)
    ones1 = np.ones((1, 128), f32)

    x_full = np.asarray(inputs["L0_attn_x"], f32).reshape(NB * T, DM)

    inv = 10000.0 ** (-(np.arange(32, dtype=np.float64) / 32.0))
    ang = np.arange(T, dtype=np.float64)[:, None] * inv[None, :]
    cosT = np.cos(ang).astype(f32)
    sinT = np.sin(ang).astype(f32)

    r = np.arange(DM)
    qcol = ((r % 512) // 32) * 64 + (r // 512) * 32 + (r % 32)
    wq_r = cont(np.asarray(inputs["L0_attn_Wq"], f32)[:, qcol])
    wk_r = cont(np.asarray(inputs["L0_attn_Wk"], f32)[:, qcol])
    wv = cont(np.asarray(inputs["L0_attn_Wv"], f32))
    wo = cont(np.asarray(inputs["L0_attn_Wout"], f32))

    nc1 = build_p1()
    maps = []
    for c in range(NCORES):
        b, half = c // 2, c % 2
        t0 = half * TPC
        cosd = cont(np.broadcast_to(cosT[t0:t0 + TPC, :].T, (32, TPC))[np.arange(128) % 32])
        sind = cont(np.broadcast_to(sinT[t0:t0 + TPC, :].T, (32, TPC))[np.arange(128) % 32])
        maps.append({"x": cont(x_full[b * T + t0: b * T + t0 + TPC]),
                     "wq": wq_r, "wk": wk_r, "wv": wv,
                     "cosd": cosd, "sind": sind, "eyed": eye, "epsd": eps})
    r1 = _run(nc1, maps)

    W1 = np.asarray(inputs["L0_ffn_W1"], f32)
    W2 = np.asarray(inputs["L0_ffn_W2"], f32)
    w1f = cont(W1.transpose(2, 3, 0, 1).reshape(DM, 32, 128).reshape(DM, 4096))
    w2f = cont(W2.transpose(3, 0, 1, 2).reshape(32, 128, DM).reshape(4096, DM))
    wcf = cont(np.asarray(inputs["L0_ffn_Wc"], f32).reshape(DM, 128))
    bcb = cont(np.asarray(inputs["L0_ffn_bc"], f32).reshape(128, 1))
    b1f = cont(np.asarray(inputs["L0_ffn_b1"], f32).reshape(128, 32))
    b2f = cont(np.asarray(inputs["L0_ffn_b2"], f32).reshape(128, DM))
    tie_row = np.linspace(0.0, 1e-6, 32, dtype=f32)[np.arange(512) % 32]
    tied = cont(np.broadcast_to(tie_row[None, :], (128, 512)))
    nc2 = build_p2()
    maps = [{"xnt": r1[c]["xnt"], "w1f": w1f, "w2f": w2f, "wcf": wcf, "bcb": bcb,
             "tied": tied, "b1fd": b1f, "b2fd": b2f} for c in range(NCORES)]
    r2 = _run(nc2, maps)

    QT = [np.concatenate([r1[2 * b]["qt"], r1[2 * b + 1]["qt"]], axis=1) for b in range(NB)]
    KT = [np.concatenate([r1[2 * b]["kt"], r1[2 * b + 1]["kt"]], axis=1) for b in range(NB)]
    VF = [np.concatenate([r1[2 * b]["vout"], r1[2 * b + 1]["vout"]], axis=0) for b in range(NB)]
    nc3 = build_p3()
    maps = []
    for c in range(NCORES):
        b = c // 2
        heads = range(0, 8) if c % 2 == 0 else range(8, 16)
        rows = np.concatenate([np.r_[h * 32:(h + 1) * 32, 512 + h * 32: 512 + (h + 1) * 32]
                               for h in heads])
        cols = np.concatenate([np.arange(h * 64, (h + 1) * 64) for h in heads])
        maps.append({"qin": cont(QT[b][rows]), "kin": cont(KT[b][rows]),
                     "vin": cont(VF[b][:, cols]), "eyed": eye})
    r3 = _run(nc3, maps)

    AO = [np.concatenate([r3[2 * b]["aout"], r3[2 * b + 1]["aout"]], axis=0) for b in range(NB)]
    nc4 = build_p4()
    maps = []
    for c in range(NCORES):
        b, half = c // 2, c % 2
        t0 = half * TPC
        maps.append({"x": cont(x_full[b * T + t0: b * T + t0 + TPC]),
                     "f0": r2[c]["f0"], "ao": cont(AO[b][:, t0:t0 + TPC]), "wo": wo})
    r4 = _run(nc4, maps)

    X1 = [np.concatenate([r4[2 * b]["x1"], r4[2 * b + 1]["x1"]], axis=0) for b in range(NB)]
    inw = cont(np.asarray(inputs["L1_attn_in_w"], f32))
    inb = cont(np.asarray(inputs["L1_attn_in_b"], f32).reshape(32, 128).T)
    cwd = cont(np.asarray(inputs["L1_attn_conv_w"], f32)[:, 0, :]
               .reshape(4, 16, 128).transpose(2, 1, 0).reshape(128, 64))
    cbd = cont(np.asarray(inputs["L1_attn_conv_b"], f32).reshape(16, 128).T)
    bwd = cont(np.asarray(inputs["L1_attn_B_w"], f32))
    bbd = cont(np.asarray(inputs["L1_attn_B_b"], f32).reshape(DS, 1))
    cwd2 = cont(np.asarray(inputs["L1_attn_C_w"], f32))
    cbd2 = cont(np.asarray(inputs["L1_attn_C_b"], f32).reshape(DS, 1))
    dtw = cont(np.asarray(inputs["L1_attn_dt_w"], f32))
    dtb = cont(np.asarray(inputs["L1_attn_dt_b"], f32).reshape(16, 128).T)
    nc5 = build_p5()
    maps = []
    for c in range(NCORES):
        b, half = c // 2, c % 2
        t0 = half * TPC
        pad = np.zeros((2176, DM), f32)
        lo = t0 - 1
        hi = min(T, t0 + TPC + 2)
        src0 = max(0, lo)
        pad[src0 - lo: hi - lo] = X1[b][src0:hi]
        maps.append({"x1p": pad, "inw": inw, "inb": inb, "cwd": cwd, "cbd": cbd,
                     "bwd": bwd, "bbd": bbd, "cwd2": cwd2, "cbd2": cbd2,
                     "dtw": dtw, "dtb": dtb, "eyed": eye, "epsd": eps, "oned": one})
    r5 = _run(nc5, maps)

    XC = [np.concatenate([r5[2 * b]["xco"], r5[2 * b + 1]["xco"]], axis=1) for b in range(NB)]
    DT_ = [np.concatenate([r5[2 * b]["dto"], r5[2 * b + 1]["dto"]], axis=1) for b in range(NB)]
    SG = [np.concatenate([r5[2 * b]["sgo"], r5[2 * b + 1]["sgo"]], axis=1) for b in range(NB)]
    BM = [np.concatenate([r5[2 * b]["bmo"], r5[2 * b + 1]["bmo"]], axis=1) for b in range(NB)]
    CM = [np.concatenate([r5[2 * b]["cmo"], r5[2 * b + 1]["cmo"]], axis=1) for b in range(NB)]
    A = np.asarray(inputs["L1_attn_A"], f32)
    Dp = np.asarray(inputs["L1_attn_Dp"], f32)
    ow = cont(np.asarray(inputs["L1_attn_out_w"], f32))
    nc6 = build_p6()
    maps = []
    for c in range(NCORES):
        b, chh = c // 2, c % 2
        c0 = chh * 1024
        acol = cont(A[c0:c0 + 1024].reshape(8, 128, DS).transpose(1, 0, 2).reshape(128, 128))
        dpc = cont(Dp[c0:c0 + 1024].reshape(8, 128).T)
        maps.append({"dtd": cont(DT_[b][c0:c0 + 1024]), "xcd": cont(XC[b][c0:c0 + 1024]),
                     "sgd": cont(SG[b][c0:c0 + 1024]), "bmd": cont(BM[b]), "cmd": cont(CM[b]),
                     "acold": acol, "dpd": dpc, "owd": cont(ow[c0:c0 + 1024]),
                     "onesd": ones1})
    r6 = _run(nc6, maps)

    gw = cont(np.asarray(inputs["L1_ffn_gate_w"], f32))
    gb = cont(np.asarray(inputs["L1_ffn_gate_b"], f32).reshape(1, NE))
    outb = cont(np.asarray(inputs["L1_attn_out_b"], f32).reshape(8, 128).T)
    nc7 = build_p7()
    maps = []
    for c in range(NCORES):
        b, half = c // 2, c % 2
        t0 = half * TPC
        maps.append({"x1": r4[c]["x1"], "xn2": r5[c]["xn2o"],
                     "opa": cont(r6[2 * b]["opo"][:, t0:t0 + TPC]),
                     "opb": cont(r6[2 * b + 1]["opo"][:, t0:t0 + TPC]),
                     "outbd": outb, "gwd": gw, "gbd": gb, "onesd": ones1, "eyed": eye})
    r7 = _run(nc7, maps)

    w1e = np.asarray(inputs["L1_ffn_w1"], f32)
    b1e = np.asarray(inputs["L1_ffn_b1"], f32)
    w2e = np.asarray(inputs["L1_ffn_w2"], f32)
    b2e = np.asarray(inputs["L1_ffn_b2"], f32)
    nc8 = build_moe()
    acc = [r7[c]["base"] for c in range(NCORES)]
    for g in range(NE // GEXP):
        e0 = g * GEXP
        w1d = cont(w1e[e0:e0 + GEXP])
        b1d = cont(np.concatenate([b1e[e0 + i].reshape(32, 128).T for i in range(GEXP)], axis=1))
        w2d = cont(w2e[e0:e0 + GEXP])
        b2d = cont(np.concatenate([b2e[e0 + i].reshape(8, 128).T for i in range(GEXP)], axis=1))
        maps = [{"xnb_d": r7[c]["xnbo"], "ctg": cont(r7[c]["cto"][e0:e0 + GEXP]),
                 "accin": acc[c], "w1g": w1d, "b1g": b1d, "w2g": w2d, "b2g": b2d,
                 "onesd": ones1} for c in range(NCORES)]
        r8 = _run(nc8, maps)
        acc = [r8[c]["accout"] for c in range(NCORES)]

    nc9 = build_p9()
    maps = [{"accd": acc[c], "eyed": eye} for c in range(NCORES)]
    r9 = _run(nc9, maps)

    out = np.empty((NB, T, DM), f32)
    for c in range(NCORES):
        b, half = c // 2, c % 2
        t0 = half * TPC
        out[b, t0:t0 + TPC] = r9[c]["outd"]
    return out
